# version 10
# baseline (speedup 1.0000x reference)
import numpy as np
import concourse.bass as bass
import concourse.mybir as mybir
from concourse.bass_utils import run_bass_kernel_spmd

F32 = mybir.dt.float32
BF16 = mybir.dt.bfloat16
U8 = mybir.dt.uint8
AF = mybir.ActivationFunctionType
OP = mybir.AluOpType
AX = mybir.AxisListType.X

T = 2048
C = 1024
H = 4096
E = 8
TT = 16
CCn = 8
HHn = 32
NQB = 4
NSG = 8
NKT = 32
DH = 64
EPS_LN = 1e-5
EPS_RMS = 1e-8

ENGINES = ("sync", "scalar", "vector", "tensor", "gpsimd")


class Sched:
    def __init__(self):
        self.q = {e: [] for e in ENGINES}
        self.cnt = {}

    def emit(self, eng, fn):
        self.q[eng].append(fn)

    def c(self, sem):
        return self.cnt.get(sem.name, 0)

    def bump(self, sem, n=1):
        self.cnt[sem.name] = self.c(sem) + n
        return self.cnt[sem.name]


def build_program(stages=("ffn1", "attn", "ffn2")):
    import os
    init_parts = set(os.environ.get("K_INIT", "consts,gb,wt,moe").split(","))
    nc = bass.Bass("TRN2", target_bir_lowering=False, debug=False)

    din = {}
    din["x"] = nc.dram_tensor("x", [T, C], F32, kind="ExternalInput").ap()
    for nm in ("wq", "wk", "wv", "wo"):
        din[nm] = nc.dram_tensor(nm, [C, C], F32, kind="ExternalInput").ap()
    din["bo"] = nc.dram_tensor("bo", [1, C], F32, kind="ExternalInput").ap()
    din["lng"] = nc.dram_tensor("lng", [1, C], F32, kind="ExternalInput").ap()
    din["lnb"] = nc.dram_tensor("lnb", [1, C], F32, kind="ExternalInput").ap()
    din["qg"] = nc.dram_tensor("qg", [1, DH], F32, kind="ExternalInput").ap()
    din["kg"] = nc.dram_tensor("kg", [1, DH], F32, kind="ExternalInput").ap()
    din["wg"] = nc.dram_tensor("wg", [C, E], F32, kind="ExternalInput").ap()
    din["w1"] = nc.dram_tensor("w1", [E, C, H], F32, kind="ExternalInput").ap()
    din["w3"] = nc.dram_tensor("w3", [E, C, H], F32, kind="ExternalInput").ap()
    din["w2"] = nc.dram_tensor("w2", [E, H, C], F32, kind="ExternalInput").ap()
    out_d = nc.dram_tensor("out", [T, C], F32, kind="ExternalOutput").ap()
    import os as _osd
    dbg_what = _osd.environ.get("K_DBG", "")
    dbg_d = nc.dram_tensor("dbg", [2048, 64], F32, kind="ExternalOutput").ap()

    w1c = nc.dram_tensor("w1c", [E, HHn, 128, CCn, 128], BF16).ap()
    w3c = nc.dram_tensor("w3c", [E, HHn, 128, CCn, 128], BF16).ap()
    w2c = nc.dram_tensor("w2c", [E, HHn, 128, C], BF16).ap()
    wqt_d = nc.dram_tensor("wqt_d", [C, C], BF16).ap()
    wkt_d = nc.dram_tensor("wkt_d", [C, C], BF16).ap()
    wvt_d = nc.dram_tensor("wvt_d", [C, C], BF16).ap()
    wot_d = nc.dram_tensor("wot_d", [C, C], BF16).ap()
    xnt_d = nc.dram_tensor("xnt_d", [T, C], BF16).ap()
    cc_in = nc.dram_tensor("cc_in", [T, C], BF16).ap()
    cc_out = nc.dram_tensor("cc_out", [2 * T, C], BF16).ap()
    resid_d = nc.dram_tensor("resid_d", [T, C], F32).ap()
    CAP = 896
    NT = CAP // 128
    NSL = E * CAP
    glist_d = nc.dram_tensor("glist_d", [NSL, 1], mybir.dt.int32).ap()
    xg_d = nc.dram_tensor("xg_d", [2, CAP, C], BF16).ap()
    eo_d = nc.dram_tensor("eo_d", [NSL, C], BF16).ap()

    arenaA = nc.alloc_sbuf_tensor("arenaA", [128, 65536], U8)
    arenaB = nc.alloc_sbuf_tensor("arenaB", [128, 67584], U8)
    arenaC = nc.alloc_sbuf_tensor("arenaC", [128, 68096], U8)

    def carve(arena, off, nbytes, dt, shape):
        ap = arena.ap()[:, off:off + nbytes].bitcast(dt)
        if len(shape) == 1:
            assert ap.shape[1] == shape[0], (ap.shape, shape)
            return ap
        names = " ".join(f"d{i}" for i in range(len(shape)))
        return ap.rearrange(f"p ({names}) -> p {names}",
                            **{f"d{i}": s for i, s in enumerate(shape)})

    xres = carve(arenaA, 0, 65536, F32, (TT, C))
    kfm = carve(arenaA, 0, 65536, BF16, (CCn, 4096))

    xnfm = carve(arenaB, 0, 32768, BF16, (CCn, T))
    hid = carve(arenaB, 32768, 32768, BF16, (HHn, 512))
    vt = carve(arenaB, 0, 66560, BF16, (NKT, 1040))
    vt_h = vt.rearrange("p t (h w) -> p t h w", w=65)

    w1t = carve(arenaC, 51200, 8192, BF16, (4, CCn, 128))
    w3t = carve(arenaC, 59392, 8192, BF16, (4, CCn, 128))
    w2t = carve(arenaC, 8192, 4096, BF16, (2, C))
    sl = carve(arenaC, 12288, 2048, BF16, (2, 512))
    sq = carve(arenaC, 14336, 8192, F32, (2, C))
    xnt_st = carve(arenaC, 22528, 4096, BF16, (2, C))
    wstripe = carve(arenaC, 26624, 8192, BF16, (2, 2048))
    g_bc = carve(arenaC, 43008, 4096, F32, (C,))
    b_bc = carve(arenaC, 47104, 4096, F32, (C,))
    stage = carve(arenaC, 0, 16384, BF16, (2, CCn, 512))
    wph = carve(arenaC, 16384, 16384, BF16, (CCn, C))
    qfm = carve(arenaC, 32768, 8192, BF16, (CCn, 512))
    Pt = carve(arenaC, 40960, 4096, BF16, (4, 512))
    out_norm = carve(arenaC, 45056, 8192, BF16, (CCn, 512))
    onrm = carve(arenaC, 53248, 4096, F32, (2, 512))
    ksq = carve(arenaC, 57344, 2048, BF16, (2, 512))
    rkt = carve(arenaC, 59392, 4096, F32, (2, 512))
    res_stage = carve(arenaC, 63488, 4096, F32, (C,))
    mxbuf = carve(arenaC, 67584, 256, F32, (2, 32))
    rden = carve(arenaC, 40960, 2048, F32, (512,))[0:1, :]
    grow = carve(arenaC, 14336, 4096, F32, (C,))[0:1, :]
    brow = carve(arenaC, 18432, 4096, F32, (C,))[0:1, :]
    biascols = carve(arenaC, 67840, 16, F32, (4,))
    I32 = mybir.dt.int32
    xgstage = carve(arenaC, 0, 4096, BF16, (2, C))
    glii = carve(arenaC, 4096, 256, I32, (64,))
    srt = carve(arenaC, 4352, 128, F32, (16, 2))
    sint = carve(arenaC, 4480, 128, I32, (16, 2))
    sint2 = carve(arenaC, 4608, 128, I32, (16, 2))
    tokid16 = None
    SRr = carve(arenaC, 4800, 512, F32, (128,))
    msumb = carve(arenaC, 5312, 256, BF16, (128,))
    rsc_f = carve(arenaC, 5568, 64, F32, (16,))
    rrow_f = carve(arenaC, 5632, 512, F32, (128,))[0:1, :]
    ecapcol = None
    msk1b = carve(arenaC, 6208, 256, BF16, (16, E))
    msk2b = carve(arenaC, 6464, 256, BF16, (16, E))
    wts = carve(arenaC, 6720, 128, F32, (16, 2))
    usub = None
    u16f = None
    ubAi = carve(arenaC, 26624, 512, I32, (128,))
    ubBi = carve(arenaC, 27136, 512, I32, (128,))
    ubCi = carve(arenaC, 27648, 512, I32, (128,))
    ubDi = carve(arenaC, 28160, 512, I32, (128,))
    tri_a = carve(arenaC, 14976, 64, I32, (16,))
    tri_b = carve(arenaC, 15040, 64, I32, (16,))
    hidg = carve(arenaB, 0, 57344, BF16, (HHn, CAP))
    xg_fm = carve(arenaC, 34816, 14336, BF16, (CCn, CAP))
    sl2 = carve(arenaC, 18432, 3584, BF16, (2, CAP))
    eostage = carve(arenaC, 14336, 4096, BF16, (2, C))
    tri_c = carve(arenaC, 15104, 64, I32, (16,))

    stats = nc.alloc_sbuf_tensor("stats", [128, 16, 8], F32).ap()
    cmat = nc.alloc_sbuf_tensor("cmat", [128, 16, E], F32).ap()
    wgs = nc.alloc_sbuf_tensor("wgs", [128, CCn, E], BF16).ap()
    gsc = nc.alloc_sbuf_tensor("gsc", [128, 3, E], F32).ap()
    gtmp = nc.alloc_sbuf_tensor("gtmp", [128, 8], F32).ap()
    ident_f = nc.alloc_sbuf_tensor("ident_f", [128, 128], F32).ap()
    ident_b = nc.alloc_sbuf_tensor("ident_b", [128, 128], BF16).ap()
    onesf = nc.alloc_sbuf_tensor("onesf", [1, 128], F32).ap()
    ones1b = nc.alloc_sbuf_tensor("ones1b", [1, 128], BF16).ap()
    ones2h = nc.alloc_sbuf_tensor("ones2h", [128, 2], BF16).ap()
    o2tmp = nc.alloc_sbuf_tensor("o2tmp", [128, 2], F32).ap()
    qgam2 = nc.alloc_sbuf_tensor("qgam2", [2, 128], F32).ap()
    kgam2 = nc.alloc_sbuf_tensor("kgam2", [2, 128], F32).ap()
    negmT = nc.alloc_sbuf_tensor("negmT", [1, 4], F32).ap()
    borow = nc.alloc_sbuf_tensor("borow", [1, C], BF16).ap()
    ceps = nc.alloc_sbuf_tensor("ceps", [128, 1], F32).ap()
    onescol_b = nc.alloc_sbuf_tensor("onescol_b", [128, 1], BF16).ap()
    usub = nc.alloc_sbuf_tensor("usub", [128, 128], BF16).ap()
    u16f = nc.alloc_sbuf_tensor("u16f", [128, 128], F32).ap()
    ecapcol = nc.alloc_sbuf_tensor("ecapcol", [128, 1], F32).ap()
    tokid16 = nc.alloc_sbuf_tensor("tokid16", [128, 16], mybir.dt.int32).ap()
    ecrow = nc.alloc_sbuf_tensor("ecrow", [1, 128], F32).ap()
    wtmp = nc.alloc_sbuf_tensor("wtmp", [128, 2, 128], BF16).ap()
    wsto = nc.alloc_sbuf_tensor("wsto", [128, 2, 128], BF16).ap()
    assert nc.sbuf_bytes_remaining >= 0, nc.sbuf_bytes_remaining

    P = [nc.alloc_psum_tensor(f"p{i}", [128, 512], F32).ap() for i in range(8)]
    p_bf = P[7].bitcast(BF16)

    S = Sched()
    import contextlib
    stack = contextlib.ExitStack()
    sems = {}
    for nm in ("sIN", "sW1", "sW3", "sW2", "sST", "sXN", "sSP", "sCC", "sWC",
               "sWS", "sOUT", "sPE1", "sPEh", "sPEeo", "sPEx", "sACT", "sDVE",
               "sDVE2", "sBAR", "sGB", "sVC", "sSC", "sGC", "sEOD"):
        sems[nm] = stack.enter_context(nc.semaphore(nm))
    sem = type("X", (), sems)

    def emit(eng, fn):
        S.emit(eng, fn)

    def wait(eng, s, val):
        if val > 0:
            emit(eng, lambda e, s=s, v=val: e.wait_ge(s, v))

    def dma(eng, s, out, in_, **kw):
        tgt = S.bump(s, 16)
        emit(eng, lambda e, o=out, i=in_, s=s, kw=kw:
             e.dma_start(out=o, in_=i, **kw).then_inc(s, 16))
        return tgt

    from concourse.bass_core_rust import IndirectOffsetOnAxis

    def idma(eng, s, out, in_, out_off=None, in_off=None):
        tgt = S.bump(s, 16)
        emit(eng, lambda e, o=out, i=in_, s=s, oo=out_off, io=in_off:
             e.indirect_dma_start(
                 o, IndirectOffsetOnAxis(oo, 0) if oo is not None else None,
                 i, IndirectOffsetOnAxis(io, 0) if io is not None else None
             ).then_inc(s, 16))
        return tgt

    def inc(eng, instr_fn, s, n=1):
        tgt = S.bump(s, n)
        emit(eng, lambda e, f=instr_fn, s=s, n=n: f(e).then_inc(s, n))
        return tgt

    def run(eng, instr_fn):
        emit(eng, lambda e, f=instr_fn: f(e))

    chain_sems = {}

    def cstep(eng, instr_fn):
        sc = chain_sems[eng]
        tgt = inc(eng, instr_fn, sc)
        wait(eng, sc, tgt)

    chain_sems.update({"vector": sem.sVC, "scalar": sem.sSC,
                       "gpsimd": sem.sGC, "tensor": sem.sPEx})

    nbar = [0]

    def barrier():
        nbar[0] += 1
        tgt = nbar[0] * len(ENGINES)
        S.bump(sem.sBAR, len(ENGINES))
        for eng in ENGINES:
            emit(eng, lambda e: e.sem_inc(sem.sBAR, 1))
            wait(eng, sem.sBAR, tgt)

    for tt in range(TT):
        dma("sync", sem.sIN, xres[:, tt, :], din["x"][tt * 128:(tt + 1) * 128, :])
    x_loaded = S.c(sem.sIN)

    cstep("gpsimd", lambda e: e.iota(
        ident_f[:, :], [[1, 128]], channel_multiplier=-1,
        allow_small_or_imprecise_dtypes=True))
    cstep("gpsimd", lambda e: e.tensor_scalar(
        ident_f[:, :], ident_f[:, :], 0.0, None, OP.is_equal))
    run("gpsimd", lambda e: e.memset(onesf[:, :], 1.0))
    run("gpsimd", lambda e: e.memset(ones1b[:, :], 1.0))
    run("gpsimd", lambda e: e.memset(qgam2[0:2, :], 0.0))
    run("gpsimd", lambda e: e.memset(kgam2[0:2, :], 0.0))
    run("gpsimd", lambda e: e.memset(ceps[:, :], EPS_LN))
    cstep("gpsimd", lambda e: e.iota(
        o2tmp[:, 0:1], [[1, 1]], channel_multiplier=1,
        allow_small_or_imprecise_dtypes=True))
    cstep("gpsimd", lambda e: e.tensor_scalar(
        o2tmp[:, 1:2], o2tmp[:, 0:1], 63.5, None, OP.is_gt))
    inc("gpsimd", lambda e: e.tensor_scalar(
        o2tmp[:, 0:1], o2tmp[:, 1:2], -1.0, 1.0, OP.mult, OP.add), sem.sGB)
    wait("scalar", sem.sGB, 1)
    inc("scalar", lambda e: e.copy(out=ident_b[:, :], in_=ident_f[:, :]), sem.sACT)
    inc("scalar", lambda e: e.copy(out=ones2h[:, :], in_=o2tmp[:, :]), sem.sACT)
    consts_cast = S.c(sem.sACT)

    def _emit_const_loads():
        dma("gpsimd", sem.sIN, qgam2[0:1, 0:64], din["qg"][0:1, :])
        dma("gpsimd", sem.sIN, qgam2[1:2, 64:128], din["qg"][0:1, :])
        dma("gpsimd", sem.sIN, kgam2[0:1, 0:64], din["kg"][0:1, :])
        dma("gpsimd", sem.sIN, kgam2[1:2, 64:128], din["kg"][0:1, :])
        dma("gpsimd", sem.sIN, grow[:, :], din["lng"][:, :])
        dma("gpsimd", sem.sIN, brow[:, :], din["lnb"][:, :])
        dma("gpsimd", sem.sIN, borow[:, :], din["bo"][:, :])
        tgt = dma("gpsimd", sem.sIN, wgs[:, :, :],
                  din["wg"].rearrange("(cc p) e -> p cc e", p=128))
        wait("vector", sem.sIN, tgt)
        inc("vector", lambda e: e.tensor_scalar_mul(
            qgam2[0:2, :], qgam2[0:2, :], 0.125), sem.sDVE)
        return tgt

    def _emit_gb_bcast():
        dma("gpsimd", sem.sIN, grow[:, :], din["lng"][:, :])
        gb_tgt = dma("gpsimd", sem.sIN, brow[:, :], din["lnb"][:, :])
        wait("tensor", sem.sIN, gb_tgt)
        wait("tensor", sem.sACT, max(consts_cast, S.c(sem.sACT)))
        wait("tensor", sem.sDVE, S.c(sem.sDVE))
        wait("tensor", sem.sDVE2, S.c(sem.sDVE2))
        for half in range(2):
            if half == 1:
                wait("tensor", sem.sDVE, S.c(sem.sDVE))
            run("tensor", lambda e, h=half: e.matmul(
                P[6][:, :], onesf[0:1, :], grow[0:1, h * 512:(h + 1) * 512],
                start=True, stop=True))
            inc("tensor", lambda e, h=half: e.matmul(
                P[5][:, :], onesf[0:1, :], brow[0:1, h * 512:(h + 1) * 512],
                start=True, stop=True), sem.sPEx)
            wait("vector", sem.sPEx, S.c(sem.sPEx))
            run("vector", lambda e, h=half: e.tensor_copy(
                out=g_bc[:, h * 512:(h + 1) * 512], in_=P[6][:, :]))
            inc("vector", lambda e, h=half: e.tensor_copy(
                out=b_bc[:, h * 512:(h + 1) * 512], in_=P[5][:, :]), sem.sDVE)

    def _emit_wt_transposes():
        import os as _os
        wtn = int(_os.environ.get("K_WTN", "256"))
        wt_skip_store = _os.environ.get("K_WT_NOSTORE", "") == "1"
        wt_skip_dve = _os.environ.get("K_WT_NODVE", "") == "1"
        wt_skip_tr = _os.environ.get("K_WT_NOTRANS", "") == "1"
        wait("tensor", sem.sACT, consts_cast)
        store_tgts = []
        n_tr = 0
        for nm, dst in (("wq", wqt_d), ("wk", wkt_d), ("wv", wvt_d),
                        ("wo", wot_d)):
            for mm in range(8):
                for cc in range(8):
                    n = n_tr
                    if n >= wtn:
                        continue
                    par = n % 2
                    if n >= 2:
                        wait("gpsimd", sem.sPEx, S.c(sem.sPEx) - 1)
                    lt = dma("gpsimd", sem.sWC, wtmp[:, par, :],
                             din[nm][mm * 128:(mm + 1) * 128,
                                     cc * 128:(cc + 1) * 128])
                    wait("tensor", sem.sWC, lt)
                    if n >= 1:
                        wait("tensor", sem.sDVE2, S.c(sem.sDVE2))
                    if wt_skip_tr:
                        inc("tensor", lambda e: e.engine_nop()
                            if False else e.nop(), sem.sPEx)
                    else:
                        inc("tensor", lambda e, par=par: e.transpose(
                            p_bf[:, par * 128:par * 128 + 128], wtmp[:, par, :],
                            ident_b[:, :]), sem.sPEx)
                    wait("vector", sem.sPEx, S.c(sem.sPEx))
                    if n >= 2:
                        wait("vector", sem.sWS, store_tgts[n - 2])
                    if wt_skip_dve or wt_skip_tr:
                        inc("vector", lambda e, par=par: e.tensor_copy(
                            out=wsto[:, par, :], in_=wtmp[:, par, :]),
                            sem.sDVE2)
                    else:
                        inc("vector", lambda e, par=par: e.tensor_copy(
                            out=wsto[:, par, :],
                            in_=p_bf[:, par * 128:par * 128 + 128]), sem.sDVE2)
                    wait("sync", sem.sDVE2, S.c(sem.sDVE2))
                    if wt_skip_store:
                        st = S.bump(sem.sWS, 16)
                        emit("sync", lambda e: e.sem_inc(sem.sWS, 16))
                    else:
                        st = dma("sync", sem.sWS,
                                 dst[cc * 128:(cc + 1) * 128,
                                     mm * 128:(mm + 1) * 128],
                                 wsto[:, par, :])
                    store_tgts.append(st)
                    n_tr += 1

    relay_tgt = {}

    def _emit_relayout_expert(eidx):
        units = []
        for cc in range(CCn):
            units += [("w1", eidx, cc, 0), ("w1", eidx, cc, 1),
                      ("w3", eidx, cc, 0), ("w3", eidx, cc, 1)]
        units += [("w2", eidx, hh, 0) for hh in range(HHn)]
        _emit_relayout_units(units)

    rl_n = [0]

    def _emit_relayout_units(units):
        for (kind, eidx, i, hf) in units:
            par = rl_n[0] % 2
            rl_n[0] += 1
            wait("gpsimd", sem.sWS, S.c(sem.sWS) - 16)
            if kind in ("w1", "w3"):
                src = din[kind]
                dstc = w1c if kind == "w1" else w3c
                lt = dma("gpsimd", sem.sWC, wstripe[:, par, :],
                         src[eidx, i * 128:(i + 1) * 128,
                             hf * 2048:(hf + 1) * 2048])
                wait("gpsimd", sem.sWC, lt)
                st = dma("gpsimd", sem.sWS,
                         dstc[eidx, hf * 16:(hf + 1) * 16, :, i, :]
                         .transpose([1, 0, 2]),
                         wstripe[:, par, :].rearrange("p (hh w) -> p hh w",
                                                      w=128))
                relay_tgt[(kind, eidx, i, hf)] = st
                relay_tgt[(kind, eidx, i)] = st
            else:
                lt = dma("gpsimd", sem.sWC, wstripe[:, par, 0:1024],
                         din["w2"][eidx, i * 128:(i + 1) * 128, :])
                wait("gpsimd", sem.sWC, lt)
                st = dma("gpsimd", sem.sWS, w2c[eidx, i, :, :],
                         wstripe[:, par, 0:1024])
                relay_tgt[("w2", eidx, i)] = st

    def _emit_moe_cache():
        st_list = []
        units = []
        for eidx in range(E):
            for cc in range(CCn):
                units.append(("w1", eidx, cc))
                units.append(("w3", eidx, cc))
            for hh in range(HHn):
                units.append(("w2", eidx, hh))
        for u, (kind, eidx, i) in enumerate(units):
            par = u % 2
            if u >= 2:
                wait("gpsimd", sem.sWS, st_list[u - 2])
            if kind == "w1":
                lt = dma("gpsimd", sem.sWC, wstripe[:, par, :],
                         din["w1"][eidx, i * 128:(i + 1) * 128, :])
                wait("gpsimd", sem.sWC, lt)
                st = dma("gpsimd", sem.sWS,
                         w1c[eidx, :, :, i, :].transpose([1, 0, 2]),
                         wstripe[:, par, :].rearrange("p (hh w) -> p hh w",
                                                      w=128))
            elif kind == "w3":
                lt = dma("gpsimd", sem.sWC, wstripe[:, par, :],
                         din["w3"][eidx, i * 128:(i + 1) * 128, :])
                wait("gpsimd", sem.sWC, lt)
                st = dma("gpsimd", sem.sWS,
                         w3c[eidx, :, :, i, :].transpose([1, 0, 2]),
                         wstripe[:, par, :].rearrange("p (hh w) -> p hh w",
                                                      w=128))
            else:
                lt = dma("gpsimd", sem.sWC, wstripe[:, par, 0:1024],
                         din["w2"][eidx, i * 128:(i + 1) * 128, :])
                wait("gpsimd", sem.sWC, lt)
                st = dma("gpsimd", sem.sWS, w2c[eidx, i, :, :],
                         wstripe[:, par, 0:1024])
            st_list.append(st)
            relay_tgt[(kind, eidx, i)] = st

    def _emit_route_consts():
        wait("gpsimd", sem.sGB, 1)
        cstep("gpsimd", lambda e: e.iota(ubBi[:, :], [[1, 128]],
              channel_multiplier=-1,
              allow_small_or_imprecise_dtypes=True))
        inc("gpsimd", lambda e: e.iota(ubDi[:, 0:1], [[1, 1]],
            channel_multiplier=1), sem.sGB)
        iotas_done = S.c(sem.sGB)
        wait("vector", sem.sGB, iotas_done)
        rfull = carve(arenaC, 5632, 512, F32, (128,))
        cstep("vector", lambda e: e.tensor_copy(out=rfull[:, :],
                                                in_=ubBi[:, :]))
        cstep("vector", lambda e: e.tensor_scalar(
            u16f[:, :], rfull[:, :], 0.0, None, OP.is_gt))
        cstep("vector", lambda e: e.tensor_copy(out=usub[:, :],
                                                in_=u16f[:, :]))
        cstep("vector", lambda e: e.tensor_scalar(
            ubCi[:, :], ubBi[:, :], 7, None, OP.bitwise_and))
        cstep("vector", lambda e: e.tensor_copy(out=rfull[:, :],
                                                in_=ubCi[:, :]))
        cstep("vector", lambda e: e.tensor_scalar(
            rfull[:, :], rfull[:, :], 0.0, None, OP.is_equal))
        cstep("vector", lambda e: e.tensor_mul(
            u16f[:, :], u16f[:, :], rfull[:, :]))
        cstep("vector", lambda e: e.tensor_scalar(
            ubDi[:, 1:2], ubDi[:, 0:1], 7, None, OP.bitwise_and))
        cstep("vector", lambda e: e.tensor_scalar(
            ubDi[:, 2:3], ubDi[:, 1:2], 9, None, OP.arith_shift_left))
        cstep("vector", lambda e: e.tensor_scalar(
            ubDi[:, 3:4], ubDi[:, 1:2], 8, None, OP.arith_shift_left))
        cstep("vector", lambda e: e.tensor_tensor(
            ubDi[:, 2:3], ubDi[:, 2:3], ubDi[:, 3:4], OP.add))
        cstep("vector", lambda e: e.tensor_scalar(
            ubDi[:, 3:4], ubDi[:, 1:2], 7, None, OP.arith_shift_left))
        cstep("vector", lambda e: e.tensor_tensor(
            ubDi[:, 2:3], ubDi[:, 2:3], ubDi[:, 3:4], OP.add))
        inc("vector", lambda e: e.tensor_copy(
            out=ecapcol[:, 0:1], in_=ubDi[:, 2:3]), sem.sDVE)
        wait("tensor", sem.sDVE, S.c(sem.sDVE))
        inc("tensor", lambda e: e.transpose(
            P[6][0:1, 0:128], ecapcol[:, 0:1], ident_f[:, :]), sem.sPE1)
        wait("vector", sem.sPE1, S.c(sem.sPE1))
        cstep("vector", lambda e: e.tensor_copy(
            out=ecrow[0:1, :], in_=P[6][0:1, 0:128]))
        inc("vector", lambda e: e.tensor_scalar_add(
            ecrow[0:1, :], ecrow[0:1, :], float(CAP - 1)), sem.sDVE)
        rc = S.c(sem.sDVE)
        cstep("gpsimd", lambda e: e.iota(
            tokid16[:, :], [[128, 16]], channel_multiplier=1))
        run("gpsimd", lambda e: e.memset(onescol_b[:, :], 1.0))
        cstep("gpsimd", lambda e: e.memset(glii[:, :], 0))
        dma("gpsimd", sem.sGB, glist_d.rearrange("(p j) one -> p (j one)", p=128),
            glii[:, 0:56])
        return rc

    def _emit_softmax_bound():
        wait("vector", sem.sIN, gam_loaded[0])
        wait("vector", sem.sDVE, S.c(sem.sDVE))
        cstep("vector", lambda e: e.reduce_max(
            negmT[0:1, 0:1], qgam2[0:1, 0:64], axis=AX,
            apply_absolute_value=True))
        cstep("vector", lambda e: e.reduce_max(
            negmT[0:1, 1:2], kgam2[0:1, 0:64], axis=AX,
            apply_absolute_value=True))
        cstep("vector", lambda e: e.tensor_mul(
            negmT[0:1, 2:3], negmT[0:1, 0:1], negmT[0:1, 1:2]))
        inc("vector", lambda e: e.tensor_scalar_mul(
            negmT[0:1, 3:4], negmT[0:1, 2:3], -64.0), sem.sDVE)
        wait("tensor", sem.sDVE, S.c(sem.sDVE))
        wait("tensor", sem.sPEx, S.c(sem.sPEx))
        inc("tensor", lambda e: e.matmul(
            P[2][:, 0:1], onesf[0:1, :], negmT[0:1, 3:4],
            start=True, stop=True), sem.sPE1)
        wait("vector", sem.sPE1, S.c(sem.sPE1))
        inc("vector", lambda e: e.tensor_copy(
            out=biascols[:, 2:3], in_=P[2][:, 0:1]), sem.sDVE)

    gam_loaded = [0]
    if "consts" in init_parts:
        gam_loaded[0] = _emit_const_loads()
    else:
        gam_loaded[0] = dma("gpsimd", sem.sIN, wgs[:, :, :],
                            din["wg"].rearrange("(cc p) e -> p cc e", p=128))
    if "gb" in init_parts:
        _emit_gb_bcast()
    if "wt" in init_parts:
        _emit_wt_transposes()
    _emit_softmax_bound()
    rc_done = _emit_route_consts()

    wait("scalar", sem.sIN, x_loaded)
    wait("vector", sem.sIN, x_loaded)
    barrier()
    if "moe" in init_parts and os.environ.get("K_DENSE", "") == "1":
        _emit_moe_cache()
    wc_done = S.c(sem.sWS)

    def emit_ln(is_attn):
        dst_d = cc_in if is_attn else xnt_d
        if is_attn:
            wait("scalar", sem.sPEx, S.c(sem.sPEx))
            wait("vector", sem.sPEx, S.c(sem.sPEx))
        ssq_base = S.c(sem.sDVE)
        act_base = S.c(sem.sACT)
        for tt in range(TT):
            par = tt % 2
            if tt >= 2:
                wait("scalar", sem.sDVE, ssq_base + tt - 1)
            inc("scalar", lambda e, tt=tt, par=par: e.square(
                out=sq[:, par, :], in_=xres[:, tt, :]), sem.sACT)
            run("vector", lambda e, tt=tt: e.reduce_sum(
                stats[:, tt, 0:1], xres[:, tt, :], axis=AX))
            wait("vector", sem.sACT, act_base + tt + 1)
            inc("vector", lambda e, tt=tt, par=par: e.reduce_sum(
                stats[:, tt, 1:2], sq[:, par, :], axis=AX), sem.sDVE)
            S.bump(sem.sDVE, 0)
        wait("vector", sem.sDVE, S.c(sem.sDVE))
        cstep("vector", lambda e: e.tensor_scalar_mul(
            stats[:, :, 2:3], stats[:, :, 0:1], 1.0 / C))
        cstep("vector", lambda e: e.tensor_mul(
            stats[:, :, 6:7], stats[:, :, 2:3], stats[:, :, 2:3]))
        inc("vector", lambda e: e.scalar_tensor_tensor(
            stats[:, :, 3:4], stats[:, :, 1:2], 1.0 / C, stats[:, :, 6:7],
            OP.mult, OP.subtract), sem.sDVE)
        wait("scalar", sem.sDVE, S.c(sem.sDVE))
        inc("scalar", lambda e: e.activation(
            out=stats[:, :, 4:5], in_=stats[:, :, 3:4], func=AF.Sqrt,
            bias=ceps[:, 0:1]), sem.sACT)
        wait("vector", sem.sACT, S.c(sem.sACT))
        cstep("vector", lambda e: e.reciprocal(stats[:, :, 5:6], stats[:, :, 4:5]))
        if is_attn:
            cstep("vector", lambda e: e.tensor_mul(
                stats[:, :, 6:7], stats[:, :, 5:6], stats[:, :, 5:6]))
            inc("vector", lambda e: e.tensor_mul(
                stats[:, :, 6:7], stats[:, :, 3:4], stats[:, :, 6:7]), sem.sDVE)
            wait("scalar", sem.sDVE, S.c(sem.sDVE))
            inc("scalar", lambda e: e.activation(
                out=stats[:, :, 4:5], in_=stats[:, :, 6:7], func=AF.Sqrt,
                bias=ceps[:, 0:1]), sem.sACT)
            wait("vector", sem.sACT, S.c(sem.sACT))
            cstep("vector", lambda e: e.reciprocal(stats[:, :, 6:7], stats[:, :, 4:5]))
            cstep("vector", lambda e: e.tensor_mul(
                stats[:, :, 5:6], stats[:, :, 5:6], stats[:, :, 6:7]))
        st_tgts = []
        for tt in range(TT):
            par = tt % 2
            if tt >= 2:
                wait("vector", sem.sXN, st_tgts[tt - 2])
            if is_attn:
                cstep("vector", lambda e, tt=tt, par=par: e.tensor_scalar(
                    sq[:, par, :], xres[:, tt, :], stats[:, tt, 2:3],
                    stats[:, tt, 5:6], OP.subtract, OP.mult))
                cstep("vector", lambda e, par=par: e.tensor_mul(
                    sq[:, par, :], sq[:, par, :], g_bc[:, :]))
                inc("vector", lambda e, par=par: e.tensor_add(
                    xnt_st[:, par, :], sq[:, par, :], b_bc[:, :]), sem.sDVE)
            else:
                inc("vector", lambda e, tt=tt, par=par: e.tensor_scalar(
                    xnt_st[:, par, :], xres[:, tt, :], stats[:, tt, 2:3],
                    stats[:, tt, 5:6], OP.subtract, OP.mult), sem.sDVE)
            wait("sync", sem.sDVE, S.c(sem.sDVE))
            st_tgts.append(dma("sync", sem.sXN,
                               dst_d[tt * 128:(tt + 1) * 128, :],
                               xnt_st[:, par, :]))
        return S.c(sem.sXN)

    def emit_gating():
        for tt in range(TT):
            for cc in range(CCn):
                fn = (lambda e, tt=tt, cc=cc: e.matmul(
                    P[0][:, tt * E:(tt + 1) * E],
                    xnfm[:, cc, tt * 128:(tt + 1) * 128],
                    wgs[:, cc, :], start=(cc == 0), stop=(cc == CCn - 1)))
                if tt == TT - 1 and cc == CCn - 1:
                    inc("tensor", fn, sem.sPE1)
                else:
                    run("tensor", fn)
        wait("vector", sem.sPE1, S.c(sem.sPE1))
        sqf = sq.rearrange("p a b -> p (a b)")
        lg = sqf[:, 0:128].rearrange("p (a b) -> p a b", b=8)
        lg2 = sqf[:, 128:256].rearrange("p (a b) -> p a b", b=8)
        m1 = sqf[:, 256:272]
        m2 = sqf[:, 272:288]
        dm = sqf[:, 288:304]
        sg = sqf[:, 304:320]
        m1b = sqf[:, 256:272].unsqueeze(2).broadcast_to([128, 16, 8])
        m2b = sqf[:, 272:288].unsqueeze(2).broadcast_to([128, 16, 8])
        msk1v = msk1b[:, :, :]
        msk2v = msk2b[:, :, :]
        cstep("vector", lambda e: e.tensor_copy(
            out=sqf[:, 0:128], in_=P[0][:, 0:128]))
        cstep("vector", lambda e: e.tensor_reduce(
            m1, lg, axis=AX, op=OP.max))
        cstep("vector", lambda e: e.tensor_tensor(
            msk1v, lg, m1b, OP.is_equal))
        cstep("vector", lambda e: e.scalar_tensor_tensor(
            lg2, msk1v, -1e30, lg, OP.mult, OP.add))
        cstep("vector", lambda e: e.tensor_reduce(
            m2, lg2, axis=AX, op=OP.max))
        cstep("vector", lambda e: e.tensor_tensor(
            msk2v, lg2, m2b, OP.is_equal))
        inc("vector", lambda e: e.tensor_tensor(
            dm, m1, m2, OP.subtract), sem.sDVE)
        wait("scalar", sem.sDVE, S.c(sem.sDVE))
        inc("scalar", lambda e: e.activation(
            out=sg, in_=dm, func=AF.Sigmoid), sem.sACT)
        wait("vector", sem.sACT, S.c(sem.sACT))
        cstep("vector", lambda e: e.tensor_scalar_mul(
            wts[:, :, 0:1].rearrange("p a b -> p (a b)"), sg, 0.5))
        inc("vector", lambda e: e.tensor_scalar(
            wts[:, :, 1:2].rearrange("p a b -> p (a b)"),
            wts[:, :, 0:1].rearrange("p a b -> p (a b)"),
            -1.0, 0.5, OP.mult, OP.add), sem.sDVE)

    import os as _os2
    ffn_parts = set(_os2.environ.get("K_FFN", "ln,gate,exp").split(","))
    n_exp = int(_os2.environ.get("K_NEXP", "8"))
    exp_parts = set(_os2.environ.get("K_EXP", "h,eo,evac").split(","))

    def emit_ffn(call):
        xn_tgt = emit_ln(is_attn=False)
        wait("sync", sem.sXN, xn_tgt)
        for cc in range(CCn):
            dma("sync", sem.sXN, xnfm[:, cc, :],
                xnt_d[:, cc * 128:(cc + 1) * 128], transpose=True)
        xnfm_tgt = S.c(sem.sXN)
        wait("tensor", sem.sXN, xnfm_tgt)
        wait("tensor", sem.sDVE2, S.c(sem.sDVE2))
        wait("tensor", sem.sACT, S.c(sem.sACT))
        if "gate" in ffn_parts:
            emit_gating()
        if "exp" not in ffn_parts:
            return
        if dbg_what in ("cmat", "logits") and call == 0:
            wait("gpsimd", sem.sDVE, S.c(sem.sDVE))
            dma("gpsimd", sem.sOUT,
                dbg_d[:, 0:8].rearrange("(tt p) e -> p tt e", p=128),
                cmat[:, :, :])
        if dbg_what == "xn" and call == 0:
            wait("gpsimd", sem.sXN, S.c(sem.sXN))
            dma("gpsimd", sem.sOUT, dbg_d, xnt_d[:, 0:64])

        if dbg_what in ("w1c", "w2c") and call == 0:
            src = (w1c if dbg_what == "w1c" else w2c.rearrange(
                "e hh p (cc w) -> e hh p cc w", w=128))
            wait("gpsimd", sem.sWS, wc_done)
            dma("gpsimd", sem.sOUT,
                dbg_d.rearrange("(p a) b -> p (a b)", p=128),
                src[0, 0, :, :, :].rearrange("p cc w -> p (cc w)"))
        wait("tensor", sem.sDVE, S.c(sem.sDVE))

        for eidx in range(n_exp):
            for tg in range(4):
                for hh in range(HHn if "h" in exp_parts else 0):
                    par = hh % 4
                    glob = S.c(sem.sPEh)
                    if glob >= 4:
                        wait("sync", sem.sPEh, glob - 3)
                    import os as _os4
                    if call == 0 and relay_tgt:
                        wait("sync", sem.sWS,
                             max(relay_tgt[("w1", eidx, 7)],
                                 relay_tgt[("w3", eidx, 7)]))
                    t1 = dma("sync", sem.sW1, w1t[:, par, :, :],
                             w1c[eidx, hh, :, :, :])
                    t3 = dma("sync", sem.sW3, w3t[:, par, :, :],
                             w3c[eidx, hh, :, :, :])
                    wait("tensor", sem.sW1, t1)
                    if hh <= 1:
                        wait("tensor", sem.sDVE2, S.c(sem.sDVE2))
                    if glob >= 2:
                        wait("tensor", sem.sACT, S.c(sem.sACT) - 1)
                        wait("tensor", sem.sDVE, S.c(sem.sDVE) - 1)
                    for cc in range(CCn):
                        fn = (lambda e, hh=hh, cc=cc, tg=tg, par=par: e.matmul(
                            P[2 * (par % 2)][:, :], w1t[:, par, cc, :],
                            xnfm[:, cc, tg * 512:(tg + 1) * 512],
                            start=(cc == 0), stop=(cc == CCn - 1)))
                        if cc == CCn - 1:
                            inc("tensor", fn, sem.sPE1)
                        else:
                            run("tensor", fn)
                    wait("tensor", sem.sW3, t3)
                    for cc in range(CCn):
                        fn = (lambda e, hh=hh, cc=cc, tg=tg, par=par: e.matmul(
                            P[2 * (par % 2) + 1][:, :], w3t[:, par, cc, :],
                            xnfm[:, cc, tg * 512:(tg + 1) * 512],
                            start=(cc == 0), stop=(cc == CCn - 1)))
                        if cc == CCn - 1:
                            inc("tensor", fn, sem.sPEh)
                        else:
                            run("tensor", fn)
                    wait("scalar", sem.sPE1, S.c(sem.sPE1))
                    if glob >= 2:
                        wait("scalar", sem.sDVE, S.c(sem.sDVE) - 1)
                    inc("scalar", lambda e, par=par: e.activation(
                        out=sl[:, par % 2, :], in_=P[2 * (par % 2)][:, :],
                        func=AF.Silu), sem.sACT)
                    wait("vector", sem.sPEh, S.c(sem.sPEh))
                    wait("vector", sem.sACT, S.c(sem.sACT))
                    inc("vector", lambda e, hh=hh, par=par: e.tensor_tensor(
                        hid[:, hh, :], sl[:, par % 2, :],
                        P[2 * (par % 2) + 1][:, :], OP.mult), sem.sDVE)
                if "eo" not in exp_parts:
                    continue
                hid_done = S.c(sem.sDVE)
                wait("tensor", sem.sDVE, hid_done)
                wait("tensor", sem.sDVE2, S.c(sem.sDVE2))
                for hh in range(HHn):
                    par = hh % 2
                    globe = S.c(sem.sPEeo)
                    if globe >= 2:
                        wait("sync", sem.sPEeo, globe - 1)
                    if call == 0 and relay_tgt:
                        wait("sync", sem.sWS, relay_tgt[("w2", eidx, hh)])
                    t2 = dma("sync", sem.sW2, w2t[:, par, :],
                             w2c[eidx, hh, :, :])
                    wait("tensor", sem.sW2, t2)
                    for j in range(8):
                        ch, tt4 = j // 4, j % 4
                        fn = (lambda e, hh=hh, par=par, ch=ch, tt4=tt4, j=j:
                              e.matmul(
                                  P[j][:, :],
                                  hid[:, hh, tt4 * 128:(tt4 + 1) * 128],
                                  w2t[:, par, ch * 512:(ch + 1) * 512],
                                  start=(hh == 0), stop=(hh == HHn - 1)))
                        if j == 7:
                            inc("tensor", fn, sem.sPEeo)
                        else:
                            run("tensor", fn)
                if "evac" not in exp_parts:
                    continue
                wait("vector", sem.sPEeo, S.c(sem.sPEeo))
                for j in range(8):
                    ch, tt4 = j // 4, j % 4
                    tta = tg * 4 + tt4
                    inc("vector", lambda e, j=j, ch=ch, tta=tta, eidx=eidx:
                        e.scalar_tensor_tensor(
                            xres[:, tta, ch * 512:(ch + 1) * 512], P[j][:, :],
                            cmat[:, tta, eidx:eidx + 1],
                            xres[:, tta, ch * 512:(ch + 1) * 512],
                            OP.mult, OP.add), sem.sDVE2)

        if dbg_what == "hid" and call == 0:
            wait("gpsimd", sem.sDVE, S.c(sem.sDVE))
            dma("gpsimd", sem.sOUT,
                dbg_d[0:1024, :].rearrange("(p a) b -> p (a b)", p=128),
                hid[:, 0, :])


    def emit_ffn_sparse(call):
        xn_tgt = emit_ln(is_attn=False)
        wait("sync", sem.sXN, xn_tgt)
        for cc in range(CCn):
            dma("sync", sem.sXN, xnfm[:, cc, :],
                xnt_d[:, cc * 128:(cc + 1) * 128], transpose=True)
        xnfm_tgt = S.c(sem.sXN)
        wait("tensor", sem.sXN, xnfm_tgt)
        wait("tensor", sem.sDVE2, S.c(sem.sDVE2))
        wait("tensor", sem.sACT, S.c(sem.sACT))
        emit_gating()
        gate_done = S.c(sem.sDVE)

        m1v = msk1b.rearrange("p a b -> p (a b)")
        m2v = msk2b.rearrange("p a b -> p (a b)")
        wait("vector", sem.sDVE, gate_done)
        inc("vector", lambda e: e.tensor_add(msumb[:, :], m1v, m2v), sem.sDVE)
        msum_tgt = S.c(sem.sDVE)
        wait("tensor", sem.sDVE, max(msum_tgt, rc_done))
        run("tensor", lambda e: e.matmul(
            P[2][:, 0:128], usub[:, :], msumb[:, :], start=True, stop=True))
        inc("tensor", lambda e: e.matmul(
            P[2][0:1, 128:256], onescol_b[:, :], msumb[:, :],
            start=True, stop=True), sem.sPE1)
        wait("vector", sem.sPE1, S.c(sem.sPE1))
        inc("vector", lambda e: e.tensor_copy(out=SRr[:, :], in_=P[2][:, 0:128]),
            sem.sDVE)
        cstep("vector", lambda e: e.tensor_copy(
            out=rrow_f[0:1, :], in_=P[2][0:1, 128:256]))
        inc("vector", lambda e: e.nop() if False else e.tensor_copy(
            out=rsc_f[:, 0:1], in_=rsc_f[:, 0:1]), sem.sDVE)
        cnt_tgt = S.c(sem.sDVE)
        wait("tensor", sem.sDVE, cnt_tgt)
        inc("tensor", lambda e: e.transpose(
            P[6][0:128, 0:1], rrow_f[0:1, :], ident_f[0:1, 0:1]), sem.sPE1)
        wait("vector", sem.sPE1, S.c(sem.sPE1))
        inc("vector", lambda e: e.tensor_copy(out=rsc_f[:, 1:2],
                                              in_=P[6][0:128, 0:1]), sem.sDVE)
        wait("tensor", sem.sDVE, S.c(sem.sDVE))
        inc("tensor", lambda e: e.matmul(
            P[6][:, 1:2], u16f[:, :], rsc_f[:, 1:2], start=True, stop=True),
            sem.sPE1)
        wait("vector", sem.sPE1, S.c(sem.sPE1))
        inc("vector", lambda e: e.tensor_add(
            rsc_f[:, 2:3], P[6][:, 1:2], ecapcol[:, 0:1]), sem.sDVE)
        wait("tensor", sem.sDVE, S.c(sem.sDVE))
        inc("tensor", lambda e: e.transpose(
            P[6][0:1, 2:130], rsc_f[:, 2:3], ident_f[:, :]), sem.sPE1)
        wait("vector", sem.sPE1, S.c(sem.sPE1))
        inc("vector", lambda e: e.tensor_copy(
            out=rrow_f[0:1, :], in_=P[6][0:1, 2:130]), sem.sDVE)
        wait("tensor", sem.sDVE, S.c(sem.sDVE))
        inc("tensor", lambda e: e.matmul(
            P[6][:, 256:384], onesf[0:1, :], rrow_f[0:1, :],
            start=True, stop=True), sem.sPE1)
        wait("vector", sem.sPE1, S.c(sem.sPE1))
        inc("vector", lambda e: e.tensor_add(
            SRr[:, :], SRr[:, :], P[6][:, 256:384]), sem.sDVE)
        if os.environ.get("K_NOCLAMP", "") != "1":
            wait("tensor", sem.sDVE, S.c(sem.sDVE))
            inc("tensor", lambda e: e.matmul(
                P[6][:, 0:128], onesf[0:1, :], ecrow[0:1, :],
                start=True, stop=True), sem.sPE1)
            wait("vector", sem.sPE1, S.c(sem.sPE1))
            inc("vector", lambda e: e.tensor_tensor(
                SRr[:, :], SRr[:, :], P[6][:, 0:128], OP.min), sem.sDVE)
        sr_tgt = S.c(sem.sDVE)
        sqf = sq.rearrange("p a b -> p (a b)")
        troute = sqf[:, 0:128]
        tr16a = sqf[:, 128:144]
        tr16b = sqf[:, 144:160]
        for k, mv in ((0, m1v), (1, m2v)):
            wait("vector", sem.sDVE, sr_tgt)
            cstep("vector", lambda e, mv=mv: e.tensor_tensor(
                troute, mv, SRr[:, :], OP.mult))
            cstep("vector", lambda e, k=k: e.tensor_reduce(
                srt[:, :, k:k + 1],
                troute.rearrange("p (a b) -> p a b", b=8), axis=AX, op=OP.add))
            srk = srt[:, :, k:k + 1].rearrange("p a b -> p (a b)")
            si = sint2[:, :, k:k + 1].rearrange("p a b -> p (a b)")
            cstep("vector", lambda e, srk=srk, si=si: e.tensor_copy(
                out=si, in_=srk))
            cstep("vector", lambda e, si=si: e.tensor_scalar(
                tri_a[:, :], si, 127, None, OP.bitwise_and))
            cstep("vector", lambda e, si=si: e.tensor_scalar(
                tri_b[:, :], si, 7, None, OP.arith_shift_right))
            cstep("vector", lambda e: e.tensor_scalar(
                tri_b[:, :], tri_b[:, :], 0, None, OP.add))
            spv = sint[:, :, k:k + 1].rearrange("p a b -> p (a b)")
            cstep("vector", lambda e, spv=spv: e.tensor_scalar(
                spv, tri_a[:, :], 5, None, OP.arith_shift_left))
            cstep("vector", lambda e: e.tensor_scalar(
                tri_c[:, :], tri_a[:, :], 4, None, OP.arith_shift_left))
            cstep("vector", lambda e, spv=spv: e.tensor_tensor(
                spv, spv, tri_c[:, :], OP.add))
            cstep("vector", lambda e: e.tensor_scalar(
                tri_c[:, :], tri_a[:, :], 3, None, OP.arith_shift_left))
            cstep("vector", lambda e, spv=spv: e.tensor_tensor(
                spv, spv, tri_c[:, :], OP.add))
            inc("vector", lambda e, spv=spv: e.tensor_tensor(
                spv, spv, tri_b[:, :], OP.add), sem.sDVE)
        sint_tgt = S.c(sem.sDVE)

        wait("gpsimd", sem.sDVE, sint_tgt)
        for tt in range(TT):
            for k in range(2):
                idma("gpsimd", sem.sGB, glist_d[:, :],
                     tokid16[:, tt:tt + 1],
                     out_off=sint[:, tt, k:k + 1])
        sc_tgt = S.c(sem.sGB)
        wait("gpsimd", sem.sGB, sc_tgt)
        glt = dma("gpsimd", sem.sGB, glii[:, 0:56],
                  glist_d.rearrange("(p j) one -> p (j one)", p=128))

        tload_tgts = {}
        gst_tgts = {}
        hdone_tgts = {}
        eod_tgts = []

        def emit_gathers(eidx):
            wait("gpsimd", sem.sGB, glt)
            if eidx >= 2:
                wait("gpsimd", sem.sST, tload_tgts[eidx - 2])
            for i in range(NT):
                gpar = i % 2
                if i >= 2:
                    wait("gpsimd", sem.sST, S.c(sem.sST) - 16)
                idma("gpsimd", sem.sGB, xgstage[:, gpar, :],
                     xnt_d[0:128, :],
                     in_off=glii[:, eidx * NT + i:eidx * NT + i + 1])
                wait("sync", sem.sGB, S.c(sem.sGB))
                dma("sync", sem.sST,
                    xg_d[eidx % 2, i * 128:(i + 1) * 128, :],
                    xgstage[:, gpar, :])
            gst_tgts[eidx] = S.c(sem.sST)

        if call == 0 and ("moe" in init_parts):
            _emit_relayout_expert(0)
        emit_gathers(0)
        for eidx in range(E):
            wait("sync", sem.sST, gst_tgts[eidx])
            if eidx >= 1:
                wait("sync", sem.sPEh, hdone_tgts[eidx - 1])
            for cc in range(CCn):
                dma("sync", sem.sST, xg_fm[:, cc, :],
                    xg_d[eidx % 2, :, cc * 128:(cc + 1) * 128], transpose=True)
            tload_tgts[eidx] = S.c(sem.sST)
            wait("tensor", sem.sST, tload_tgts[eidx])
            if call == 0 and ("moe" in init_parts) and eidx + 1 < E:
                _emit_relayout_expert(eidx + 1)

            for hh in range(HHn):
                par = hh % 4
                pp = hh % 2
                glob = S.c(sem.sPEh)
                if glob >= 4:
                    wait("scalar", sem.sPEh, glob - 3)
                if call == 0 and relay_tgt:
                    wait("scalar", sem.sWS, max(relay_tgt[("w1", eidx, 7)],
                                                relay_tgt[("w3", eidx, 7)]))
                t1 = dma("scalar", sem.sW1, w1t[:, par, :, :],
                         w1c[eidx, hh, :, :, :])
                t3 = dma("scalar", sem.sW3, w3t[:, par, :, :],
                         w3c[eidx, hh, :, :, :])
                wait("tensor", sem.sW1, t1)
                if hh <= 1:
                    wait("tensor", sem.sACT, S.c(sem.sACT))
                if glob >= 2:
                    wait("tensor", sem.sACT, S.c(sem.sACT) - 2)
                    wait("tensor", sem.sDVE, S.c(sem.sDVE) - 2)
                for cc in range(CCn):
                    run("tensor", lambda e, cc=cc, par=par, pp=pp: e.matmul(
                        P[4 * pp][:, :], w1t[:, par, cc, :],
                        xg_fm[:, cc, 0:512],
                        start=(cc == 0), stop=(cc == CCn - 1)))
                for cc in range(CCn):
                    fn = (lambda e, cc=cc, par=par, pp=pp: e.matmul(
                        P[4 * pp + 1][:, 0:384], w1t[:, par, cc, :],
                        xg_fm[:, cc, 512:896],
                        start=(cc == 0), stop=(cc == CCn - 1)))
                    if cc == CCn - 1:
                        inc("tensor", fn, sem.sPE1)
                    else:
                        run("tensor", fn)
                h1_tgt = S.c(sem.sPE1)
                wait("tensor", sem.sW3, t3)
                for cc in range(CCn):
                    run("tensor", lambda e, cc=cc, par=par, pp=pp: e.matmul(
                        P[4 * pp + 2][:, :], w3t[:, par, cc, :],
                        xg_fm[:, cc, 0:512],
                        start=(cc == 0), stop=(cc == CCn - 1)))
                for cc in range(CCn):
                    fn = (lambda e, cc=cc, par=par, pp=pp: e.matmul(
                        P[4 * pp + 3][:, 0:384], w3t[:, par, cc, :],
                        xg_fm[:, cc, 512:896],
                        start=(cc == 0), stop=(cc == CCn - 1)))
                    if cc == CCn - 1:
                        inc("tensor", fn, sem.sPEh)
                    else:
                        run("tensor", fn)
                wait("scalar", sem.sPE1, h1_tgt)
                if glob >= 2:
                    wait("scalar", sem.sDVE, S.c(sem.sDVE) - 2)
                run("scalar", lambda e, pp=pp: e.activation(
                    out=sl2[:, pp, 0:512], in_=P[4 * pp][:, :], func=AF.Silu))
                inc("scalar", lambda e, pp=pp: e.activation(
                    out=sl2[:, pp, 512:896], in_=P[4 * pp + 1][:, 0:384],
                    func=AF.Silu), sem.sACT, 2)
                wait("vector", sem.sPEh, S.c(sem.sPEh))
                wait("vector", sem.sACT, S.c(sem.sACT))
                run("vector", lambda e, hh=hh, pp=pp: e.tensor_tensor(
                    hidg[:, hh, 0:512], sl2[:, pp, 0:512],
                    P[4 * pp + 2][:, :], OP.mult))
                inc("vector", lambda e, hh=hh, pp=pp: e.tensor_tensor(
                    hidg[:, hh, 512:896], sl2[:, pp, 512:896],
                    P[4 * pp + 3][:, 0:384], OP.mult), sem.sDVE, 2)
            hdone_tgts[eidx] = S.c(sem.sPEh)
            hid_done = S.c(sem.sDVE)

            if eidx + 1 < E:
                emit_gathers(eidx + 1)

            wait("tensor", sem.sDVE, hid_done)
            for w, (t0, nt) in enumerate(((0, 4), (4, 3))):
                wait("tensor", sem.sACT, S.c(sem.sACT))
                for hh in range(HHn):
                    par = hh % 2
                    globe = S.c(sem.sPEeo)
                    if globe >= 2:
                        wait("scalar", sem.sPEeo, globe - 1)
                    if call == 0 and relay_tgt:
                        wait("scalar", sem.sWS, relay_tgt[("w2", eidx, hh)])
                    t2 = dma("scalar", sem.sW2, w2t[:, par, :],
                             w2c[eidx, hh, :, :])
                    wait("tensor", sem.sW2, t2)
                    for j in range(2 * nt):
                        tl, ch = j // 2, j % 2
                        fn = (lambda e, hh=hh, par=par, tl=tl, ch=ch, j=j,
                              t0=t0: e.matmul(
                                  P[j][:, :],
                                  hidg[:, hh, (t0 + tl) * 128:
                                       (t0 + tl + 1) * 128],
                                  w2t[:, par, ch * 512:(ch + 1) * 512],
                                  start=(hh == 0), stop=(hh == HHn - 1)))
                        if j == 2 * nt - 1:
                            inc("tensor", fn, sem.sPEeo)
                        else:
                            run("tensor", fn)
                eo_tgt = S.c(sem.sPEeo)
                wait("scalar", sem.sPEeo, eo_tgt)
                for tl in range(nt):
                    spar = len(eod_tgts) % 2
                    if len(eod_tgts) >= 2:
                        wait("scalar", sem.sEOD, eod_tgts[-2])
                    run("scalar", lambda e, tl=tl, spar=spar: e.copy(
                        out=eostage[:, spar, 0:512], in_=P[2 * tl][:, :]))
                    inc("scalar", lambda e, tl=tl, spar=spar: e.copy(
                        out=eostage[:, spar, 512:1024], in_=P[2 * tl + 1][:, :]),
                        sem.sACT)
                    wait("gpsimd", sem.sACT, S.c(sem.sACT))
                    base_row = eidx * CAP + (t0 + tl) * 128
                    eod_tgts.append(dma(
                        "gpsimd", sem.sEOD, eo_d[base_row:base_row + 128, :],
                        eostage[:, spar, :]))
        eo_all = S.c(sem.sEOD)

        if dbg_what == "sr" and call == 0:
            wait("gpsimd", sem.sDVE, sint_tgt)
            dma("gpsimd", sem.sOUT,
                dbg_d[0:256, :].rearrange("(p a) b -> p (a b)", p=128),
                SRr[:, :])
        if dbg_what == "w1c" and call == 0:
            wait("gpsimd", sem.sWS, S.c(sem.sWS))
            dma("gpsimd", sem.sOUT,
                dbg_d.rearrange("(p a) b -> p (a b)", p=128),
                w1c[0, 0, :, :, :].rearrange("p cc w -> p (cc w)"))
        if dbg_what == "glist" and call == 0:
            wait("gpsimd", sem.sGB, S.c(sem.sGB))
            dma("gpsimd", sem.sOUT, dbg_d[0:128, 0:56], glii[:, 0:56])
        wait("gpsimd", sem.sEOD, eo_all)
        wait("gpsimd", sem.sDVE, sint_tgt)
        for tt in range(TT):
            if tt >= 1:
                wait("gpsimd", sem.sDVE2, S.c(sem.sDVE2))
            for k in range(2):
                idma("gpsimd", sem.sGB, xgstage[:, k, :], eo_d[0:128, :],
                     in_off=sint2[:, tt, k:k + 1])
            gb_tgt = S.c(sem.sGB)
            wait("vector", sem.sGB, gb_tgt)
            run("vector", lambda e, tt=tt: e.scalar_tensor_tensor(
                xres[:, tt, :], xgstage[:, 0, :], wts[:, tt, 0:1],
                xres[:, tt, :], OP.mult, OP.add))
            inc("vector", lambda e, tt=tt: e.scalar_tensor_tensor(
                xres[:, tt, :], xgstage[:, 1, :], wts[:, tt, 1:2],
                xres[:, tt, :], OP.mult, OP.add), sem.sDVE2)

    def emit_attn():
        _emit_gb_bcast()
        if dbg_what == "attnc":
            wait("gpsimd", sem.sDVE, S.c(sem.sDVE))
            dma("gpsimd", sem.sOUT, dbg_d[0:128, 0:4], biascols[:, 0:4])
            dma("gpsimd", sem.sOUT, dbg_d[128:256, 0:64],
                g_bc[:, 0:64].bitcast(F32))
            dma("gpsimd", sem.sOUT, dbg_d[256:384, 0:64], b_bc[:, 0:64])
        cc_tgt = emit_ln(is_attn=True)
        if dbg_what == "xn2":
            wait("gpsimd", sem.sXN, cc_tgt)
            dma("gpsimd", sem.sOUT, dbg_d, cc_in[:, 0:64])
        for tt in range(TT):
            dma("gpsimd", sem.sSP, resid_d[tt * 128:(tt + 1) * 128, :],
                xres[:, tt, :])
        sp_tgt = S.c(sem.sSP)
        wait("gpsimd", sem.sXN, cc_tgt)
        inc("gpsimd", lambda e: e.collective_compute(
            "AllGather", OP.bypass,
            replica_groups=[[0, 1], [2, 3], [4, 5], [6, 7]],
            ins=[cc_in.opt()], outs=[cc_out.opt()]), sem.sCC)
        cc_done = S.c(sem.sCC)
        wait("gpsimd", sem.sSP, sp_tgt)

        barrier()

        wait("sync", sem.sWS, wc_done)
        wait("sync", sem.sCC, cc_done)
        wait("sync", sem.sSP, sp_tgt)
        for cc in range(CCn):
            dma("sync", sem.sST, wph[:, cc, :], wkt_d[cc * 128:(cc + 1) * 128, :])
        wkt_tgt = S.c(sem.sST)


        def emit_proj_rms(n_units, proj_dst, rhs_of, gam, sq_tgts, sqrt_tgts,
                          recip_tgts, mult_tgts):
            for step in range(n_units + 1):
                kc = step
                pc = step - 1
                if pc >= 0:
                    wait("tensor", sem.sACT, sq_tgts[pc])
                    inc("tensor", lambda e, pc=pc: e.matmul(
                        P[2][0:2, :], ones2h[:, :], ksq[:, pc % 2, :],
                        start=True, stop=True), sem.sPE1)
                    ss_tgt = S.c(sem.sPE1)
                if kc < n_units:
                    if kc >= 2:
                        wait("tensor", sem.sACT, sq_tgts[kc - 2])
                    for cc in range(CCn):
                        fn = (lambda e, cc=cc, kc=kc: e.matmul(
                            P[kc % 2][:, :], wph[:, cc, kc * 128:(kc + 1) * 128],
                            rhs_of(kc, cc),
                            start=(cc == 0), stop=(cc == CCn - 1)))
                        if cc == CCn - 1:
                            inc("tensor", fn, sem.sPE1)
                        else:
                            run("tensor", fn)
                    proj_tgt = S.c(sem.sPE1)
                if pc >= 0:
                    wait("scalar", sem.sPE1, ss_tgt)
                    inc("scalar", lambda e, pc=pc: e.activation(
                        out=rkt[0:2, pc % 2, :], in_=P[2][0:2, :], func=AF.Sqrt,
                        scale=1.0 / DH), sem.sACT)
                    sqrt_tgts.append(S.c(sem.sACT))
                if kc < n_units:
                    wait("scalar", sem.sPE1, proj_tgt)
                    run("scalar", lambda e, kc=kc: e.copy(
                        out=proj_dst(kc), in_=P[kc % 2][:, :]))
                    inc("scalar", lambda e, kc=kc: e.activation(
                        out=ksq[:, kc % 2, :], in_=P[kc % 2][:, :],
                        func=AF.Square), sem.sACT)
                    sq_tgts.append(S.c(sem.sACT))
                if pc >= 0:
                    wait("vector", sem.sACT, sqrt_tgts[pc])
                    cstep("vector", lambda e, pc=pc: e.tensor_scalar_max(
                        rkt[0:2, pc % 2, :], rkt[0:2, pc % 2, :], EPS_RMS))
                    inc("vector", lambda e, pc=pc: e.reciprocal(
                        rkt[0:2, pc % 2, :], rkt[0:2, pc % 2, :]), sem.sDVE)
                    recip_tgts.append(S.c(sem.sDVE))
                    wait("tensor", sem.sDVE, recip_tgts[pc])
                    if pc >= 1:
                        wait("tensor", sem.sDVE, mult_tgts[pc - 1])
                    inc("tensor", lambda e, pc=pc: e.matmul(
                        P[3][:, :], gam[:, :], rkt[0:2, pc % 2, :],
                        start=True, stop=True), sem.sPE1)
                    bc_tgt = S.c(sem.sPE1)
                    wait("vector", sem.sPE1, bc_tgt)
                    inc("vector", lambda e, pc=pc: e.tensor_tensor(
                        proj_dst(pc), proj_dst(pc), P[3][:, :], OP.mult),
                        sem.sDVE)
                    mult_tgts.append(S.c(sem.sDVE))

        k_sg_pe = []
        for sg in range(NSG):
            par = sg % 2
            if sg >= 2:
                wait("sync", sem.sPE1, k_sg_pe[sg - 2])
            for cc in range(CCn):
                dma("sync", sem.sST, stage[:, par, cc, :],
                    cc_out[sg * 512:(sg + 1) * 512, cc * 128:(cc + 1) * 128],
                    transpose=True)
            st_tgt = S.c(sem.sST)
            wait("tensor", sem.sST, st_tgt)
            if sg == 0:
                wait("tensor", sem.sST, wkt_tgt)
            wait("tensor", sem.sACT, S.c(sem.sACT))
            wait("tensor", sem.sDVE, S.c(sem.sDVE))
            emit_proj_rms(
                CCn,
                (lambda kc, sg=sg: kfm[:, kc, sg * 512:(sg + 1) * 512]),
                (lambda kc, cc, par=par: stage[:, par, cc, :]),
                kgam2, [], [], [], [])
            k_sg_pe.append(S.c(sem.sPE1))

        wait("sync", sem.sPE1, S.c(sem.sPE1))
        for cc in range(CCn):
            dma("sync", sem.sST, wph[:, cc, :], wvt_d[cc * 128:(cc + 1) * 128, :])
        wvt_tgt = S.c(sem.sST)
        for kt in range(NKT - 1):
            run("gpsimd", lambda e, kt=kt: e.memset(vt_h[:, kt, :, 64:65], 1.0))
        inc("gpsimd", lambda e: e.memset(vt_h[:, NKT - 1, :, 64:65], 1.0),
            sem.sGB)
        vones = S.c(sem.sGB)
        v_sg_pe = []
        for sg in range(NSG):
            par = sg % 2
            if sg >= 2:
                wait("sync", sem.sPE1, v_sg_pe[sg - 2])
            for cc in range(CCn):
                dma("sync", sem.sST, stage[:, par, cc, :],
                    cc_out[sg * 512:(sg + 1) * 512, cc * 128:(cc + 1) * 128],
                    transpose=True)
            st_tgt = S.c(sem.sST)
            wait("tensor", sem.sST, st_tgt)
            if sg == 0:
                wait("tensor", sem.sST, wvt_tgt)
                wait("tensor", sem.sGB, vones)
            for i in range(4):
                kt = sg * 4 + i
                for mh in range(2):
                    u = 2 * i + mh
                    pV = P[4 + u % 2]
                    if sg or i or mh:
                        wait("tensor", sem.sDVE2, S.c(sem.sDVE2) - 8)
                    for cc in range(CCn):
                        fn = (lambda e, cc=cc, i=i, mh=mh, par=par, pV=pV:
                              e.matmul(
                                  pV[:, :],
                                  stage[:, par, cc, i * 128:(i + 1) * 128],
                                  wph[:, cc, mh * 512:(mh + 1) * 512],
                                  start=(cc == 0), stop=(cc == CCn - 1)))
                        if cc == CCn - 1:
                            inc("tensor", fn, sem.sPE1)
                        else:
                            run("tensor", fn)
                    wait("vector", sem.sPE1, S.c(sem.sPE1))
                    for hp in range(8):
                        head = mh * 8 + hp
                        inc("vector", lambda e, kt=kt, head=head, hp=hp, pV=pV:
                            e.tensor_copy(
                                out=vt_h[:, kt, head, 0:64],
                                in_=pV[:, hp * 64:(hp + 1) * 64]), sem.sDVE2)
            v_sg_pe.append(S.c(sem.sPE1))

        for qb in range(NQB):
            wait("sync", sem.sDVE2, S.c(sem.sDVE2))
            wait("sync", sem.sPE1, S.c(sem.sPE1))
            wait("tensor", sem.sDVE2, S.c(sem.sDVE2))
            for cc in range(CCn):
                dma("sync", sem.sST, wph[:, cc, :],
                    wqt_d[cc * 128:(cc + 1) * 128, :])
            for cc in range(CCn):
                dma("sync", sem.sST, stage[:, 0, cc, :],
                    cc_in[qb * 512:(qb + 1) * 512, cc * 128:(cc + 1) * 128],
                    transpose=True)
            qst_tgt = S.c(sem.sST)
            wait("tensor", sem.sST, qst_tgt)
            wait("tensor", sem.sACT, S.c(sem.sACT))
            wait("tensor", sem.sDVE, S.c(sem.sDVE))
            emit_proj_rms(
                CCn,
                (lambda qc: qfm[:, qc, :]),
                (lambda qc, cc: stage[:, 0, cc, :]),
                qgam2, [], [], [], [])
            q_done_dve = S.c(sem.sDVE)

            for pr in range(CCn):
                wait("tensor", sem.sDVE, S.c(sem.sDVE))
                hA, hB = 2 * pr, 2 * pr + 1
                for kt in range(NKT):
                    pn = kt % 2
                    pSA, pSB = P[0 + pn], P[2 + pn]
                    if S.c(sem.sACT) >= 1:
                        wait("tensor", sem.sACT, S.c(sem.sACT) - 1)
                    run("tensor", lambda e, pr=pr, kt=kt, pSA=pSA: e.matmul(
                        pSA[:, :], kfm[0:64, pr, kt * 128:(kt + 1) * 128],
                        qfm[0:64, pr, :], tile_position=(0, 0),
                        start=True, stop=True))
                    inc("tensor", lambda e, pr=pr, kt=kt, pSB=pSB: e.matmul(
                        pSB[:, :], kfm[64:128, pr, kt * 128:(kt + 1) * 128],
                        qfm[64:128, pr, :], tile_position=(64, 0),
                        start=True, stop=True), sem.sPE1)
                    wait("scalar", sem.sPE1, S.c(sem.sPE1))
                    run("scalar", lambda e, pn=pn, pSA=pSA: e.activation(
                        out=Pt[:, 2 * pn, :], in_=pSA[:, :], func=AF.Exp,
                        bias=biascols[:, 2:3]))
                    inc("scalar", lambda e, pn=pn, pSB=pSB: e.activation(
                        out=Pt[:, 2 * pn + 1, :], in_=pSB[:, :], func=AF.Exp,
                        bias=biascols[:, 2:3]), sem.sACT)
                    if kt >= 1:
                        po = (kt - 1) % 2
                        wait("tensor", sem.sACT, S.c(sem.sACT) - 1)
                        run("tensor", lambda e, kt=kt, po=po, hA=hA: e.matmul(
                            P[4][0:65, :], vt_h[:, kt - 1, hA, 0:65],
                            Pt[:, 2 * po, :],
                            start=(kt == 1), stop=False))
                        inc("tensor", lambda e, kt=kt, po=po, hB=hB: e.matmul(
                            P[5][0:65, :], vt_h[:, kt - 1, hB, 0:65],
                            Pt[:, 2 * po + 1, :],
                            start=(kt == 1), stop=False), sem.sPE1)
                po = (NKT - 1) % 2
                wait("tensor", sem.sACT, S.c(sem.sACT))
                run("tensor", lambda e, po=po, hA=hA: e.matmul(
                    P[4][0:65, :], vt_h[:, NKT - 1, hA, 0:65],
                    Pt[:, 2 * po, :], start=False, stop=True))
                inc("tensor", lambda e, po=po, hB=hB: e.matmul(
                    P[5][0:65, :], vt_h[:, NKT - 1, hB, 0:65],
                    Pt[:, 2 * po + 1, :], start=False, stop=True), sem.sPE1)
                pv_done = S.c(sem.sPE1)
                wait("vector", sem.sPE1, pv_done)
                for hb in range(2):
                    pO = P[4 + hb]
                    cstep("vector", lambda e, hb=hb, pO=pO: e.tensor_copy(
                        out=onrm[0:65, hb, :], in_=pO[0:65, :]))
                    inc("vector", lambda e, hb=hb: e.reciprocal(
                        rden[0:1, :], onrm[64:65, hb, :]), sem.sDVE)
                    wait("tensor", sem.sDVE, S.c(sem.sDVE))
                    inc("tensor", lambda e: e.matmul(
                        P[6][0:64, :], onesf[0:1, 0:64], rden[0:1, :],
                        start=True, stop=True), sem.sPE1)
                    wait("vector", sem.sPE1, S.c(sem.sPE1))
                    base = 64 * hb
                    inc("vector", lambda e, hb=hb, pr=pr, base=base:
                        e.tensor_tensor(
                            out_norm[base:base + 64, pr, :],
                            onrm[0:64, hb, :], P[6][0:64, :], OP.mult),
                        sem.sDVE)
            onorm_done = S.c(sem.sDVE)

            wait("sync", sem.sDVE, onorm_done)
            for cc in range(CCn):
                dma("sync", sem.sST, wph[:, cc, :],
                    wot_d[cc * 128:(cc + 1) * 128, :])
            wo_tgt = S.c(sem.sST)
            wait("tensor", sem.sST, wo_tgt)
            wait("tensor", sem.sDVE, onorm_done)
            prev_store = None
            for tt4 in range(4):
                tta = qb * 4 + tt4
                if prev_store is not None:
                    wait("sync", sem.sXN, prev_store)
                rs_t = dma("sync", sem.sXN, res_stage[:, :],
                           resid_d[tta * 128:(tta + 1) * 128, :])
                for mh in range(2):
                    pW = P[6 + (2 * tt4 + mh) % 2]
                    if tt4 or mh:
                        wait("tensor", sem.sDVE2, S.c(sem.sDVE2))
                    for pr in range(CCn):
                        run("tensor", lambda e, pr=pr, tt4=tt4, mh=mh, pW=pW:
                            e.matmul(
                                pW[:, :],
                                out_norm[:, pr, tt4 * 128:(tt4 + 1) * 128],
                                wph[:, pr, mh * 512:(mh + 1) * 512],
                                start=(pr == 0), stop=False))
                    inc("tensor", lambda e, mh=mh, pW=pW: e.matmul(
                        pW[:, :], ones1b[0:1, :],
                        borow[0:1, mh * 512:(mh + 1) * 512],
                        start=False, stop=True), sem.sPE1)
                    wait("vector", sem.sPE1, S.c(sem.sPE1))
                    wait("vector", sem.sXN, rs_t)
                    inc("vector", lambda e, mh=mh, pW=pW: e.tensor_add(
                        res_stage[:, mh * 512:(mh + 1) * 512], pW[:, :],
                        res_stage[:, mh * 512:(mh + 1) * 512]), sem.sDVE2)
                wait("sync", sem.sDVE2, S.c(sem.sDVE2))
                prev_store = dma("sync", sem.sXN,
                                 resid_d[tta * 128:(tta + 1) * 128, :],
                                 res_stage[:, :])

        barrier()
        wait("sync", sem.sXN, S.c(sem.sXN))
        for tt in range(TT):
            dma("sync", sem.sXN, xres[:, tt, :],
                resid_d[tt * 128:(tt + 1) * 128, :])
        wait("tensor", sem.sXN, S.c(sem.sXN))
        wait("vector", sem.sXN, S.c(sem.sXN))
        wait("scalar", sem.sXN, S.c(sem.sXN))
        barrier()

    use_sparse = os.environ.get("K_DENSE", "") != "1"
    ffn_fn = emit_ffn_sparse if use_sparse else emit_ffn
    if "ffn1" in stages:
        ffn_fn(0)
        barrier()
    if "attn" in stages:
        emit_attn()
    if "ffn2" in stages:
        ffn_fn(1)
        barrier()

    wait("sync", sem.sDVE2, S.c(sem.sDVE2))
    wait("sync", sem.sDVE, S.c(sem.sDVE))
    for tt in range(TT):
        dma("sync", sem.sOUT, out_d[tt * 128:(tt + 1) * 128, :], xres[:, tt, :])
    wait("sync", sem.sOUT, S.c(sem.sOUT))

    with nc.Block() as block:
        @block.sync
        def _(eng):
            for fn in S.q["sync"]:
                fn(eng)

        @block.scalar
        def _(eng):
            for fn in S.q["scalar"]:
                fn(eng)

        @block.vector
        def _(eng):
            for fn in S.q["vector"]:
                fn(eng)

        @block.tensor
        def _(eng):
            for fn in S.q["tensor"]:
                fn(eng)

        @block.gpsimd
        def _(eng):
            for fn in S.q["gpsimd"]:
                fn(eng)

    stack.close()
    return nc


_NC_CACHE = {}


def kernel(L0_attn_x, L0_attn_Wq, L0_attn_Wk, L0_attn_Wv, L0_attn_Wo,
           L0_attn_bo, L0_attn_ln_g, L0_attn_ln_b, L0_attn_q_gamma,
           L0_attn_k_gamma, L0_ffn_Wg, L0_ffn_W1, L0_ffn_W3, L0_ffn_W2,
           _stages=("ffn1", "attn", "ffn2")):
    x = np.ascontiguousarray(np.asarray(L0_attn_x, dtype=np.float32))
    B, N, Cm = x.shape
    assert (B, N, Cm) == (4, 4096, 1024)
    weights = {
        "wq": np.ascontiguousarray(L0_attn_Wq, dtype=np.float32),
        "wk": np.ascontiguousarray(L0_attn_Wk, dtype=np.float32),
        "wv": np.ascontiguousarray(L0_attn_Wv, dtype=np.float32),
        "wo": np.ascontiguousarray(L0_attn_Wo, dtype=np.float32),
        "bo": np.ascontiguousarray(np.asarray(L0_attn_bo, np.float32).reshape(1, C)),
        "lng": np.ascontiguousarray(np.asarray(L0_attn_ln_g, np.float32).reshape(1, C)),
        "lnb": np.ascontiguousarray(np.asarray(L0_attn_ln_b, np.float32).reshape(1, C)),
        "qg": np.ascontiguousarray(np.asarray(L0_attn_q_gamma, np.float32).reshape(1, DH)),
        "kg": np.ascontiguousarray(np.asarray(L0_attn_k_gamma, np.float32).reshape(1, DH)),
        "wg": np.ascontiguousarray(L0_ffn_Wg, dtype=np.float32),
        "w1": np.ascontiguousarray(L0_ffn_W1, dtype=np.float32),
        "w3": np.ascontiguousarray(L0_ffn_W3, dtype=np.float32),
        "w2": np.ascontiguousarray(L0_ffn_W2, dtype=np.float32),
    }
    key = tuple(_stages)
    if key not in _NC_CACHE:
        _NC_CACHE[key] = build_program(_stages)
    nc = _NC_CACHE[key]
    in_maps = []
    for core in range(8):
        bi, half = core // 2, core % 2
        shard = np.ascontiguousarray(x[bi, half * T:(half + 1) * T, :])
        m = {"x": shard}
        m.update(weights)
        in_maps.append(m)
    res = run_bass_kernel_spmd(nc, in_maps, core_ids=list(range(8)))
    out = np.empty((B, N, Cm), dtype=np.float32)
    global DBG
    DBG = []
    for core in range(8):
        bi, half = core // 2, core % 2
        out[bi, half * T:(half + 1) * T, :] = res.results[core]["out"]
        if "dbg" in res.results[core]:
            DBG.append(np.asarray(res.results[core]["dbg"]))
    return out


# version 11
# speedup vs baseline: 1.0793x; 1.0793x over previous
import numpy as np
import concourse.bass as bass
import concourse.mybir as mybir
from concourse.bass_utils import run_bass_kernel_spmd

F32 = mybir.dt.float32
BF16 = mybir.dt.bfloat16
U8 = mybir.dt.uint8
AF = mybir.ActivationFunctionType
OP = mybir.AluOpType
AX = mybir.AxisListType.X

T = 2048
C = 1024
H = 4096
E = 8
TT = 16
CCn = 8
HHn = 32
NQB = 4
NSG = 8
NKT = 32
DH = 64
EPS_LN = 1e-5
EPS_RMS = 1e-8

ENGINES = ("sync", "scalar", "vector", "tensor", "gpsimd")


class Sched:
    def __init__(self):
        self.q = {e: [] for e in ENGINES}
        self.cnt = {}

    def emit(self, eng, fn):
        self.q[eng].append(fn)

    def c(self, sem):
        return self.cnt.get(sem.name, 0)

    def bump(self, sem, n=1):
        self.cnt[sem.name] = self.c(sem) + n
        return self.cnt[sem.name]


def build_program(stages=("ffn1", "attn", "ffn2")):
    import os
    init_parts = set(os.environ.get("K_INIT", "consts,gb,wt,moe").split(","))
    nc = bass.Bass("TRN2", target_bir_lowering=False, debug=False)

    din = {}
    din["x"] = nc.dram_tensor("x", [T, C], F32, kind="ExternalInput").ap()
    for nm in ("wq", "wk", "wv", "wo"):
        din[nm] = nc.dram_tensor(nm, [C, C], F32, kind="ExternalInput").ap()
    din["bo"] = nc.dram_tensor("bo", [1, C], F32, kind="ExternalInput").ap()
    din["lng"] = nc.dram_tensor("lng", [1, C], F32, kind="ExternalInput").ap()
    din["lnb"] = nc.dram_tensor("lnb", [1, C], F32, kind="ExternalInput").ap()
    din["qg"] = nc.dram_tensor("qg", [1, DH], F32, kind="ExternalInput").ap()
    din["kg"] = nc.dram_tensor("kg", [1, DH], F32, kind="ExternalInput").ap()
    din["wg"] = nc.dram_tensor("wg", [C, E], F32, kind="ExternalInput").ap()
    din["w1"] = nc.dram_tensor("w1", [E, C, H], F32, kind="ExternalInput").ap()
    din["w3"] = nc.dram_tensor("w3", [E, C, H], F32, kind="ExternalInput").ap()
    din["w2"] = nc.dram_tensor("w2", [E, H, C], F32, kind="ExternalInput").ap()
    out_d = nc.dram_tensor("out", [T, C], F32, kind="ExternalOutput").ap()
    import os as _osd
    dbg_what = _osd.environ.get("K_DBG", "")
    dbg_d = nc.dram_tensor("dbg", [2048, 64], F32, kind="ExternalOutput").ap()

    w1c = nc.dram_tensor("w1c", [E, HHn, 128, CCn, 128], BF16).ap()
    w3c = nc.dram_tensor("w3c", [E, HHn, 128, CCn, 128], BF16).ap()
    w2c = nc.dram_tensor("w2c", [E, HHn, 128, C], BF16).ap()
    wqt_d = nc.dram_tensor("wqt_d", [C, C], BF16).ap()
    wkt_d = nc.dram_tensor("wkt_d", [C, C], BF16).ap()
    wvt_d = nc.dram_tensor("wvt_d", [C, C], BF16).ap()
    wot_d = nc.dram_tensor("wot_d", [C, C], BF16).ap()
    xnt_d = nc.dram_tensor("xnt_d", [T, C], BF16).ap()
    cc_in = nc.dram_tensor("cc_in", [T, C], BF16).ap()
    cc_out = nc.dram_tensor("cc_out", [2 * T, C], BF16).ap()
    resid_d = nc.dram_tensor("resid_d", [T, C], F32).ap()
    CAP = 896
    NT = CAP // 128
    NSL = E * CAP
    glist_d = nc.dram_tensor("glist_d", [NSL, 1], mybir.dt.int32).ap()
    xg_d = nc.dram_tensor("xg_d", [2, CAP, C], BF16).ap()
    eo_d = nc.dram_tensor("eo_d", [NSL, C], BF16).ap()

    arenaA = nc.alloc_sbuf_tensor("arenaA", [128, 65536], U8)
    arenaB = nc.alloc_sbuf_tensor("arenaB", [128, 67584], U8)
    arenaC = nc.alloc_sbuf_tensor("arenaC", [128, 68096], U8)

    def carve(arena, off, nbytes, dt, shape):
        ap = arena.ap()[:, off:off + nbytes].bitcast(dt)
        if len(shape) == 1:
            assert ap.shape[1] == shape[0], (ap.shape, shape)
            return ap
        names = " ".join(f"d{i}" for i in range(len(shape)))
        return ap.rearrange(f"p ({names}) -> p {names}",
                            **{f"d{i}": s for i, s in enumerate(shape)})

    xres = carve(arenaA, 0, 65536, F32, (TT, C))
    kfm = carve(arenaA, 0, 65536, BF16, (CCn, 4096))

    xnfm = carve(arenaB, 0, 32768, BF16, (CCn, T))
    hid = carve(arenaB, 32768, 32768, BF16, (HHn, 512))
    vt = carve(arenaB, 0, 66560, BF16, (NKT, 1040))
    vt_h = vt.rearrange("p t (h w) -> p t h w", w=65)

    w1t = carve(arenaC, 51200, 8192, BF16, (4, CCn, 128))
    w3t = carve(arenaC, 59392, 8192, BF16, (4, CCn, 128))
    w2t = carve(arenaC, 8192, 4096, BF16, (2, C))
    sl = carve(arenaC, 12288, 2048, BF16, (2, 512))
    sq = carve(arenaC, 14336, 8192, F32, (2, C))
    xnt_st = carve(arenaC, 22528, 4096, BF16, (2, C))
    wstripe = carve(arenaC, 26624, 8192, BF16, (2, 2048))
    g_bc = carve(arenaC, 43008, 4096, F32, (C,))
    b_bc = carve(arenaC, 47104, 4096, F32, (C,))
    stage = carve(arenaC, 0, 16384, BF16, (2, CCn, 512))
    wph = carve(arenaC, 16384, 16384, BF16, (CCn, C))
    qfm = carve(arenaC, 32768, 8192, BF16, (CCn, 512))
    Pt = carve(arenaC, 40960, 4096, BF16, (4, 512))
    out_norm = carve(arenaC, 45056, 8192, BF16, (CCn, 512))
    onrm = carve(arenaC, 53248, 4096, F32, (2, 512))
    ksq = carve(arenaC, 57344, 2048, BF16, (2, 512))
    rkt = carve(arenaC, 59392, 4096, F32, (2, 512))
    res_stage = carve(arenaC, 63488, 4096, F32, (C,))
    mxbuf = carve(arenaC, 67584, 256, F32, (2, 32))
    rden = carve(arenaC, 40960, 2048, F32, (512,))[0:1, :]
    grow = carve(arenaC, 14336, 4096, F32, (C,))[0:1, :]
    brow = carve(arenaC, 18432, 4096, F32, (C,))[0:1, :]
    biascols = carve(arenaC, 67840, 16, F32, (4,))
    I32 = mybir.dt.int32
    xgstage = carve(arenaC, 0, 4096, BF16, (2, C))
    glii = carve(arenaC, 4096, 256, I32, (64,))
    srt = carve(arenaC, 4352, 128, F32, (16, 2))
    sint = carve(arenaC, 4480, 128, I32, (16, 2))
    sint2 = carve(arenaC, 4608, 128, I32, (16, 2))
    tokid16 = None
    SRr = carve(arenaC, 4800, 512, F32, (128,))
    msumb = carve(arenaC, 5312, 256, BF16, (128,))
    rsc_f = carve(arenaC, 5568, 64, F32, (16,))
    rrow_f = carve(arenaC, 5632, 512, F32, (128,))[0:1, :]
    ecapcol = None
    msk1b = carve(arenaC, 6208, 256, BF16, (16, E))
    msk2b = carve(arenaC, 6464, 256, BF16, (16, E))
    wts = carve(arenaC, 6720, 128, F32, (16, 2))
    usub = None
    u16f = None
    ubAi = carve(arenaC, 26624, 512, I32, (128,))
    ubBi = carve(arenaC, 27136, 512, I32, (128,))
    ubCi = carve(arenaC, 27648, 512, I32, (128,))
    ubDi = carve(arenaC, 28160, 512, I32, (128,))
    tri_a = carve(arenaC, 14976, 64, I32, (16,))
    tri_b = carve(arenaC, 15040, 64, I32, (16,))
    hidg = carve(arenaB, 0, 57344, BF16, (HHn, CAP))
    xg_fm = carve(arenaC, 34816, 14336, BF16, (CCn, CAP))
    sl2 = carve(arenaC, 18432, 3584, BF16, (2, CAP))
    eostage = carve(arenaC, 14336, 4096, BF16, (2, C))
    tri_c = carve(arenaC, 15104, 64, I32, (16,))

    stats = nc.alloc_sbuf_tensor("stats", [128, 16, 8], F32).ap()
    cmat = nc.alloc_sbuf_tensor("cmat", [128, 16, E], F32).ap()
    wgs = nc.alloc_sbuf_tensor("wgs", [128, CCn, E], BF16).ap()
    gsc = nc.alloc_sbuf_tensor("gsc", [128, 3, E], F32).ap()
    gtmp = nc.alloc_sbuf_tensor("gtmp", [128, 8], F32).ap()
    ident_f = nc.alloc_sbuf_tensor("ident_f", [128, 128], F32).ap()
    ident_b = nc.alloc_sbuf_tensor("ident_b", [128, 128], BF16).ap()
    onesf = nc.alloc_sbuf_tensor("onesf", [1, 128], F32).ap()
    ones1b = nc.alloc_sbuf_tensor("ones1b", [1, 128], BF16).ap()
    ones2h = nc.alloc_sbuf_tensor("ones2h", [128, 2], BF16).ap()
    o2tmp = nc.alloc_sbuf_tensor("o2tmp", [128, 2], F32).ap()
    qgam2 = nc.alloc_sbuf_tensor("qgam2", [2, 128], F32).ap()
    kgam2 = nc.alloc_sbuf_tensor("kgam2", [2, 128], F32).ap()
    negmT = nc.alloc_sbuf_tensor("negmT", [1, 4], F32).ap()
    borow = nc.alloc_sbuf_tensor("borow", [1, C], BF16).ap()
    ceps = nc.alloc_sbuf_tensor("ceps", [128, 1], F32).ap()
    onescol_b = nc.alloc_sbuf_tensor("onescol_b", [128, 1], BF16).ap()
    usub = nc.alloc_sbuf_tensor("usub", [128, 128], BF16).ap()
    u16f = nc.alloc_sbuf_tensor("u16f", [128, 128], F32).ap()
    ecapcol = nc.alloc_sbuf_tensor("ecapcol", [128, 1], F32).ap()
    tokid16 = nc.alloc_sbuf_tensor("tokid16", [128, 16], mybir.dt.int32).ap()
    ecrow = nc.alloc_sbuf_tensor("ecrow", [1, 128], F32).ap()
    wtmp = nc.alloc_sbuf_tensor("wtmp", [128, 2, 128], BF16).ap()
    wsto = nc.alloc_sbuf_tensor("wsto", [128, 2, 128], BF16).ap()
    assert nc.sbuf_bytes_remaining >= 0, nc.sbuf_bytes_remaining

    P = [nc.alloc_psum_tensor(f"p{i}", [128, 512], F32).ap() for i in range(8)]
    p_bf = P[7].bitcast(BF16)

    S = Sched()
    import contextlib
    stack = contextlib.ExitStack()
    sems = {}
    for nm in ("sIN", "sW1", "sW3", "sW2", "sST", "sXN", "sSP", "sCC", "sWC",
               "sWS", "sOUT", "sPE1", "sPEh", "sPEeo", "sPEx", "sACT", "sDVE",
               "sDVE2", "sBAR", "sGB", "sVC", "sSC", "sGC", "sEOD"):
        sems[nm] = stack.enter_context(nc.semaphore(nm))
    sem = type("X", (), sems)

    def emit(eng, fn):
        S.emit(eng, fn)

    def wait(eng, s, val):
        if val > 0:
            emit(eng, lambda e, s=s, v=val: e.wait_ge(s, v))

    def dma(eng, s, out, in_, **kw):
        tgt = S.bump(s, 16)
        emit(eng, lambda e, o=out, i=in_, s=s, kw=kw:
             e.dma_start(out=o, in_=i, **kw).then_inc(s, 16))
        return tgt

    from concourse.bass_core_rust import IndirectOffsetOnAxis

    def idma(eng, s, out, in_, out_off=None, in_off=None):
        tgt = S.bump(s, 16)
        emit(eng, lambda e, o=out, i=in_, s=s, oo=out_off, io=in_off:
             e.indirect_dma_start(
                 o, IndirectOffsetOnAxis(oo, 0) if oo is not None else None,
                 i, IndirectOffsetOnAxis(io, 0) if io is not None else None
             ).then_inc(s, 16))
        return tgt

    def inc(eng, instr_fn, s, n=1):
        tgt = S.bump(s, n)
        emit(eng, lambda e, f=instr_fn, s=s, n=n: f(e).then_inc(s, n))
        return tgt

    def run(eng, instr_fn):
        emit(eng, lambda e, f=instr_fn: f(e))

    chain_sems = {}

    def cstep(eng, instr_fn):
        sc = chain_sems[eng]
        tgt = inc(eng, instr_fn, sc)
        wait(eng, sc, tgt)

    chain_sems.update({"vector": sem.sVC, "scalar": sem.sSC,
                       "gpsimd": sem.sGC, "tensor": sem.sPEx})

    nbar = [0]

    def barrier():
        nbar[0] += 1
        tgt = nbar[0] * len(ENGINES)
        S.bump(sem.sBAR, len(ENGINES))
        for eng in ENGINES:
            emit(eng, lambda e: e.sem_inc(sem.sBAR, 1))
            wait(eng, sem.sBAR, tgt)

    for tt in range(TT):
        dma("sync", sem.sIN, xres[:, tt, :], din["x"][tt * 128:(tt + 1) * 128, :])
    x_loaded = S.c(sem.sIN)

    cstep("gpsimd", lambda e: e.iota(
        ident_f[:, :], [[1, 128]], channel_multiplier=-1,
        allow_small_or_imprecise_dtypes=True))
    cstep("gpsimd", lambda e: e.tensor_scalar(
        ident_f[:, :], ident_f[:, :], 0.0, None, OP.is_equal))
    run("gpsimd", lambda e: e.memset(onesf[:, :], 1.0))
    run("gpsimd", lambda e: e.memset(ones1b[:, :], 1.0))
    run("gpsimd", lambda e: e.memset(qgam2[0:2, :], 0.0))
    run("gpsimd", lambda e: e.memset(kgam2[0:2, :], 0.0))
    run("gpsimd", lambda e: e.memset(ceps[:, :], EPS_LN))
    cstep("gpsimd", lambda e: e.iota(
        o2tmp[:, 0:1], [[1, 1]], channel_multiplier=1,
        allow_small_or_imprecise_dtypes=True))
    cstep("gpsimd", lambda e: e.tensor_scalar(
        o2tmp[:, 1:2], o2tmp[:, 0:1], 63.5, None, OP.is_gt))
    inc("gpsimd", lambda e: e.tensor_scalar(
        o2tmp[:, 0:1], o2tmp[:, 1:2], -1.0, 1.0, OP.mult, OP.add), sem.sGB)
    wait("scalar", sem.sGB, 1)
    inc("scalar", lambda e: e.copy(out=ident_b[:, :], in_=ident_f[:, :]), sem.sACT)
    inc("scalar", lambda e: e.copy(out=ones2h[:, :], in_=o2tmp[:, :]), sem.sACT)
    consts_cast = S.c(sem.sACT)

    def _emit_const_loads():
        dma("gpsimd", sem.sIN, qgam2[0:1, 0:64], din["qg"][0:1, :])
        dma("gpsimd", sem.sIN, qgam2[1:2, 64:128], din["qg"][0:1, :])
        dma("gpsimd", sem.sIN, kgam2[0:1, 0:64], din["kg"][0:1, :])
        dma("gpsimd", sem.sIN, kgam2[1:2, 64:128], din["kg"][0:1, :])
        dma("gpsimd", sem.sIN, grow[:, :], din["lng"][:, :])
        dma("gpsimd", sem.sIN, brow[:, :], din["lnb"][:, :])
        dma("gpsimd", sem.sIN, borow[:, :], din["bo"][:, :])
        tgt = dma("gpsimd", sem.sIN, wgs[:, :, :],
                  din["wg"].rearrange("(cc p) e -> p cc e", p=128))
        wait("vector", sem.sIN, tgt)
        inc("vector", lambda e: e.tensor_scalar_mul(
            qgam2[0:2, :], qgam2[0:2, :], 0.125), sem.sDVE)
        return tgt

    def _emit_gb_bcast():
        dma("gpsimd", sem.sIN, grow[:, :], din["lng"][:, :])
        gb_tgt = dma("gpsimd", sem.sIN, brow[:, :], din["lnb"][:, :])
        wait("tensor", sem.sIN, gb_tgt)
        wait("tensor", sem.sACT, max(consts_cast, S.c(sem.sACT)))
        wait("tensor", sem.sDVE, S.c(sem.sDVE))
        wait("tensor", sem.sDVE2, S.c(sem.sDVE2))
        for half in range(2):
            if half == 1:
                wait("tensor", sem.sDVE, S.c(sem.sDVE))
            run("tensor", lambda e, h=half: e.matmul(
                P[6][:, :], onesf[0:1, :], grow[0:1, h * 512:(h + 1) * 512],
                start=True, stop=True))
            inc("tensor", lambda e, h=half: e.matmul(
                P[5][:, :], onesf[0:1, :], brow[0:1, h * 512:(h + 1) * 512],
                start=True, stop=True), sem.sPEx)
            wait("vector", sem.sPEx, S.c(sem.sPEx))
            run("vector", lambda e, h=half: e.tensor_copy(
                out=g_bc[:, h * 512:(h + 1) * 512], in_=P[6][:, :]))
            inc("vector", lambda e, h=half: e.tensor_copy(
                out=b_bc[:, h * 512:(h + 1) * 512], in_=P[5][:, :]), sem.sDVE)

    def _emit_wt_transposes():
        import os as _os
        wtn = int(_os.environ.get("K_WTN", "256"))
        wt_skip_store = _os.environ.get("K_WT_NOSTORE", "") == "1"
        wt_skip_dve = _os.environ.get("K_WT_NODVE", "") == "1"
        wt_skip_tr = _os.environ.get("K_WT_NOTRANS", "") == "1"
        wait("tensor", sem.sACT, consts_cast)
        store_tgts = []
        n_tr = 0
        for nm, dst in (("wq", wqt_d), ("wk", wkt_d), ("wv", wvt_d),
                        ("wo", wot_d)):
            for mm in range(8):
                for cc in range(8):
                    n = n_tr
                    if n >= wtn:
                        continue
                    par = n % 2
                    if n >= 2:
                        wait("gpsimd", sem.sPEx, S.c(sem.sPEx) - 1)
                    lt = dma("gpsimd", sem.sWC, wtmp[:, par, :],
                             din[nm][mm * 128:(mm + 1) * 128,
                                     cc * 128:(cc + 1) * 128])
                    wait("tensor", sem.sWC, lt)
                    if n >= 1:
                        wait("tensor", sem.sDVE2, S.c(sem.sDVE2))
                    if wt_skip_tr:
                        inc("tensor", lambda e: e.engine_nop()
                            if False else e.nop(), sem.sPEx)
                    else:
                        inc("tensor", lambda e, par=par: e.transpose(
                            p_bf[:, par * 128:par * 128 + 128], wtmp[:, par, :],
                            ident_b[:, :]), sem.sPEx)
                    wait("vector", sem.sPEx, S.c(sem.sPEx))
                    if n >= 2:
                        wait("vector", sem.sWS, store_tgts[n - 2])
                    if wt_skip_dve or wt_skip_tr:
                        inc("vector", lambda e, par=par: e.tensor_copy(
                            out=wsto[:, par, :], in_=wtmp[:, par, :]),
                            sem.sDVE2)
                    else:
                        inc("vector", lambda e, par=par: e.tensor_copy(
                            out=wsto[:, par, :],
                            in_=p_bf[:, par * 128:par * 128 + 128]), sem.sDVE2)
                    wait("sync", sem.sDVE2, S.c(sem.sDVE2))
                    if wt_skip_store:
                        st = S.bump(sem.sWS, 16)
                        emit("sync", lambda e: e.sem_inc(sem.sWS, 16))
                    else:
                        st = dma("sync", sem.sWS,
                                 dst[cc * 128:(cc + 1) * 128,
                                     mm * 128:(mm + 1) * 128],
                                 wsto[:, par, :])
                    store_tgts.append(st)
                    n_tr += 1

    relay_tgt = {}

    def _emit_relayout_expert(eidx):
        units = []
        for cc in range(CCn):
            units += [("w1", eidx, cc, 0), ("w1", eidx, cc, 1),
                      ("w3", eidx, cc, 0), ("w3", eidx, cc, 1)]
        units += [("w2", eidx, hh, 0) for hh in range(HHn)]
        _emit_relayout_units(units)

    rl_n = [0]

    def _emit_relayout_units(units):
        for (kind, eidx, i, hf) in units:
            par = rl_n[0] % 2
            rl_n[0] += 1
            wait("gpsimd", sem.sWS, S.c(sem.sWS) - 16)
            if kind in ("w1", "w3"):
                src = din[kind]
                dstc = w1c if kind == "w1" else w3c
                lt = dma("gpsimd", sem.sWC, wstripe[:, par, :],
                         src[eidx, i * 128:(i + 1) * 128,
                             hf * 2048:(hf + 1) * 2048])
                wait("gpsimd", sem.sWC, lt)
                st = dma("gpsimd", sem.sWS,
                         dstc[eidx, hf * 16:(hf + 1) * 16, :, i, :]
                         .transpose([1, 0, 2]),
                         wstripe[:, par, :].rearrange("p (hh w) -> p hh w",
                                                      w=128))
                relay_tgt[(kind, eidx, i, hf)] = st
                relay_tgt[(kind, eidx, i)] = st
            else:
                lt = dma("gpsimd", sem.sWC, wstripe[:, par, 0:1024],
                         din["w2"][eidx, i * 128:(i + 1) * 128, :])
                wait("gpsimd", sem.sWC, lt)
                st = dma("gpsimd", sem.sWS, w2c[eidx, i, :, :],
                         wstripe[:, par, 0:1024])
                relay_tgt[("w2", eidx, i)] = st

    def _emit_moe_cache():
        st_list = []
        units = []
        for eidx in range(E):
            for cc in range(CCn):
                units.append(("w1", eidx, cc))
                units.append(("w3", eidx, cc))
            for hh in range(HHn):
                units.append(("w2", eidx, hh))
        for u, (kind, eidx, i) in enumerate(units):
            par = u % 2
            if u >= 2:
                wait("gpsimd", sem.sWS, st_list[u - 2])
            if kind == "w1":
                lt = dma("gpsimd", sem.sWC, wstripe[:, par, :],
                         din["w1"][eidx, i * 128:(i + 1) * 128, :])
                wait("gpsimd", sem.sWC, lt)
                st = dma("gpsimd", sem.sWS,
                         w1c[eidx, :, :, i, :].transpose([1, 0, 2]),
                         wstripe[:, par, :].rearrange("p (hh w) -> p hh w",
                                                      w=128))
            elif kind == "w3":
                lt = dma("gpsimd", sem.sWC, wstripe[:, par, :],
                         din["w3"][eidx, i * 128:(i + 1) * 128, :])
                wait("gpsimd", sem.sWC, lt)
                st = dma("gpsimd", sem.sWS,
                         w3c[eidx, :, :, i, :].transpose([1, 0, 2]),
                         wstripe[:, par, :].rearrange("p (hh w) -> p hh w",
                                                      w=128))
            else:
                lt = dma("gpsimd", sem.sWC, wstripe[:, par, 0:1024],
                         din["w2"][eidx, i * 128:(i + 1) * 128, :])
                wait("gpsimd", sem.sWC, lt)
                st = dma("gpsimd", sem.sWS, w2c[eidx, i, :, :],
                         wstripe[:, par, 0:1024])
            st_list.append(st)
            relay_tgt[(kind, eidx, i)] = st

    def _emit_route_consts():
        wait("gpsimd", sem.sGB, 1)
        cstep("gpsimd", lambda e: e.iota(ubBi[:, :], [[1, 128]],
              channel_multiplier=-1,
              allow_small_or_imprecise_dtypes=True))
        inc("gpsimd", lambda e: e.iota(ubDi[:, 0:1], [[1, 1]],
            channel_multiplier=1), sem.sGB)
        iotas_done = S.c(sem.sGB)
        wait("vector", sem.sGB, iotas_done)
        rfull = carve(arenaC, 5632, 512, F32, (128,))
        cstep("vector", lambda e: e.tensor_copy(out=rfull[:, :],
                                                in_=ubBi[:, :]))
        cstep("vector", lambda e: e.tensor_scalar(
            u16f[:, :], rfull[:, :], 0.0, None, OP.is_gt))
        cstep("vector", lambda e: e.tensor_copy(out=usub[:, :],
                                                in_=u16f[:, :]))
        cstep("vector", lambda e: e.tensor_scalar(
            ubCi[:, :], ubBi[:, :], 7, None, OP.bitwise_and))
        cstep("vector", lambda e: e.tensor_copy(out=rfull[:, :],
                                                in_=ubCi[:, :]))
        cstep("vector", lambda e: e.tensor_scalar(
            rfull[:, :], rfull[:, :], 0.0, None, OP.is_equal))
        cstep("vector", lambda e: e.tensor_mul(
            u16f[:, :], u16f[:, :], rfull[:, :]))
        cstep("vector", lambda e: e.tensor_scalar(
            ubDi[:, 1:2], ubDi[:, 0:1], 7, None, OP.bitwise_and))
        cstep("vector", lambda e: e.tensor_scalar(
            ubDi[:, 2:3], ubDi[:, 1:2], 9, None, OP.arith_shift_left))
        cstep("vector", lambda e: e.tensor_scalar(
            ubDi[:, 3:4], ubDi[:, 1:2], 8, None, OP.arith_shift_left))
        cstep("vector", lambda e: e.tensor_tensor(
            ubDi[:, 2:3], ubDi[:, 2:3], ubDi[:, 3:4], OP.add))
        cstep("vector", lambda e: e.tensor_scalar(
            ubDi[:, 3:4], ubDi[:, 1:2], 7, None, OP.arith_shift_left))
        cstep("vector", lambda e: e.tensor_tensor(
            ubDi[:, 2:3], ubDi[:, 2:3], ubDi[:, 3:4], OP.add))
        inc("vector", lambda e: e.tensor_copy(
            out=ecapcol[:, 0:1], in_=ubDi[:, 2:3]), sem.sDVE)
        wait("tensor", sem.sDVE, S.c(sem.sDVE))
        inc("tensor", lambda e: e.transpose(
            P[6][0:1, 0:128], ecapcol[:, 0:1], ident_f[:, :]), sem.sPE1)
        wait("vector", sem.sPE1, S.c(sem.sPE1))
        cstep("vector", lambda e: e.tensor_copy(
            out=ecrow[0:1, :], in_=P[6][0:1, 0:128]))
        inc("vector", lambda e: e.tensor_scalar_add(
            ecrow[0:1, :], ecrow[0:1, :], float(CAP - 1)), sem.sDVE)
        rc = S.c(sem.sDVE)
        cstep("gpsimd", lambda e: e.iota(
            tokid16[:, :], [[128, 16]], channel_multiplier=1))
        run("gpsimd", lambda e: e.memset(onescol_b[:, :], 1.0))
        cstep("gpsimd", lambda e: e.memset(glii[:, :], 0))
        dma("gpsimd", sem.sGB, glist_d.rearrange("(p j) one -> p (j one)", p=128),
            glii[:, 0:56])
        return rc

    def _emit_softmax_bound():
        wait("vector", sem.sIN, gam_loaded[0])
        wait("vector", sem.sDVE, S.c(sem.sDVE))
        cstep("vector", lambda e: e.reduce_max(
            negmT[0:1, 0:1], qgam2[0:1, 0:64], axis=AX,
            apply_absolute_value=True))
        cstep("vector", lambda e: e.reduce_max(
            negmT[0:1, 1:2], kgam2[0:1, 0:64], axis=AX,
            apply_absolute_value=True))
        cstep("vector", lambda e: e.tensor_mul(
            negmT[0:1, 2:3], negmT[0:1, 0:1], negmT[0:1, 1:2]))
        inc("vector", lambda e: e.tensor_scalar_mul(
            negmT[0:1, 3:4], negmT[0:1, 2:3], -64.0), sem.sDVE)
        wait("tensor", sem.sDVE, S.c(sem.sDVE))
        wait("tensor", sem.sPEx, S.c(sem.sPEx))
        inc("tensor", lambda e: e.matmul(
            P[2][:, 0:1], onesf[0:1, :], negmT[0:1, 3:4],
            start=True, stop=True), sem.sPE1)
        wait("vector", sem.sPE1, S.c(sem.sPE1))
        inc("vector", lambda e: e.tensor_copy(
            out=biascols[:, 2:3], in_=P[2][:, 0:1]), sem.sDVE)

    gam_loaded = [0]
    if "consts" in init_parts:
        gam_loaded[0] = _emit_const_loads()
    else:
        gam_loaded[0] = dma("gpsimd", sem.sIN, wgs[:, :, :],
                            din["wg"].rearrange("(cc p) e -> p cc e", p=128))
    if "gb" in init_parts:
        _emit_gb_bcast()
    if "wt" in init_parts:
        _emit_wt_transposes()
    _emit_softmax_bound()
    rc_done = _emit_route_consts()

    wait("scalar", sem.sIN, x_loaded)
    wait("vector", sem.sIN, x_loaded)
    barrier()
    if "moe" in init_parts and os.environ.get("K_DENSE", "") == "1":
        _emit_moe_cache()
    wc_done = S.c(sem.sWS)

    def emit_ln(is_attn):
        dst_d = cc_in if is_attn else xnt_d
        if is_attn:
            wait("scalar", sem.sPEx, S.c(sem.sPEx))
            wait("vector", sem.sPEx, S.c(sem.sPEx))
        ssq_base = S.c(sem.sDVE)
        act_base = S.c(sem.sACT)
        for tt in range(TT):
            par = tt % 2
            if tt >= 2:
                wait("scalar", sem.sDVE, ssq_base + tt - 1)
            inc("scalar", lambda e, tt=tt, par=par: e.square(
                out=sq[:, par, :], in_=xres[:, tt, :]), sem.sACT)
            run("vector", lambda e, tt=tt: e.reduce_sum(
                stats[:, tt, 0:1], xres[:, tt, :], axis=AX))
            wait("vector", sem.sACT, act_base + tt + 1)
            inc("vector", lambda e, tt=tt, par=par: e.reduce_sum(
                stats[:, tt, 1:2], sq[:, par, :], axis=AX), sem.sDVE)
            S.bump(sem.sDVE, 0)
        wait("vector", sem.sDVE, S.c(sem.sDVE))
        cstep("vector", lambda e: e.tensor_scalar_mul(
            stats[:, :, 2:3], stats[:, :, 0:1], 1.0 / C))
        cstep("vector", lambda e: e.tensor_mul(
            stats[:, :, 6:7], stats[:, :, 2:3], stats[:, :, 2:3]))
        inc("vector", lambda e: e.scalar_tensor_tensor(
            stats[:, :, 3:4], stats[:, :, 1:2], 1.0 / C, stats[:, :, 6:7],
            OP.mult, OP.subtract), sem.sDVE)
        wait("scalar", sem.sDVE, S.c(sem.sDVE))
        inc("scalar", lambda e: e.activation(
            out=stats[:, :, 4:5], in_=stats[:, :, 3:4], func=AF.Sqrt,
            bias=ceps[:, 0:1]), sem.sACT)
        wait("vector", sem.sACT, S.c(sem.sACT))
        cstep("vector", lambda e: e.reciprocal(stats[:, :, 5:6], stats[:, :, 4:5]))
        if is_attn:
            cstep("vector", lambda e: e.tensor_mul(
                stats[:, :, 6:7], stats[:, :, 5:6], stats[:, :, 5:6]))
            inc("vector", lambda e: e.tensor_mul(
                stats[:, :, 6:7], stats[:, :, 3:4], stats[:, :, 6:7]), sem.sDVE)
            wait("scalar", sem.sDVE, S.c(sem.sDVE))
            inc("scalar", lambda e: e.activation(
                out=stats[:, :, 4:5], in_=stats[:, :, 6:7], func=AF.Sqrt,
                bias=ceps[:, 0:1]), sem.sACT)
            wait("vector", sem.sACT, S.c(sem.sACT))
            cstep("vector", lambda e: e.reciprocal(stats[:, :, 6:7], stats[:, :, 4:5]))
            cstep("vector", lambda e: e.tensor_mul(
                stats[:, :, 5:6], stats[:, :, 5:6], stats[:, :, 6:7]))
        st_tgts = []
        for tt in range(TT):
            par = tt % 2
            if tt >= 2:
                wait("vector", sem.sXN, st_tgts[tt - 2])
            if is_attn:
                cstep("vector", lambda e, tt=tt, par=par: e.tensor_scalar(
                    sq[:, par, :], xres[:, tt, :], stats[:, tt, 2:3],
                    stats[:, tt, 5:6], OP.subtract, OP.mult))
                cstep("vector", lambda e, par=par: e.tensor_mul(
                    sq[:, par, :], sq[:, par, :], g_bc[:, :]))
                inc("vector", lambda e, par=par: e.tensor_add(
                    xnt_st[:, par, :], sq[:, par, :], b_bc[:, :]), sem.sDVE)
            else:
                inc("vector", lambda e, tt=tt, par=par: e.tensor_scalar(
                    xnt_st[:, par, :], xres[:, tt, :], stats[:, tt, 2:3],
                    stats[:, tt, 5:6], OP.subtract, OP.mult), sem.sDVE)
            wait("sync", sem.sDVE, S.c(sem.sDVE))
            st_tgts.append(dma("sync", sem.sXN,
                               dst_d[tt * 128:(tt + 1) * 128, :],
                               xnt_st[:, par, :]))
        return S.c(sem.sXN)

    def emit_gating():
        for tt in range(TT):
            for cc in range(CCn):
                fn = (lambda e, tt=tt, cc=cc: e.matmul(
                    P[0][:, tt * E:(tt + 1) * E],
                    xnfm[:, cc, tt * 128:(tt + 1) * 128],
                    wgs[:, cc, :], start=(cc == 0), stop=(cc == CCn - 1)))
                if tt == TT - 1 and cc == CCn - 1:
                    inc("tensor", fn, sem.sPE1)
                else:
                    run("tensor", fn)
        wait("vector", sem.sPE1, S.c(sem.sPE1))
        sqf = sq.rearrange("p a b -> p (a b)")
        lg = sqf[:, 0:128].rearrange("p (a b) -> p a b", b=8)
        lg2 = sqf[:, 128:256].rearrange("p (a b) -> p a b", b=8)
        m1 = sqf[:, 256:272]
        m2 = sqf[:, 272:288]
        dm = sqf[:, 288:304]
        sg = sqf[:, 304:320]
        m1b = sqf[:, 256:272].unsqueeze(2).broadcast_to([128, 16, 8])
        m2b = sqf[:, 272:288].unsqueeze(2).broadcast_to([128, 16, 8])
        msk1v = msk1b[:, :, :]
        msk2v = msk2b[:, :, :]
        cstep("vector", lambda e: e.tensor_copy(
            out=sqf[:, 0:128], in_=P[0][:, 0:128]))
        cstep("vector", lambda e: e.tensor_reduce(
            m1, lg, axis=AX, op=OP.max))
        cstep("vector", lambda e: e.tensor_tensor(
            msk1v, lg, m1b, OP.is_equal))
        cstep("vector", lambda e: e.scalar_tensor_tensor(
            lg2, msk1v, -1e30, lg, OP.mult, OP.add))
        cstep("vector", lambda e: e.tensor_reduce(
            m2, lg2, axis=AX, op=OP.max))
        cstep("vector", lambda e: e.tensor_tensor(
            msk2v, lg2, m2b, OP.is_equal))
        inc("vector", lambda e: e.tensor_tensor(
            dm, m1, m2, OP.subtract), sem.sDVE)
        wait("scalar", sem.sDVE, S.c(sem.sDVE))
        inc("scalar", lambda e: e.activation(
            out=sg, in_=dm, func=AF.Sigmoid), sem.sACT)
        wait("vector", sem.sACT, S.c(sem.sACT))
        cstep("vector", lambda e: e.tensor_scalar_mul(
            wts[:, :, 0:1].rearrange("p a b -> p (a b)"), sg, 0.5))
        inc("vector", lambda e: e.tensor_scalar(
            wts[:, :, 1:2].rearrange("p a b -> p (a b)"),
            wts[:, :, 0:1].rearrange("p a b -> p (a b)"),
            -1.0, 0.5, OP.mult, OP.add), sem.sDVE)

    import os as _os2
    ffn_parts = set(_os2.environ.get("K_FFN", "ln,gate,exp").split(","))
    n_exp = int(_os2.environ.get("K_NEXP", "8"))
    exp_parts = set(_os2.environ.get("K_EXP", "h,eo,evac").split(","))

    def emit_ffn(call):
        xn_tgt = emit_ln(is_attn=False)
        wait("sync", sem.sXN, xn_tgt)
        for cc in range(CCn):
            dma("sync", sem.sXN, xnfm[:, cc, :],
                xnt_d[:, cc * 128:(cc + 1) * 128], transpose=True)
        xnfm_tgt = S.c(sem.sXN)
        wait("tensor", sem.sXN, xnfm_tgt)
        wait("tensor", sem.sDVE2, S.c(sem.sDVE2))
        wait("tensor", sem.sACT, S.c(sem.sACT))
        if "gate" in ffn_parts:
            emit_gating()
        if "exp" not in ffn_parts:
            return
        if dbg_what in ("cmat", "logits") and call == 0:
            wait("gpsimd", sem.sDVE, S.c(sem.sDVE))
            dma("gpsimd", sem.sOUT,
                dbg_d[:, 0:8].rearrange("(tt p) e -> p tt e", p=128),
                cmat[:, :, :])
        if dbg_what == "xn" and call == 0:
            wait("gpsimd", sem.sXN, S.c(sem.sXN))
            dma("gpsimd", sem.sOUT, dbg_d, xnt_d[:, 0:64])

        if dbg_what in ("w1c", "w2c") and call == 0:
            src = (w1c if dbg_what == "w1c" else w2c.rearrange(
                "e hh p (cc w) -> e hh p cc w", w=128))
            wait("gpsimd", sem.sWS, wc_done)
            dma("gpsimd", sem.sOUT,
                dbg_d.rearrange("(p a) b -> p (a b)", p=128),
                src[0, 0, :, :, :].rearrange("p cc w -> p (cc w)"))
        wait("tensor", sem.sDVE, S.c(sem.sDVE))

        for eidx in range(n_exp):
            for tg in range(4):
                for hh in range(HHn if "h" in exp_parts else 0):
                    par = hh % 4
                    glob = S.c(sem.sPEh)
                    if glob >= 4:
                        wait("sync", sem.sPEh, glob - 3)
                    import os as _os4
                    if call == 0 and relay_tgt:
                        wait("sync", sem.sWS,
                             max(relay_tgt[("w1", eidx, 7)],
                                 relay_tgt[("w3", eidx, 7)]))
                    t1 = dma("sync", sem.sW1, w1t[:, par, :, :],
                             w1c[eidx, hh, :, :, :])
                    t3 = dma("sync", sem.sW3, w3t[:, par, :, :],
                             w3c[eidx, hh, :, :, :])
                    wait("tensor", sem.sW1, t1)
                    if hh <= 1:
                        wait("tensor", sem.sDVE2, S.c(sem.sDVE2))
                    if glob >= 2:
                        wait("tensor", sem.sACT, S.c(sem.sACT) - 1)
                        wait("tensor", sem.sDVE, S.c(sem.sDVE) - 1)
                    for cc in range(CCn):
                        fn = (lambda e, hh=hh, cc=cc, tg=tg, par=par: e.matmul(
                            P[2 * (par % 2)][:, :], w1t[:, par, cc, :],
                            xnfm[:, cc, tg * 512:(tg + 1) * 512],
                            start=(cc == 0), stop=(cc == CCn - 1)))
                        if cc == CCn - 1:
                            inc("tensor", fn, sem.sPE1)
                        else:
                            run("tensor", fn)
                    wait("tensor", sem.sW3, t3)
                    for cc in range(CCn):
                        fn = (lambda e, hh=hh, cc=cc, tg=tg, par=par: e.matmul(
                            P[2 * (par % 2) + 1][:, :], w3t[:, par, cc, :],
                            xnfm[:, cc, tg * 512:(tg + 1) * 512],
                            start=(cc == 0), stop=(cc == CCn - 1)))
                        if cc == CCn - 1:
                            inc("tensor", fn, sem.sPEh)
                        else:
                            run("tensor", fn)
                    wait("scalar", sem.sPE1, S.c(sem.sPE1))
                    if glob >= 2:
                        wait("scalar", sem.sDVE, S.c(sem.sDVE) - 1)
                    inc("scalar", lambda e, par=par: e.activation(
                        out=sl[:, par % 2, :], in_=P[2 * (par % 2)][:, :],
                        func=AF.Silu), sem.sACT)
                    wait("vector", sem.sPEh, S.c(sem.sPEh))
                    wait("vector", sem.sACT, S.c(sem.sACT))
                    inc("vector", lambda e, hh=hh, par=par: e.tensor_tensor(
                        hid[:, hh, :], sl[:, par % 2, :],
                        P[2 * (par % 2) + 1][:, :], OP.mult), sem.sDVE)
                if "eo" not in exp_parts:
                    continue
                hid_done = S.c(sem.sDVE)
                wait("tensor", sem.sDVE, hid_done)
                wait("tensor", sem.sDVE2, S.c(sem.sDVE2))
                for hh in range(HHn):
                    par = hh % 2
                    globe = S.c(sem.sPEeo)
                    if globe >= 2:
                        wait("sync", sem.sPEeo, globe - 1)
                    if call == 0 and relay_tgt:
                        wait("sync", sem.sWS, relay_tgt[("w2", eidx, hh)])
                    t2 = dma("sync", sem.sW2, w2t[:, par, :],
                             w2c[eidx, hh, :, :])
                    wait("tensor", sem.sW2, t2)
                    for j in range(8):
                        ch, tt4 = j // 4, j % 4
                        fn = (lambda e, hh=hh, par=par, ch=ch, tt4=tt4, j=j:
                              e.matmul(
                                  P[j][:, :],
                                  hid[:, hh, tt4 * 128:(tt4 + 1) * 128],
                                  w2t[:, par, ch * 512:(ch + 1) * 512],
                                  start=(hh == 0), stop=(hh == HHn - 1)))
                        if j == 7:
                            inc("tensor", fn, sem.sPEeo)
                        else:
                            run("tensor", fn)
                if "evac" not in exp_parts:
                    continue
                wait("vector", sem.sPEeo, S.c(sem.sPEeo))
                for j in range(8):
                    ch, tt4 = j // 4, j % 4
                    tta = tg * 4 + tt4
                    inc("vector", lambda e, j=j, ch=ch, tta=tta, eidx=eidx:
                        e.scalar_tensor_tensor(
                            xres[:, tta, ch * 512:(ch + 1) * 512], P[j][:, :],
                            cmat[:, tta, eidx:eidx + 1],
                            xres[:, tta, ch * 512:(ch + 1) * 512],
                            OP.mult, OP.add), sem.sDVE2)

        if dbg_what == "hid" and call == 0:
            wait("gpsimd", sem.sDVE, S.c(sem.sDVE))
            dma("gpsimd", sem.sOUT,
                dbg_d[0:1024, :].rearrange("(p a) b -> p (a b)", p=128),
                hid[:, 0, :])


    def emit_ffn_sparse(call):
        xn_tgt = emit_ln(is_attn=False)
        wait("sync", sem.sXN, xn_tgt)
        for cc in range(CCn):
            dma("sync", sem.sXN, xnfm[:, cc, :],
                xnt_d[:, cc * 128:(cc + 1) * 128], transpose=True)
        xnfm_tgt = S.c(sem.sXN)
        wait("tensor", sem.sXN, xnfm_tgt)
        wait("tensor", sem.sDVE2, S.c(sem.sDVE2))
        wait("tensor", sem.sACT, S.c(sem.sACT))
        emit_gating()
        gate_done = S.c(sem.sDVE)

        m1v = msk1b.rearrange("p a b -> p (a b)")
        m2v = msk2b.rearrange("p a b -> p (a b)")
        wait("vector", sem.sDVE, gate_done)
        inc("vector", lambda e: e.tensor_add(msumb[:, :], m1v, m2v), sem.sDVE)
        msum_tgt = S.c(sem.sDVE)
        wait("tensor", sem.sDVE, max(msum_tgt, rc_done))
        run("tensor", lambda e: e.matmul(
            P[2][:, 0:128], usub[:, :], msumb[:, :], start=True, stop=True))
        inc("tensor", lambda e: e.matmul(
            P[2][0:1, 128:256], onescol_b[:, :], msumb[:, :],
            start=True, stop=True), sem.sPE1)
        wait("vector", sem.sPE1, S.c(sem.sPE1))
        inc("vector", lambda e: e.tensor_copy(out=SRr[:, :], in_=P[2][:, 0:128]),
            sem.sDVE)
        cstep("vector", lambda e: e.tensor_copy(
            out=rrow_f[0:1, :], in_=P[2][0:1, 128:256]))
        inc("vector", lambda e: e.nop() if False else e.tensor_copy(
            out=rsc_f[:, 0:1], in_=rsc_f[:, 0:1]), sem.sDVE)
        cnt_tgt = S.c(sem.sDVE)
        wait("tensor", sem.sDVE, cnt_tgt)
        inc("tensor", lambda e: e.transpose(
            P[6][0:128, 0:1], rrow_f[0:1, :], ident_f[0:1, 0:1]), sem.sPE1)
        wait("vector", sem.sPE1, S.c(sem.sPE1))
        inc("vector", lambda e: e.tensor_copy(out=rsc_f[:, 1:2],
                                              in_=P[6][0:128, 0:1]), sem.sDVE)
        wait("tensor", sem.sDVE, S.c(sem.sDVE))
        inc("tensor", lambda e: e.matmul(
            P[6][:, 1:2], u16f[:, :], rsc_f[:, 1:2], start=True, stop=True),
            sem.sPE1)
        wait("vector", sem.sPE1, S.c(sem.sPE1))
        inc("vector", lambda e: e.tensor_add(
            rsc_f[:, 2:3], P[6][:, 1:2], ecapcol[:, 0:1]), sem.sDVE)
        wait("tensor", sem.sDVE, S.c(sem.sDVE))
        inc("tensor", lambda e: e.transpose(
            P[6][0:1, 2:130], rsc_f[:, 2:3], ident_f[:, :]), sem.sPE1)
        wait("vector", sem.sPE1, S.c(sem.sPE1))
        inc("vector", lambda e: e.tensor_copy(
            out=rrow_f[0:1, :], in_=P[6][0:1, 2:130]), sem.sDVE)
        wait("tensor", sem.sDVE, S.c(sem.sDVE))
        inc("tensor", lambda e: e.matmul(
            P[6][:, 256:384], onesf[0:1, :], rrow_f[0:1, :],
            start=True, stop=True), sem.sPE1)
        wait("vector", sem.sPE1, S.c(sem.sPE1))
        inc("vector", lambda e: e.tensor_add(
            SRr[:, :], SRr[:, :], P[6][:, 256:384]), sem.sDVE)
        if os.environ.get("K_NOCLAMP", "") != "1":
            wait("tensor", sem.sDVE, S.c(sem.sDVE))
            inc("tensor", lambda e: e.matmul(
                P[6][:, 0:128], onesf[0:1, :], ecrow[0:1, :],
                start=True, stop=True), sem.sPE1)
            wait("vector", sem.sPE1, S.c(sem.sPE1))
            inc("vector", lambda e: e.tensor_tensor(
                SRr[:, :], SRr[:, :], P[6][:, 0:128], OP.min), sem.sDVE)
        sr_tgt = S.c(sem.sDVE)
        sqf = sq.rearrange("p a b -> p (a b)")
        troute = sqf[:, 0:128]
        tr16a = sqf[:, 128:144]
        tr16b = sqf[:, 144:160]
        for k, mv in ((0, m1v), (1, m2v)):
            wait("vector", sem.sDVE, sr_tgt)
            cstep("vector", lambda e, mv=mv: e.tensor_tensor(
                troute, mv, SRr[:, :], OP.mult))
            cstep("vector", lambda e, k=k: e.tensor_reduce(
                srt[:, :, k:k + 1],
                troute.rearrange("p (a b) -> p a b", b=8), axis=AX, op=OP.add))
            srk = srt[:, :, k:k + 1].rearrange("p a b -> p (a b)")
            si = sint2[:, :, k:k + 1].rearrange("p a b -> p (a b)")
            cstep("vector", lambda e, srk=srk, si=si: e.tensor_copy(
                out=si, in_=srk))
            cstep("vector", lambda e, si=si: e.tensor_scalar(
                tri_a[:, :], si, 127, None, OP.bitwise_and))
            cstep("vector", lambda e, si=si: e.tensor_scalar(
                tri_b[:, :], si, 7, None, OP.arith_shift_right))
            cstep("vector", lambda e: e.tensor_scalar(
                tri_b[:, :], tri_b[:, :], 0, None, OP.add))
            spv = sint[:, :, k:k + 1].rearrange("p a b -> p (a b)")
            cstep("vector", lambda e, spv=spv: e.tensor_scalar(
                spv, tri_a[:, :], 5, None, OP.arith_shift_left))
            cstep("vector", lambda e: e.tensor_scalar(
                tri_c[:, :], tri_a[:, :], 4, None, OP.arith_shift_left))
            cstep("vector", lambda e, spv=spv: e.tensor_tensor(
                spv, spv, tri_c[:, :], OP.add))
            cstep("vector", lambda e: e.tensor_scalar(
                tri_c[:, :], tri_a[:, :], 3, None, OP.arith_shift_left))
            cstep("vector", lambda e, spv=spv: e.tensor_tensor(
                spv, spv, tri_c[:, :], OP.add))
            inc("vector", lambda e, spv=spv: e.tensor_tensor(
                spv, spv, tri_b[:, :], OP.add), sem.sDVE)
        sint_tgt = S.c(sem.sDVE)

        wait("gpsimd", sem.sDVE, sint_tgt)
        for tt in range(TT):
            for k in range(2):
                idma("gpsimd", sem.sGB, glist_d[:, :],
                     tokid16[:, tt:tt + 1],
                     out_off=sint[:, tt, k:k + 1])
        sc_tgt = S.c(sem.sGB)
        wait("gpsimd", sem.sGB, sc_tgt)
        glt = dma("gpsimd", sem.sGB, glii[:, 0:56],
                  glist_d.rearrange("(p j) one -> p (j one)", p=128))

        tload_tgts = {}
        gst_tgts = {}
        hdone_tgts = {}
        eod_tgts = []

        def emit_gathers(eidx):
            wait("gpsimd", sem.sGB, glt)
            if eidx >= 2:
                wait("gpsimd", sem.sST, tload_tgts[eidx - 2])
            for i in range(NT):
                gpar = i % 2
                if i >= 2:
                    wait("gpsimd", sem.sST, S.c(sem.sST) - 16)
                idma("gpsimd", sem.sGB, xgstage[:, gpar, :],
                     xnt_d[0:128, :],
                     in_off=glii[:, eidx * NT + i:eidx * NT + i + 1])
                wait("sync", sem.sGB, S.c(sem.sGB))
                dma("sync", sem.sST,
                    xg_d[eidx % 2, i * 128:(i + 1) * 128, :],
                    xgstage[:, gpar, :])
            gst_tgts[eidx] = S.c(sem.sST)

        def emit_tloads(eidx):
            wait("scalar", sem.sST, gst_tgts[eidx])
            if eidx >= 1:
                wait("scalar", sem.sPEh, hdone_tgts[eidx - 1])
            for cc in range(CCn):
                dma("scalar", sem.sST, xg_fm[:, cc, :],
                    xg_d[eidx % 2, :, cc * 128:(cc + 1) * 128], transpose=True)
            tload_tgts[eidx] = S.c(sem.sST)

        if call == 0 and ("moe" in init_parts):
            _emit_relayout_expert(0)
        emit_gathers(0)
        emit_tloads(0)
        for eidx in range(E):
            wait("tensor", sem.sST, tload_tgts[eidx])
            if call == 0 and ("moe" in init_parts) and eidx + 1 < E:
                _emit_relayout_expert(eidx + 1)

            for hh in range(HHn):
                par = hh % 4
                pp = hh % 2
                glob = S.c(sem.sPEh)
                if glob >= 4:
                    wait("sync", sem.sPEh, glob - 3)
                if call == 0 and relay_tgt:
                    wait("sync", sem.sWS, max(relay_tgt[("w1", eidx, 7)],
                                              relay_tgt[("w3", eidx, 7)]))
                t1 = dma("sync", sem.sW1, w1t[:, par, :, :],
                         w1c[eidx, hh, :, :, :])
                t3 = dma("sync", sem.sW3, w3t[:, par, :, :],
                         w3c[eidx, hh, :, :, :])
                wait("tensor", sem.sW1, t1)
                if hh <= 1:
                    wait("tensor", sem.sACT, S.c(sem.sACT))
                if glob >= 2:
                    wait("tensor", sem.sACT, S.c(sem.sACT) - 2)
                    wait("tensor", sem.sDVE, S.c(sem.sDVE) - 2)
                for cc in range(CCn):
                    run("tensor", lambda e, cc=cc, par=par, pp=pp: e.matmul(
                        P[4 * pp][:, :], w1t[:, par, cc, :],
                        xg_fm[:, cc, 0:512],
                        start=(cc == 0), stop=(cc == CCn - 1)))
                for cc in range(CCn):
                    fn = (lambda e, cc=cc, par=par, pp=pp: e.matmul(
                        P[4 * pp + 1][:, 0:384], w1t[:, par, cc, :],
                        xg_fm[:, cc, 512:896],
                        start=(cc == 0), stop=(cc == CCn - 1)))
                    if cc == CCn - 1:
                        inc("tensor", fn, sem.sPE1)
                    else:
                        run("tensor", fn)
                h1_tgt = S.c(sem.sPE1)
                wait("tensor", sem.sW3, t3)
                for cc in range(CCn):
                    run("tensor", lambda e, cc=cc, par=par, pp=pp: e.matmul(
                        P[4 * pp + 2][:, :], w3t[:, par, cc, :],
                        xg_fm[:, cc, 0:512],
                        start=(cc == 0), stop=(cc == CCn - 1)))
                for cc in range(CCn):
                    fn = (lambda e, cc=cc, par=par, pp=pp: e.matmul(
                        P[4 * pp + 3][:, 0:384], w3t[:, par, cc, :],
                        xg_fm[:, cc, 512:896],
                        start=(cc == 0), stop=(cc == CCn - 1)))
                    if cc == CCn - 1:
                        inc("tensor", fn, sem.sPEh)
                    else:
                        run("tensor", fn)
                wait("scalar", sem.sPE1, h1_tgt)
                if glob >= 2:
                    wait("scalar", sem.sDVE, S.c(sem.sDVE) - 2)
                run("scalar", lambda e, pp=pp: e.activation(
                    out=sl2[:, pp, 0:512], in_=P[4 * pp][:, :], func=AF.Silu))
                inc("scalar", lambda e, pp=pp: e.activation(
                    out=sl2[:, pp, 512:896], in_=P[4 * pp + 1][:, 0:384],
                    func=AF.Silu), sem.sACT, 2)
                wait("vector", sem.sPEh, S.c(sem.sPEh))
                wait("vector", sem.sACT, S.c(sem.sACT))
                run("vector", lambda e, hh=hh, pp=pp: e.tensor_tensor(
                    hidg[:, hh, 0:512], sl2[:, pp, 0:512],
                    P[4 * pp + 2][:, :], OP.mult))
                inc("vector", lambda e, hh=hh, pp=pp: e.tensor_tensor(
                    hidg[:, hh, 512:896], sl2[:, pp, 512:896],
                    P[4 * pp + 3][:, 0:384], OP.mult), sem.sDVE, 2)
            hdone_tgts[eidx] = S.c(sem.sPEh)
            hid_done = S.c(sem.sDVE)

            if eidx + 1 < E:
                emit_gathers(eidx + 1)
                emit_tloads(eidx + 1)

            wait("tensor", sem.sDVE, hid_done)
            for w, (t0, nt) in enumerate(((0, 4), (4, 3))):
                wait("tensor", sem.sACT, S.c(sem.sACT))
                for hh in range(HHn):
                    par = hh % 2
                    globe = S.c(sem.sPEeo)
                    if globe >= 2:
                        wait("sync", sem.sPEeo, globe - 1)
                    if call == 0 and relay_tgt:
                        wait("sync", sem.sWS, relay_tgt[("w2", eidx, hh)])
                    t2 = dma("sync", sem.sW2, w2t[:, par, :],
                             w2c[eidx, hh, :, :])
                    wait("tensor", sem.sW2, t2)
                    for j in range(2 * nt):
                        tl, ch = j // 2, j % 2
                        fn = (lambda e, hh=hh, par=par, tl=tl, ch=ch, j=j,
                              t0=t0: e.matmul(
                                  P[j][:, :],
                                  hidg[:, hh, (t0 + tl) * 128:
                                       (t0 + tl + 1) * 128],
                                  w2t[:, par, ch * 512:(ch + 1) * 512],
                                  start=(hh == 0), stop=(hh == HHn - 1)))
                        if j == 2 * nt - 1:
                            inc("tensor", fn, sem.sPEeo)
                        else:
                            run("tensor", fn)
                eo_tgt = S.c(sem.sPEeo)
                wait("scalar", sem.sPEeo, eo_tgt)
                for tl in range(nt):
                    spar = len(eod_tgts) % 2
                    if len(eod_tgts) >= 2:
                        wait("scalar", sem.sEOD, eod_tgts[-2])
                    run("scalar", lambda e, tl=tl, spar=spar: e.copy(
                        out=eostage[:, spar, 0:512], in_=P[2 * tl][:, :]))
                    inc("scalar", lambda e, tl=tl, spar=spar: e.copy(
                        out=eostage[:, spar, 512:1024], in_=P[2 * tl + 1][:, :]),
                        sem.sACT)
                    wait("gpsimd", sem.sACT, S.c(sem.sACT))
                    base_row = eidx * CAP + (t0 + tl) * 128
                    eod_tgts.append(dma(
                        "gpsimd", sem.sEOD, eo_d[base_row:base_row + 128, :],
                        eostage[:, spar, :]))
        eo_all = S.c(sem.sEOD)

        if dbg_what == "sr" and call == 0:
            wait("gpsimd", sem.sDVE, sint_tgt)
            dma("gpsimd", sem.sOUT,
                dbg_d[0:256, :].rearrange("(p a) b -> p (a b)", p=128),
                SRr[:, :])
        if dbg_what == "w1c" and call == 0:
            wait("gpsimd", sem.sWS, S.c(sem.sWS))
            dma("gpsimd", sem.sOUT,
                dbg_d.rearrange("(p a) b -> p (a b)", p=128),
                w1c[0, 0, :, :, :].rearrange("p cc w -> p (cc w)"))
        if dbg_what == "glist" and call == 0:
            wait("gpsimd", sem.sGB, S.c(sem.sGB))
            dma("gpsimd", sem.sOUT, dbg_d[0:128, 0:56], glii[:, 0:56])
        wait("gpsimd", sem.sEOD, eo_all)
        wait("gpsimd", sem.sDVE, sint_tgt)
        for tt in range(TT):
            if tt >= 1:
                wait("gpsimd", sem.sDVE2, S.c(sem.sDVE2))
            for k in range(2):
                idma("gpsimd", sem.sGB, xgstage[:, k, :], eo_d[0:128, :],
                     in_off=sint2[:, tt, k:k + 1])
            gb_tgt = S.c(sem.sGB)
            wait("vector", sem.sGB, gb_tgt)
            run("vector", lambda e, tt=tt: e.scalar_tensor_tensor(
                xres[:, tt, :], xgstage[:, 0, :], wts[:, tt, 0:1],
                xres[:, tt, :], OP.mult, OP.add))
            inc("vector", lambda e, tt=tt: e.scalar_tensor_tensor(
                xres[:, tt, :], xgstage[:, 1, :], wts[:, tt, 1:2],
                xres[:, tt, :], OP.mult, OP.add), sem.sDVE2)

    def emit_attn():
        _emit_gb_bcast()
        if dbg_what == "attnc":
            wait("gpsimd", sem.sDVE, S.c(sem.sDVE))
            dma("gpsimd", sem.sOUT, dbg_d[0:128, 0:4], biascols[:, 0:4])
            dma("gpsimd", sem.sOUT, dbg_d[128:256, 0:64],
                g_bc[:, 0:64].bitcast(F32))
            dma("gpsimd", sem.sOUT, dbg_d[256:384, 0:64], b_bc[:, 0:64])
        cc_tgt = emit_ln(is_attn=True)
        if dbg_what == "xn2":
            wait("gpsimd", sem.sXN, cc_tgt)
            dma("gpsimd", sem.sOUT, dbg_d, cc_in[:, 0:64])
        for tt in range(TT):
            dma("gpsimd", sem.sSP, resid_d[tt * 128:(tt + 1) * 128, :],
                xres[:, tt, :])
        sp_tgt = S.c(sem.sSP)
        wait("gpsimd", sem.sXN, cc_tgt)
        inc("gpsimd", lambda e: e.collective_compute(
            "AllGather", OP.bypass,
            replica_groups=[[0, 1], [2, 3], [4, 5], [6, 7]],
            ins=[cc_in.opt()], outs=[cc_out.opt()]), sem.sCC)
        cc_done = S.c(sem.sCC)
        wait("gpsimd", sem.sSP, sp_tgt)

        barrier()

        wait("sync", sem.sWS, wc_done)
        wait("sync", sem.sCC, cc_done)
        wait("sync", sem.sSP, sp_tgt)
        for cc in range(CCn):
            dma("sync", sem.sST, wph[:, cc, :], wkt_d[cc * 128:(cc + 1) * 128, :])
        wkt_tgt = S.c(sem.sST)


        def emit_proj_rms(n_units, proj_dst, rhs_of, gam, sq_tgts, sqrt_tgts,
                          recip_tgts, mult_tgts):
            for step in range(n_units + 1):
                kc = step
                pc = step - 1
                if pc >= 0:
                    wait("tensor", sem.sACT, sq_tgts[pc])
                    inc("tensor", lambda e, pc=pc: e.matmul(
                        P[2][0:2, :], ones2h[:, :], ksq[:, pc % 2, :],
                        start=True, stop=True), sem.sPE1)
                    ss_tgt = S.c(sem.sPE1)
                if kc < n_units:
                    if kc >= 2:
                        wait("tensor", sem.sACT, sq_tgts[kc - 2])
                    for cc in range(CCn):
                        fn = (lambda e, cc=cc, kc=kc: e.matmul(
                            P[kc % 2][:, :], wph[:, cc, kc * 128:(kc + 1) * 128],
                            rhs_of(kc, cc),
                            start=(cc == 0), stop=(cc == CCn - 1)))
                        if cc == CCn - 1:
                            inc("tensor", fn, sem.sPE1)
                        else:
                            run("tensor", fn)
                    proj_tgt = S.c(sem.sPE1)
                if pc >= 0:
                    wait("scalar", sem.sPE1, ss_tgt)
                    inc("scalar", lambda e, pc=pc: e.activation(
                        out=rkt[0:2, pc % 2, :], in_=P[2][0:2, :], func=AF.Sqrt,
                        scale=1.0 / DH), sem.sACT)
                    sqrt_tgts.append(S.c(sem.sACT))
                if kc < n_units:
                    wait("scalar", sem.sPE1, proj_tgt)
                    run("scalar", lambda e, kc=kc: e.copy(
                        out=proj_dst(kc), in_=P[kc % 2][:, :]))
                    inc("scalar", lambda e, kc=kc: e.activation(
                        out=ksq[:, kc % 2, :], in_=P[kc % 2][:, :],
                        func=AF.Square), sem.sACT)
                    sq_tgts.append(S.c(sem.sACT))
                if pc >= 0:
                    wait("vector", sem.sACT, sqrt_tgts[pc])
                    cstep("vector", lambda e, pc=pc: e.tensor_scalar_max(
                        rkt[0:2, pc % 2, :], rkt[0:2, pc % 2, :], EPS_RMS))
                    inc("vector", lambda e, pc=pc: e.reciprocal(
                        rkt[0:2, pc % 2, :], rkt[0:2, pc % 2, :]), sem.sDVE)
                    recip_tgts.append(S.c(sem.sDVE))
                    wait("tensor", sem.sDVE, recip_tgts[pc])
                    if pc >= 1:
                        wait("tensor", sem.sDVE, mult_tgts[pc - 1])
                    inc("tensor", lambda e, pc=pc: e.matmul(
                        P[3][:, :], gam[:, :], rkt[0:2, pc % 2, :],
                        start=True, stop=True), sem.sPE1)
                    bc_tgt = S.c(sem.sPE1)
                    wait("vector", sem.sPE1, bc_tgt)
                    inc("vector", lambda e, pc=pc: e.tensor_tensor(
                        proj_dst(pc), proj_dst(pc), P[3][:, :], OP.mult),
                        sem.sDVE)
                    mult_tgts.append(S.c(sem.sDVE))

        k_sg_pe = []
        for sg in range(NSG):
            par = sg % 2
            if sg >= 2:
                wait("sync", sem.sPE1, k_sg_pe[sg - 2])
            for cc in range(CCn):
                dma("sync", sem.sST, stage[:, par, cc, :],
                    cc_out[sg * 512:(sg + 1) * 512, cc * 128:(cc + 1) * 128],
                    transpose=True)
            st_tgt = S.c(sem.sST)
            wait("tensor", sem.sST, st_tgt)
            if sg == 0:
                wait("tensor", sem.sST, wkt_tgt)
            wait("tensor", sem.sACT, S.c(sem.sACT))
            wait("tensor", sem.sDVE, S.c(sem.sDVE))
            emit_proj_rms(
                CCn,
                (lambda kc, sg=sg: kfm[:, kc, sg * 512:(sg + 1) * 512]),
                (lambda kc, cc, par=par: stage[:, par, cc, :]),
                kgam2, [], [], [], [])
            k_sg_pe.append(S.c(sem.sPE1))

        wait("sync", sem.sPE1, S.c(sem.sPE1))
        for cc in range(CCn):
            dma("sync", sem.sST, wph[:, cc, :], wvt_d[cc * 128:(cc + 1) * 128, :])
        wvt_tgt = S.c(sem.sST)
        for kt in range(NKT - 1):
            run("gpsimd", lambda e, kt=kt: e.memset(vt_h[:, kt, :, 64:65], 1.0))
        inc("gpsimd", lambda e: e.memset(vt_h[:, NKT - 1, :, 64:65], 1.0),
            sem.sGB)
        vones = S.c(sem.sGB)
        v_sg_pe = []
        for sg in range(NSG):
            par = sg % 2
            if sg >= 2:
                wait("sync", sem.sPE1, v_sg_pe[sg - 2])
            for cc in range(CCn):
                dma("sync", sem.sST, stage[:, par, cc, :],
                    cc_out[sg * 512:(sg + 1) * 512, cc * 128:(cc + 1) * 128],
                    transpose=True)
            st_tgt = S.c(sem.sST)
            wait("tensor", sem.sST, st_tgt)
            if sg == 0:
                wait("tensor", sem.sST, wvt_tgt)
                wait("tensor", sem.sGB, vones)
            for i in range(4):
                kt = sg * 4 + i
                for mh in range(2):
                    u = 2 * i + mh
                    pV = P[4 + u % 2]
                    if sg or i or mh:
                        wait("tensor", sem.sDVE2, S.c(sem.sDVE2) - 8)
                    for cc in range(CCn):
                        fn = (lambda e, cc=cc, i=i, mh=mh, par=par, pV=pV:
                              e.matmul(
                                  pV[:, :],
                                  stage[:, par, cc, i * 128:(i + 1) * 128],
                                  wph[:, cc, mh * 512:(mh + 1) * 512],
                                  start=(cc == 0), stop=(cc == CCn - 1)))
                        if cc == CCn - 1:
                            inc("tensor", fn, sem.sPE1)
                        else:
                            run("tensor", fn)
                    wait("vector", sem.sPE1, S.c(sem.sPE1))
                    for hp in range(8):
                        head = mh * 8 + hp
                        inc("vector", lambda e, kt=kt, head=head, hp=hp, pV=pV:
                            e.tensor_copy(
                                out=vt_h[:, kt, head, 0:64],
                                in_=pV[:, hp * 64:(hp + 1) * 64]), sem.sDVE2)
            v_sg_pe.append(S.c(sem.sPE1))

        for qb in range(NQB):
            wait("sync", sem.sDVE2, S.c(sem.sDVE2))
            wait("sync", sem.sPE1, S.c(sem.sPE1))
            wait("tensor", sem.sDVE2, S.c(sem.sDVE2))
            for cc in range(CCn):
                dma("sync", sem.sST, wph[:, cc, :],
                    wqt_d[cc * 128:(cc + 1) * 128, :])
            for cc in range(CCn):
                dma("sync", sem.sST, stage[:, 0, cc, :],
                    cc_in[qb * 512:(qb + 1) * 512, cc * 128:(cc + 1) * 128],
                    transpose=True)
            qst_tgt = S.c(sem.sST)
            wait("tensor", sem.sST, qst_tgt)
            wait("tensor", sem.sACT, S.c(sem.sACT))
            wait("tensor", sem.sDVE, S.c(sem.sDVE))
            emit_proj_rms(
                CCn,
                (lambda qc: qfm[:, qc, :]),
                (lambda qc, cc: stage[:, 0, cc, :]),
                qgam2, [], [], [], [])
            q_done_dve = S.c(sem.sDVE)

            for pr in range(CCn):
                wait("tensor", sem.sDVE, S.c(sem.sDVE))
                hA, hB = 2 * pr, 2 * pr + 1
                for kt in range(NKT):
                    pn = kt % 2
                    pSA, pSB = P[0 + pn], P[2 + pn]
                    if S.c(sem.sACT) >= 1:
                        wait("tensor", sem.sACT, S.c(sem.sACT) - 1)
                    run("tensor", lambda e, pr=pr, kt=kt, pSA=pSA: e.matmul(
                        pSA[:, :], kfm[0:64, pr, kt * 128:(kt + 1) * 128],
                        qfm[0:64, pr, :], tile_position=(0, 0),
                        start=True, stop=True))
                    inc("tensor", lambda e, pr=pr, kt=kt, pSB=pSB: e.matmul(
                        pSB[:, :], kfm[64:128, pr, kt * 128:(kt + 1) * 128],
                        qfm[64:128, pr, :], tile_position=(64, 0),
                        start=True, stop=True), sem.sPE1)
                    wait("scalar", sem.sPE1, S.c(sem.sPE1))
                    run("scalar", lambda e, pn=pn, pSA=pSA: e.activation(
                        out=Pt[:, 2 * pn, :], in_=pSA[:, :], func=AF.Exp,
                        bias=biascols[:, 2:3]))
                    inc("scalar", lambda e, pn=pn, pSB=pSB: e.activation(
                        out=Pt[:, 2 * pn + 1, :], in_=pSB[:, :], func=AF.Exp,
                        bias=biascols[:, 2:3]), sem.sACT)
                    if kt >= 1:
                        po = (kt - 1) % 2
                        wait("tensor", sem.sACT, S.c(sem.sACT) - 1)
                        run("tensor", lambda e, kt=kt, po=po, hA=hA: e.matmul(
                            P[4][0:65, :], vt_h[:, kt - 1, hA, 0:65],
                            Pt[:, 2 * po, :],
                            start=(kt == 1), stop=False))
                        inc("tensor", lambda e, kt=kt, po=po, hB=hB: e.matmul(
                            P[5][0:65, :], vt_h[:, kt - 1, hB, 0:65],
                            Pt[:, 2 * po + 1, :],
                            start=(kt == 1), stop=False), sem.sPE1)
                po = (NKT - 1) % 2
                wait("tensor", sem.sACT, S.c(sem.sACT))
                run("tensor", lambda e, po=po, hA=hA: e.matmul(
                    P[4][0:65, :], vt_h[:, NKT - 1, hA, 0:65],
                    Pt[:, 2 * po, :], start=False, stop=True))
                inc("tensor", lambda e, po=po, hB=hB: e.matmul(
                    P[5][0:65, :], vt_h[:, NKT - 1, hB, 0:65],
                    Pt[:, 2 * po + 1, :], start=False, stop=True), sem.sPE1)
                pv_done = S.c(sem.sPE1)
                wait("vector", sem.sPE1, pv_done)
                for hb in range(2):
                    pO = P[4 + hb]
                    cstep("vector", lambda e, hb=hb, pO=pO: e.tensor_copy(
                        out=onrm[0:65, hb, :], in_=pO[0:65, :]))
                    inc("vector", lambda e, hb=hb: e.reciprocal(
                        rden[0:1, :], onrm[64:65, hb, :]), sem.sDVE)
                    wait("tensor", sem.sDVE, S.c(sem.sDVE))
                    inc("tensor", lambda e: e.matmul(
                        P[6][0:64, :], onesf[0:1, 0:64], rden[0:1, :],
                        start=True, stop=True), sem.sPE1)
                    wait("vector", sem.sPE1, S.c(sem.sPE1))
                    base = 64 * hb
                    inc("vector", lambda e, hb=hb, pr=pr, base=base:
                        e.tensor_tensor(
                            out_norm[base:base + 64, pr, :],
                            onrm[0:64, hb, :], P[6][0:64, :], OP.mult),
                        sem.sDVE)
            onorm_done = S.c(sem.sDVE)

            wait("sync", sem.sDVE, onorm_done)
            for cc in range(CCn):
                dma("sync", sem.sST, wph[:, cc, :],
                    wot_d[cc * 128:(cc + 1) * 128, :])
            wo_tgt = S.c(sem.sST)
            wait("tensor", sem.sST, wo_tgt)
            wait("tensor", sem.sDVE, onorm_done)
            prev_store = None
            for tt4 in range(4):
                tta = qb * 4 + tt4
                if prev_store is not None:
                    wait("sync", sem.sXN, prev_store)
                rs_t = dma("sync", sem.sXN, res_stage[:, :],
                           resid_d[tta * 128:(tta + 1) * 128, :])
                for mh in range(2):
                    pW = P[6 + (2 * tt4 + mh) % 2]
                    if tt4 or mh:
                        wait("tensor", sem.sDVE2, S.c(sem.sDVE2))
                    for pr in range(CCn):
                        run("tensor", lambda e, pr=pr, tt4=tt4, mh=mh, pW=pW:
                            e.matmul(
                                pW[:, :],
                                out_norm[:, pr, tt4 * 128:(tt4 + 1) * 128],
                                wph[:, pr, mh * 512:(mh + 1) * 512],
                                start=(pr == 0), stop=False))
                    inc("tensor", lambda e, mh=mh, pW=pW: e.matmul(
                        pW[:, :], ones1b[0:1, :],
                        borow[0:1, mh * 512:(mh + 1) * 512],
                        start=False, stop=True), sem.sPE1)
                    wait("vector", sem.sPE1, S.c(sem.sPE1))
                    wait("vector", sem.sXN, rs_t)
                    inc("vector", lambda e, mh=mh, pW=pW: e.tensor_add(
                        res_stage[:, mh * 512:(mh + 1) * 512], pW[:, :],
                        res_stage[:, mh * 512:(mh + 1) * 512]), sem.sDVE2)
                wait("sync", sem.sDVE2, S.c(sem.sDVE2))
                prev_store = dma("sync", sem.sXN,
                                 resid_d[tta * 128:(tta + 1) * 128, :],
                                 res_stage[:, :])

        barrier()
        wait("sync", sem.sXN, S.c(sem.sXN))
        for tt in range(TT):
            dma("sync", sem.sXN, xres[:, tt, :],
                resid_d[tt * 128:(tt + 1) * 128, :])
        wait("tensor", sem.sXN, S.c(sem.sXN))
        wait("vector", sem.sXN, S.c(sem.sXN))
        wait("scalar", sem.sXN, S.c(sem.sXN))
        barrier()

    use_sparse = os.environ.get("K_DENSE", "") != "1"
    ffn_fn = emit_ffn_sparse if use_sparse else emit_ffn
    if "ffn1" in stages:
        ffn_fn(0)
        barrier()
    if "attn" in stages:
        emit_attn()
    if "ffn2" in stages:
        ffn_fn(1)
        barrier()

    wait("sync", sem.sDVE2, S.c(sem.sDVE2))
    wait("sync", sem.sDVE, S.c(sem.sDVE))
    for tt in range(TT):
        dma("sync", sem.sOUT, out_d[tt * 128:(tt + 1) * 128, :], xres[:, tt, :])
    wait("sync", sem.sOUT, S.c(sem.sOUT))

    with nc.Block() as block:
        @block.sync
        def _(eng):
            for fn in S.q["sync"]:
                fn(eng)

        @block.scalar
        def _(eng):
            for fn in S.q["scalar"]:
                fn(eng)

        @block.vector
        def _(eng):
            for fn in S.q["vector"]:
                fn(eng)

        @block.tensor
        def _(eng):
            for fn in S.q["tensor"]:
                fn(eng)

        @block.gpsimd
        def _(eng):
            for fn in S.q["gpsimd"]:
                fn(eng)

    stack.close()
    return nc


_NC_CACHE = {}


def kernel(L0_attn_x, L0_attn_Wq, L0_attn_Wk, L0_attn_Wv, L0_attn_Wo,
           L0_attn_bo, L0_attn_ln_g, L0_attn_ln_b, L0_attn_q_gamma,
           L0_attn_k_gamma, L0_ffn_Wg, L0_ffn_W1, L0_ffn_W3, L0_ffn_W2,
           _stages=("ffn1", "attn", "ffn2")):
    x = np.ascontiguousarray(np.asarray(L0_attn_x, dtype=np.float32))
    B, N, Cm = x.shape
    assert (B, N, Cm) == (4, 4096, 1024)
    weights = {
        "wq": np.ascontiguousarray(L0_attn_Wq, dtype=np.float32),
        "wk": np.ascontiguousarray(L0_attn_Wk, dtype=np.float32),
        "wv": np.ascontiguousarray(L0_attn_Wv, dtype=np.float32),
        "wo": np.ascontiguousarray(L0_attn_Wo, dtype=np.float32),
        "bo": np.ascontiguousarray(np.asarray(L0_attn_bo, np.float32).reshape(1, C)),
        "lng": np.ascontiguousarray(np.asarray(L0_attn_ln_g, np.float32).reshape(1, C)),
        "lnb": np.ascontiguousarray(np.asarray(L0_attn_ln_b, np.float32).reshape(1, C)),
        "qg": np.ascontiguousarray(np.asarray(L0_attn_q_gamma, np.float32).reshape(1, DH)),
        "kg": np.ascontiguousarray(np.asarray(L0_attn_k_gamma, np.float32).reshape(1, DH)),
        "wg": np.ascontiguousarray(L0_ffn_Wg, dtype=np.float32),
        "w1": np.ascontiguousarray(L0_ffn_W1, dtype=np.float32),
        "w3": np.ascontiguousarray(L0_ffn_W3, dtype=np.float32),
        "w2": np.ascontiguousarray(L0_ffn_W2, dtype=np.float32),
    }
    key = tuple(_stages)
    if key not in _NC_CACHE:
        _NC_CACHE[key] = build_program(_stages)
    nc = _NC_CACHE[key]
    in_maps = []
    for core in range(8):
        bi, half = core // 2, core % 2
        shard = np.ascontiguousarray(x[bi, half * T:(half + 1) * T, :])
        m = {"x": shard}
        m.update(weights)
        in_maps.append(m)
    res = run_bass_kernel_spmd(nc, in_maps, core_ids=list(range(8)))
    out = np.empty((B, N, Cm), dtype=np.float32)
    global DBG
    DBG = []
    for core in range(8):
        bi, half = core // 2, core % 2
        out[bi, half * T:(half + 1) * T, :] = res.results[core]["out"]
        if "dbg" in res.results[core]:
            DBG.append(np.asarray(res.results[core]["dbg"]))
    return out


# version 12
# speedup vs baseline: 1.1270x; 1.0442x over previous
import numpy as np
import concourse.bass as bass
import concourse.mybir as mybir
from concourse.bass_utils import run_bass_kernel_spmd

F32 = mybir.dt.float32
BF16 = mybir.dt.bfloat16
U8 = mybir.dt.uint8
AF = mybir.ActivationFunctionType
OP = mybir.AluOpType
AX = mybir.AxisListType.X

T = 2048
C = 1024
H = 4096
E = 8
TT = 16
CCn = 8
HHn = 32
NQB = 4
NSG = 8
NKT = 32
DH = 64
EPS_LN = 1e-5
EPS_RMS = 1e-8

ENGINES = ("sync", "scalar", "vector", "tensor", "gpsimd")


class Sched:
    def __init__(self):
        self.q = {e: [] for e in ENGINES}
        self.cnt = {}

    def emit(self, eng, fn):
        self.q[eng].append(fn)

    def c(self, sem):
        return self.cnt.get(sem.name, 0)

    def bump(self, sem, n=1):
        self.cnt[sem.name] = self.c(sem) + n
        return self.cnt[sem.name]


def build_program(stages=("ffn1", "attn", "ffn2")):
    import os
    init_parts = set(os.environ.get("K_INIT", "consts,gb,wt,moe").split(","))
    nc = bass.Bass("TRN2", target_bir_lowering=False, debug=False)

    din = {}
    din["x"] = nc.dram_tensor("x", [T, C], F32, kind="ExternalInput").ap()
    for nm in ("wq", "wk", "wv", "wo"):
        din[nm] = nc.dram_tensor(nm, [C, C], F32, kind="ExternalInput").ap()
    din["bo"] = nc.dram_tensor("bo", [1, C], F32, kind="ExternalInput").ap()
    din["lng"] = nc.dram_tensor("lng", [1, C], F32, kind="ExternalInput").ap()
    din["lnb"] = nc.dram_tensor("lnb", [1, C], F32, kind="ExternalInput").ap()
    din["qg"] = nc.dram_tensor("qg", [1, DH], F32, kind="ExternalInput").ap()
    din["kg"] = nc.dram_tensor("kg", [1, DH], F32, kind="ExternalInput").ap()
    din["wg"] = nc.dram_tensor("wg", [C, E], F32, kind="ExternalInput").ap()
    din["w1"] = nc.dram_tensor("w1", [E, C, H], F32, kind="ExternalInput").ap()
    din["w3"] = nc.dram_tensor("w3", [E, C, H], F32, kind="ExternalInput").ap()
    din["w2"] = nc.dram_tensor("w2", [E, H, C], F32, kind="ExternalInput").ap()
    out_d = nc.dram_tensor("out", [T, C], F32, kind="ExternalOutput").ap()
    import os as _osd
    dbg_what = _osd.environ.get("K_DBG", "")
    dbg_d = nc.dram_tensor("dbg", [2048, 64], F32, kind="ExternalOutput").ap()

    w1c = nc.dram_tensor("w1c", [E, HHn, 128, CCn, 128], BF16).ap()
    w3c = nc.dram_tensor("w3c", [E, HHn, 128, CCn, 128], BF16).ap()
    w2c = nc.dram_tensor("w2c", [E, HHn, 128, C], BF16).ap()
    wqt_d = nc.dram_tensor("wqt_d", [C, C], BF16).ap()
    wkt_d = nc.dram_tensor("wkt_d", [C, C], BF16).ap()
    wvt_d = nc.dram_tensor("wvt_d", [C, C], BF16).ap()
    wot_d = nc.dram_tensor("wot_d", [C, C], BF16).ap()
    xnt_d = nc.dram_tensor("xnt_d", [T, C], BF16).ap()
    cc_in = nc.dram_tensor("cc_in", [T, C], BF16).ap()
    cc_out = nc.dram_tensor("cc_out", [2 * T, C], BF16).ap()
    resid_d = nc.dram_tensor("resid_d", [T, C], F32).ap()
    CAP = 896
    NT = CAP // 128
    NSL = E * CAP
    glist_d = nc.dram_tensor("glist_d", [NSL, 1], mybir.dt.int32).ap()
    xg_d = nc.dram_tensor("xg_d", [2, CAP, C], BF16).ap()
    eo_d = nc.dram_tensor("eo_d", [NSL, C], BF16).ap()

    arenaA = nc.alloc_sbuf_tensor("arenaA", [128, 65536], U8)
    arenaB = nc.alloc_sbuf_tensor("arenaB", [128, 67584], U8)
    arenaC = nc.alloc_sbuf_tensor("arenaC", [128, 68096], U8)

    def carve(arena, off, nbytes, dt, shape):
        ap = arena.ap()[:, off:off + nbytes].bitcast(dt)
        if len(shape) == 1:
            assert ap.shape[1] == shape[0], (ap.shape, shape)
            return ap
        names = " ".join(f"d{i}" for i in range(len(shape)))
        return ap.rearrange(f"p ({names}) -> p {names}",
                            **{f"d{i}": s for i, s in enumerate(shape)})

    xres = carve(arenaA, 0, 65536, F32, (TT, C))
    kfm = carve(arenaA, 0, 65536, BF16, (CCn, 4096))

    xnfm = carve(arenaB, 0, 32768, BF16, (CCn, T))
    hid = carve(arenaB, 32768, 32768, BF16, (HHn, 512))
    vt = carve(arenaB, 0, 66560, BF16, (NKT, 1040))
    vt_h = vt.rearrange("p t (h w) -> p t h w", w=65)

    w1t = carve(arenaC, 51200, 8192, BF16, (4, CCn, 128))
    w3t = carve(arenaC, 59392, 8192, BF16, (4, CCn, 128))
    w2t = carve(arenaC, 8192, 4096, BF16, (2, C))
    sl = carve(arenaC, 12288, 2048, BF16, (2, 512))
    sq = carve(arenaC, 14336, 8192, F32, (2, C))
    xnt_st = carve(arenaC, 22528, 4096, BF16, (2, C))
    wstripe = carve(arenaC, 26624, 8192, BF16, (2, 2048))
    g_bc = carve(arenaC, 43008, 4096, F32, (C,))
    b_bc = carve(arenaC, 47104, 4096, F32, (C,))
    stage = carve(arenaC, 0, 16384, BF16, (2, CCn, 512))
    wph = carve(arenaC, 16384, 16384, BF16, (CCn, C))
    qfm = carve(arenaC, 32768, 8192, BF16, (CCn, 512))
    Pt = carve(arenaC, 40960, 4096, BF16, (4, 512))
    out_norm = carve(arenaC, 45056, 8192, BF16, (CCn, 512))
    onrm = carve(arenaC, 53248, 4096, F32, (2, 512))
    ksq = carve(arenaC, 57344, 2048, BF16, (2, 512))
    rkt = carve(arenaC, 59392, 4096, F32, (2, 512))
    res_stage = carve(arenaC, 63488, 4096, F32, (C,))
    mxbuf = carve(arenaC, 67584, 256, F32, (2, 32))
    rden = carve(arenaC, 40960, 2048, F32, (512,))[0:1, :]
    grow = carve(arenaC, 14336, 4096, F32, (C,))[0:1, :]
    brow = carve(arenaC, 18432, 4096, F32, (C,))[0:1, :]
    biascols = carve(arenaC, 67840, 16, F32, (4,))
    I32 = mybir.dt.int32
    xgstage = carve(arenaC, 0, 4096, BF16, (2, C))
    glii = carve(arenaC, 4096, 256, I32, (64,))
    srt = carve(arenaC, 4352, 128, F32, (16, 2))
    sint = carve(arenaC, 4480, 128, I32, (16, 2))
    sint2 = carve(arenaC, 4608, 128, I32, (16, 2))
    tokid16 = None
    SRr = carve(arenaC, 4800, 512, F32, (128,))
    msumb = carve(arenaC, 5312, 256, BF16, (128,))
    rsc_f = carve(arenaC, 5568, 64, F32, (16,))
    rrow_f = carve(arenaC, 5632, 512, F32, (128,))[0:1, :]
    ecapcol = None
    msk1b = carve(arenaC, 6208, 256, BF16, (16, E))
    msk2b = carve(arenaC, 6464, 256, BF16, (16, E))
    wts = carve(arenaC, 6720, 128, F32, (16, 2))
    usub = None
    u16f = None
    ubAi = carve(arenaC, 26624, 512, I32, (128,))
    ubBi = carve(arenaC, 27136, 512, I32, (128,))
    ubCi = carve(arenaC, 27648, 512, I32, (128,))
    ubDi = carve(arenaC, 28160, 512, I32, (128,))
    tri_a = carve(arenaC, 14976, 64, I32, (16,))
    tri_b = carve(arenaC, 15040, 64, I32, (16,))
    hidg = carve(arenaB, 0, 57344, BF16, (HHn, CAP))
    xg_fm = carve(arenaC, 34816, 14336, BF16, (CCn, CAP))
    sl2 = carve(arenaC, 18432, 3584, BF16, (2, CAP))
    eostage = carve(arenaC, 14336, 4096, BF16, (2, C))
    tri_c = carve(arenaC, 15104, 64, I32, (16,))

    stats = nc.alloc_sbuf_tensor("stats", [128, 16, 8], F32).ap()
    cmat = nc.alloc_sbuf_tensor("cmat", [128, 16, E], F32).ap()
    wgs = nc.alloc_sbuf_tensor("wgs", [128, CCn, E], BF16).ap()
    gsc = nc.alloc_sbuf_tensor("gsc", [128, 3, E], F32).ap()
    gtmp = nc.alloc_sbuf_tensor("gtmp", [128, 8], F32).ap()
    ident_f = nc.alloc_sbuf_tensor("ident_f", [128, 128], F32).ap()
    ident_b = nc.alloc_sbuf_tensor("ident_b", [128, 128], BF16).ap()
    onesf = nc.alloc_sbuf_tensor("onesf", [1, 128], F32).ap()
    ones1b = nc.alloc_sbuf_tensor("ones1b", [1, 128], BF16).ap()
    ones2h = nc.alloc_sbuf_tensor("ones2h", [128, 2], BF16).ap()
    o2tmp = nc.alloc_sbuf_tensor("o2tmp", [128, 2], F32).ap()
    qgam2 = nc.alloc_sbuf_tensor("qgam2", [2, 128], F32).ap()
    kgam2 = nc.alloc_sbuf_tensor("kgam2", [2, 128], F32).ap()
    negmT = nc.alloc_sbuf_tensor("negmT", [1, 4], F32).ap()
    borow = nc.alloc_sbuf_tensor("borow", [1, C], BF16).ap()
    ceps = nc.alloc_sbuf_tensor("ceps", [128, 1], F32).ap()
    onescol_b = nc.alloc_sbuf_tensor("onescol_b", [128, 1], BF16).ap()
    usub = nc.alloc_sbuf_tensor("usub", [128, 128], BF16).ap()
    u16f = nc.alloc_sbuf_tensor("u16f", [128, 128], F32).ap()
    ecapcol = nc.alloc_sbuf_tensor("ecapcol", [128, 1], F32).ap()
    tokid16 = nc.alloc_sbuf_tensor("tokid16", [128, 16], mybir.dt.int32).ap()
    ecrow = nc.alloc_sbuf_tensor("ecrow", [1, 128], F32).ap()
    wtmp = nc.alloc_sbuf_tensor("wtmp", [128, 2, 128], BF16).ap()
    wsto = nc.alloc_sbuf_tensor("wsto", [128, 2, 128], BF16).ap()
    assert nc.sbuf_bytes_remaining >= 0, nc.sbuf_bytes_remaining

    P = [nc.alloc_psum_tensor(f"p{i}", [128, 512], F32).ap() for i in range(8)]
    p_bf = P[7].bitcast(BF16)

    S = Sched()
    import contextlib
    stack = contextlib.ExitStack()
    sems = {}
    for nm in ("sIN", "sW1", "sW3", "sW2", "sST", "sXN", "sSP", "sCC", "sWC",
               "sWS", "sOUT", "sPE1", "sPEh", "sPEeo", "sPEx", "sACT", "sDVE",
               "sDVE2", "sBAR", "sGB", "sVC", "sSC", "sGC", "sEOD"):
        sems[nm] = stack.enter_context(nc.semaphore(nm))
    sem = type("X", (), sems)

    def emit(eng, fn):
        S.emit(eng, fn)

    def wait(eng, s, val):
        if val > 0:
            emit(eng, lambda e, s=s, v=val: e.wait_ge(s, v))

    def dma(eng, s, out, in_, **kw):
        tgt = S.bump(s, 16)
        emit(eng, lambda e, o=out, i=in_, s=s, kw=kw:
             e.dma_start(out=o, in_=i, **kw).then_inc(s, 16))
        return tgt

    from concourse.bass_core_rust import IndirectOffsetOnAxis

    def idma(eng, s, out, in_, out_off=None, in_off=None):
        tgt = S.bump(s, 16)
        emit(eng, lambda e, o=out, i=in_, s=s, oo=out_off, io=in_off:
             e.indirect_dma_start(
                 o, IndirectOffsetOnAxis(oo, 0) if oo is not None else None,
                 i, IndirectOffsetOnAxis(io, 0) if io is not None else None
             ).then_inc(s, 16))
        return tgt

    def inc(eng, instr_fn, s, n=1):
        tgt = S.bump(s, n)
        emit(eng, lambda e, f=instr_fn, s=s, n=n: f(e).then_inc(s, n))
        return tgt

    def run(eng, instr_fn):
        emit(eng, lambda e, f=instr_fn: f(e))

    chain_sems = {}

    def cstep(eng, instr_fn):
        sc = chain_sems[eng]
        tgt = inc(eng, instr_fn, sc)
        wait(eng, sc, tgt)

    chain_sems.update({"vector": sem.sVC, "scalar": sem.sSC,
                       "gpsimd": sem.sGC, "tensor": sem.sPEx})

    nbar = [0]

    def barrier():
        nbar[0] += 1
        tgt = nbar[0] * len(ENGINES)
        S.bump(sem.sBAR, len(ENGINES))
        for eng in ENGINES:
            emit(eng, lambda e: e.sem_inc(sem.sBAR, 1))
            wait(eng, sem.sBAR, tgt)

    for tt in range(TT):
        dma("sync", sem.sIN, xres[:, tt, :], din["x"][tt * 128:(tt + 1) * 128, :])
    x_loaded = S.c(sem.sIN)

    cstep("gpsimd", lambda e: e.iota(
        ident_f[:, :], [[1, 128]], channel_multiplier=-1,
        allow_small_or_imprecise_dtypes=True))
    cstep("gpsimd", lambda e: e.tensor_scalar(
        ident_f[:, :], ident_f[:, :], 0.0, None, OP.is_equal))
    run("gpsimd", lambda e: e.memset(onesf[:, :], 1.0))
    run("gpsimd", lambda e: e.memset(ones1b[:, :], 1.0))
    run("gpsimd", lambda e: e.memset(qgam2[0:2, :], 0.0))
    run("gpsimd", lambda e: e.memset(kgam2[0:2, :], 0.0))
    run("gpsimd", lambda e: e.memset(ceps[:, :], EPS_LN))
    cstep("gpsimd", lambda e: e.iota(
        o2tmp[:, 0:1], [[1, 1]], channel_multiplier=1,
        allow_small_or_imprecise_dtypes=True))
    cstep("gpsimd", lambda e: e.tensor_scalar(
        o2tmp[:, 1:2], o2tmp[:, 0:1], 63.5, None, OP.is_gt))
    inc("gpsimd", lambda e: e.tensor_scalar(
        o2tmp[:, 0:1], o2tmp[:, 1:2], -1.0, 1.0, OP.mult, OP.add), sem.sGB)
    wait("scalar", sem.sGB, 1)
    inc("scalar", lambda e: e.copy(out=ident_b[:, :], in_=ident_f[:, :]), sem.sACT)
    inc("scalar", lambda e: e.copy(out=ones2h[:, :], in_=o2tmp[:, :]), sem.sACT)
    consts_cast = S.c(sem.sACT)

    def _emit_const_loads():
        dma("gpsimd", sem.sIN, qgam2[0:1, 0:64], din["qg"][0:1, :])
        dma("gpsimd", sem.sIN, qgam2[1:2, 64:128], din["qg"][0:1, :])
        dma("gpsimd", sem.sIN, kgam2[0:1, 0:64], din["kg"][0:1, :])
        dma("gpsimd", sem.sIN, kgam2[1:2, 64:128], din["kg"][0:1, :])
        dma("gpsimd", sem.sIN, grow[:, :], din["lng"][:, :])
        dma("gpsimd", sem.sIN, brow[:, :], din["lnb"][:, :])
        dma("gpsimd", sem.sIN, borow[:, :], din["bo"][:, :])
        tgt = dma("gpsimd", sem.sIN, wgs[:, :, :],
                  din["wg"].rearrange("(cc p) e -> p cc e", p=128))
        wait("vector", sem.sIN, tgt)
        inc("vector", lambda e: e.tensor_scalar_mul(
            qgam2[0:2, :], qgam2[0:2, :], 0.125), sem.sDVE)
        return tgt

    def _emit_gb_bcast():
        dma("gpsimd", sem.sIN, grow[:, :], din["lng"][:, :])
        gb_tgt = dma("gpsimd", sem.sIN, brow[:, :], din["lnb"][:, :])
        wait("tensor", sem.sIN, gb_tgt)
        wait("tensor", sem.sACT, max(consts_cast, S.c(sem.sACT)))
        wait("tensor", sem.sDVE, S.c(sem.sDVE))
        wait("tensor", sem.sDVE2, S.c(sem.sDVE2))
        for half in range(2):
            if half == 1:
                wait("tensor", sem.sDVE, S.c(sem.sDVE))
            run("tensor", lambda e, h=half: e.matmul(
                P[6][:, :], onesf[0:1, :], grow[0:1, h * 512:(h + 1) * 512],
                start=True, stop=True))
            inc("tensor", lambda e, h=half: e.matmul(
                P[5][:, :], onesf[0:1, :], brow[0:1, h * 512:(h + 1) * 512],
                start=True, stop=True), sem.sPEx)
            wait("vector", sem.sPEx, S.c(sem.sPEx))
            run("vector", lambda e, h=half: e.tensor_copy(
                out=g_bc[:, h * 512:(h + 1) * 512], in_=P[6][:, :]))
            inc("vector", lambda e, h=half: e.tensor_copy(
                out=b_bc[:, h * 512:(h + 1) * 512], in_=P[5][:, :]), sem.sDVE)

    def _emit_wt_transposes():
        import os as _os
        wtn = int(_os.environ.get("K_WTN", "256"))
        wt_skip_store = _os.environ.get("K_WT_NOSTORE", "") == "1"
        wt_skip_dve = _os.environ.get("K_WT_NODVE", "") == "1"
        wt_skip_tr = _os.environ.get("K_WT_NOTRANS", "") == "1"
        wait("tensor", sem.sACT, consts_cast)
        store_tgts = []
        n_tr = 0
        for nm, dst in (("wq", wqt_d), ("wk", wkt_d), ("wv", wvt_d),
                        ("wo", wot_d)):
            for mm in range(8):
                for cc in range(8):
                    n = n_tr
                    if n >= wtn:
                        continue
                    par = n % 2
                    if n >= 2:
                        wait("gpsimd", sem.sPEx, S.c(sem.sPEx) - 1)
                    lt = dma("gpsimd", sem.sWC, wtmp[:, par, :],
                             din[nm][mm * 128:(mm + 1) * 128,
                                     cc * 128:(cc + 1) * 128])
                    wait("tensor", sem.sWC, lt)
                    if n >= 1:
                        wait("tensor", sem.sDVE2, S.c(sem.sDVE2))
                    if wt_skip_tr:
                        inc("tensor", lambda e: e.engine_nop()
                            if False else e.nop(), sem.sPEx)
                    else:
                        inc("tensor", lambda e, par=par: e.transpose(
                            p_bf[:, par * 128:par * 128 + 128], wtmp[:, par, :],
                            ident_b[:, :]), sem.sPEx)
                    wait("vector", sem.sPEx, S.c(sem.sPEx))
                    if n >= 2:
                        wait("vector", sem.sWS, store_tgts[n - 2])
                    if wt_skip_dve or wt_skip_tr:
                        inc("vector", lambda e, par=par: e.tensor_copy(
                            out=wsto[:, par, :], in_=wtmp[:, par, :]),
                            sem.sDVE2)
                    else:
                        inc("vector", lambda e, par=par: e.tensor_copy(
                            out=wsto[:, par, :],
                            in_=p_bf[:, par * 128:par * 128 + 128]), sem.sDVE2)
                    wait("sync", sem.sDVE2, S.c(sem.sDVE2))
                    if wt_skip_store:
                        st = S.bump(sem.sWS, 16)
                        emit("sync", lambda e: e.sem_inc(sem.sWS, 16))
                    else:
                        st = dma("sync", sem.sWS,
                                 dst[cc * 128:(cc + 1) * 128,
                                     mm * 128:(mm + 1) * 128],
                                 wsto[:, par, :])
                    store_tgts.append(st)
                    n_tr += 1

    relay_tgt = {}

    def _emit_relayout_expert(eidx):
        units = []
        for cc in range(CCn):
            units += [("w1", eidx, cc, 0), ("w1", eidx, cc, 1),
                      ("w3", eidx, cc, 0), ("w3", eidx, cc, 1)]
        units += [("w2", eidx, hh, 0) for hh in range(HHn)]
        _emit_relayout_units(units)

    rl_n = [0]

    def _emit_relayout_units(units):
        for (kind, eidx, i, hf) in units:
            par = rl_n[0] % 2
            rl_n[0] += 1
            wait("gpsimd", sem.sWS, S.c(sem.sWS) - 16)
            if kind in ("w1", "w3"):
                src = din[kind]
                dstc = w1c if kind == "w1" else w3c
                lt = dma("gpsimd", sem.sWC, wstripe[:, par, :],
                         src[eidx, i * 128:(i + 1) * 128,
                             hf * 2048:(hf + 1) * 2048])
                wait("gpsimd", sem.sWC, lt)
                st = dma("gpsimd", sem.sWS,
                         dstc[eidx, hf * 16:(hf + 1) * 16, :, i, :]
                         .transpose([1, 0, 2]),
                         wstripe[:, par, :].rearrange("p (hh w) -> p hh w",
                                                      w=128))
                relay_tgt[(kind, eidx, i, hf)] = st
                relay_tgt[(kind, eidx, i)] = st
            else:
                lt = dma("gpsimd", sem.sWC, wstripe[:, par, 0:1024],
                         din["w2"][eidx, i * 128:(i + 1) * 128, :])
                wait("gpsimd", sem.sWC, lt)
                st = dma("gpsimd", sem.sWS, w2c[eidx, i, :, :],
                         wstripe[:, par, 0:1024])
                relay_tgt[("w2", eidx, i)] = st

    def _emit_moe_cache():
        st_list = []
        units = []
        for eidx in range(E):
            for cc in range(CCn):
                units.append(("w1", eidx, cc))
                units.append(("w3", eidx, cc))
            for hh in range(HHn):
                units.append(("w2", eidx, hh))
        for u, (kind, eidx, i) in enumerate(units):
            par = u % 2
            if u >= 2:
                wait("gpsimd", sem.sWS, st_list[u - 2])
            if kind == "w1":
                lt = dma("gpsimd", sem.sWC, wstripe[:, par, :],
                         din["w1"][eidx, i * 128:(i + 1) * 128, :])
                wait("gpsimd", sem.sWC, lt)
                st = dma("gpsimd", sem.sWS,
                         w1c[eidx, :, :, i, :].transpose([1, 0, 2]),
                         wstripe[:, par, :].rearrange("p (hh w) -> p hh w",
                                                      w=128))
            elif kind == "w3":
                lt = dma("gpsimd", sem.sWC, wstripe[:, par, :],
                         din["w3"][eidx, i * 128:(i + 1) * 128, :])
                wait("gpsimd", sem.sWC, lt)
                st = dma("gpsimd", sem.sWS,
                         w3c[eidx, :, :, i, :].transpose([1, 0, 2]),
                         wstripe[:, par, :].rearrange("p (hh w) -> p hh w",
                                                      w=128))
            else:
                lt = dma("gpsimd", sem.sWC, wstripe[:, par, 0:1024],
                         din["w2"][eidx, i * 128:(i + 1) * 128, :])
                wait("gpsimd", sem.sWC, lt)
                st = dma("gpsimd", sem.sWS, w2c[eidx, i, :, :],
                         wstripe[:, par, 0:1024])
            st_list.append(st)
            relay_tgt[(kind, eidx, i)] = st

    def _emit_route_consts():
        wait("gpsimd", sem.sGB, 1)
        cstep("gpsimd", lambda e: e.iota(ubBi[:, :], [[1, 128]],
              channel_multiplier=-1,
              allow_small_or_imprecise_dtypes=True))
        inc("gpsimd", lambda e: e.iota(ubDi[:, 0:1], [[1, 1]],
            channel_multiplier=1), sem.sGB)
        iotas_done = S.c(sem.sGB)
        wait("vector", sem.sGB, iotas_done)
        rfull = carve(arenaC, 5632, 512, F32, (128,))
        cstep("vector", lambda e: e.tensor_copy(out=rfull[:, :],
                                                in_=ubBi[:, :]))
        cstep("vector", lambda e: e.tensor_scalar(
            u16f[:, :], rfull[:, :], 0.0, None, OP.is_gt))
        cstep("vector", lambda e: e.tensor_copy(out=usub[:, :],
                                                in_=u16f[:, :]))
        cstep("vector", lambda e: e.tensor_scalar(
            ubCi[:, :], ubBi[:, :], 7, None, OP.bitwise_and))
        cstep("vector", lambda e: e.tensor_copy(out=rfull[:, :],
                                                in_=ubCi[:, :]))
        cstep("vector", lambda e: e.tensor_scalar(
            rfull[:, :], rfull[:, :], 0.0, None, OP.is_equal))
        cstep("vector", lambda e: e.tensor_mul(
            u16f[:, :], u16f[:, :], rfull[:, :]))
        cstep("vector", lambda e: e.tensor_scalar(
            ubDi[:, 1:2], ubDi[:, 0:1], 7, None, OP.bitwise_and))
        cstep("vector", lambda e: e.tensor_scalar(
            ubDi[:, 2:3], ubDi[:, 1:2], 9, None, OP.arith_shift_left))
        cstep("vector", lambda e: e.tensor_scalar(
            ubDi[:, 3:4], ubDi[:, 1:2], 8, None, OP.arith_shift_left))
        cstep("vector", lambda e: e.tensor_tensor(
            ubDi[:, 2:3], ubDi[:, 2:3], ubDi[:, 3:4], OP.add))
        cstep("vector", lambda e: e.tensor_scalar(
            ubDi[:, 3:4], ubDi[:, 1:2], 7, None, OP.arith_shift_left))
        cstep("vector", lambda e: e.tensor_tensor(
            ubDi[:, 2:3], ubDi[:, 2:3], ubDi[:, 3:4], OP.add))
        inc("vector", lambda e: e.tensor_copy(
            out=ecapcol[:, 0:1], in_=ubDi[:, 2:3]), sem.sDVE)
        wait("tensor", sem.sDVE, S.c(sem.sDVE))
        inc("tensor", lambda e: e.transpose(
            P[6][0:1, 0:128], ecapcol[:, 0:1], ident_f[:, :]), sem.sPE1)
        wait("vector", sem.sPE1, S.c(sem.sPE1))
        cstep("vector", lambda e: e.tensor_copy(
            out=ecrow[0:1, :], in_=P[6][0:1, 0:128]))
        inc("vector", lambda e: e.tensor_scalar_add(
            ecrow[0:1, :], ecrow[0:1, :], float(CAP - 1)), sem.sDVE)
        rc = S.c(sem.sDVE)
        cstep("gpsimd", lambda e: e.iota(
            tokid16[:, :], [[128, 16]], channel_multiplier=1))
        run("gpsimd", lambda e: e.memset(onescol_b[:, :], 1.0))
        cstep("gpsimd", lambda e: e.memset(glii[:, :], 0))
        dma("gpsimd", sem.sGB, glist_d.rearrange("(p j) one -> p (j one)", p=128),
            glii[:, 0:56])
        return rc

    def _emit_softmax_bound():
        wait("vector", sem.sIN, gam_loaded[0])
        wait("vector", sem.sDVE, S.c(sem.sDVE))
        cstep("vector", lambda e: e.reduce_max(
            negmT[0:1, 0:1], qgam2[0:1, 0:64], axis=AX,
            apply_absolute_value=True))
        cstep("vector", lambda e: e.reduce_max(
            negmT[0:1, 1:2], kgam2[0:1, 0:64], axis=AX,
            apply_absolute_value=True))
        cstep("vector", lambda e: e.tensor_mul(
            negmT[0:1, 2:3], negmT[0:1, 0:1], negmT[0:1, 1:2]))
        inc("vector", lambda e: e.tensor_scalar_mul(
            negmT[0:1, 3:4], negmT[0:1, 2:3], -64.0), sem.sDVE)
        wait("tensor", sem.sDVE, S.c(sem.sDVE))
        wait("tensor", sem.sPEx, S.c(sem.sPEx))
        inc("tensor", lambda e: e.matmul(
            P[2][:, 0:1], onesf[0:1, :], negmT[0:1, 3:4],
            start=True, stop=True), sem.sPE1)
        wait("vector", sem.sPE1, S.c(sem.sPE1))
        inc("vector", lambda e: e.tensor_copy(
            out=biascols[:, 2:3], in_=P[2][:, 0:1]), sem.sDVE)

    gam_loaded = [0]
    if "consts" in init_parts:
        gam_loaded[0] = _emit_const_loads()
    else:
        gam_loaded[0] = dma("gpsimd", sem.sIN, wgs[:, :, :],
                            din["wg"].rearrange("(cc p) e -> p cc e", p=128))
    if "gb" in init_parts:
        _emit_gb_bcast()
    if "wt" in init_parts:
        _emit_wt_transposes()
    _emit_softmax_bound()
    rc_done = _emit_route_consts()

    wait("scalar", sem.sIN, x_loaded)
    wait("vector", sem.sIN, x_loaded)
    barrier()
    if "moe" in init_parts and os.environ.get("K_DENSE", "") == "1":
        _emit_moe_cache()
    wc_done = S.c(sem.sWS)

    def emit_ln(is_attn):
        dst_d = cc_in if is_attn else xnt_d
        if is_attn:
            wait("scalar", sem.sPEx, S.c(sem.sPEx))
            wait("vector", sem.sPEx, S.c(sem.sPEx))
        ssq_base = S.c(sem.sDVE)
        act_base = S.c(sem.sACT)
        for tt in range(TT):
            par = tt % 2
            if tt >= 2:
                wait("scalar", sem.sDVE, ssq_base + tt - 1)
            inc("scalar", lambda e, tt=tt, par=par: e.square(
                out=sq[:, par, :], in_=xres[:, tt, :]), sem.sACT)
            run("vector", lambda e, tt=tt: e.reduce_sum(
                stats[:, tt, 0:1], xres[:, tt, :], axis=AX))
            wait("vector", sem.sACT, act_base + tt + 1)
            inc("vector", lambda e, tt=tt, par=par: e.reduce_sum(
                stats[:, tt, 1:2], sq[:, par, :], axis=AX), sem.sDVE)
            S.bump(sem.sDVE, 0)
        wait("vector", sem.sDVE, S.c(sem.sDVE))
        cstep("vector", lambda e: e.tensor_scalar_mul(
            stats[:, :, 2:3], stats[:, :, 0:1], 1.0 / C))
        cstep("vector", lambda e: e.tensor_mul(
            stats[:, :, 6:7], stats[:, :, 2:3], stats[:, :, 2:3]))
        inc("vector", lambda e: e.scalar_tensor_tensor(
            stats[:, :, 3:4], stats[:, :, 1:2], 1.0 / C, stats[:, :, 6:7],
            OP.mult, OP.subtract), sem.sDVE)
        wait("scalar", sem.sDVE, S.c(sem.sDVE))
        inc("scalar", lambda e: e.activation(
            out=stats[:, :, 4:5], in_=stats[:, :, 3:4], func=AF.Sqrt,
            bias=ceps[:, 0:1]), sem.sACT)
        wait("vector", sem.sACT, S.c(sem.sACT))
        cstep("vector", lambda e: e.reciprocal(stats[:, :, 5:6], stats[:, :, 4:5]))
        if is_attn:
            cstep("vector", lambda e: e.tensor_mul(
                stats[:, :, 6:7], stats[:, :, 5:6], stats[:, :, 5:6]))
            inc("vector", lambda e: e.tensor_mul(
                stats[:, :, 6:7], stats[:, :, 3:4], stats[:, :, 6:7]), sem.sDVE)
            wait("scalar", sem.sDVE, S.c(sem.sDVE))
            inc("scalar", lambda e: e.activation(
                out=stats[:, :, 4:5], in_=stats[:, :, 6:7], func=AF.Sqrt,
                bias=ceps[:, 0:1]), sem.sACT)
            wait("vector", sem.sACT, S.c(sem.sACT))
            cstep("vector", lambda e: e.reciprocal(stats[:, :, 6:7], stats[:, :, 4:5]))
            cstep("vector", lambda e: e.tensor_mul(
                stats[:, :, 5:6], stats[:, :, 5:6], stats[:, :, 6:7]))
        st_tgts = []
        for tt in range(TT):
            par = tt % 2
            if tt >= 2:
                wait("vector", sem.sXN, st_tgts[tt - 2])
            if is_attn:
                cstep("vector", lambda e, tt=tt, par=par: e.tensor_scalar(
                    sq[:, par, :], xres[:, tt, :], stats[:, tt, 2:3],
                    stats[:, tt, 5:6], OP.subtract, OP.mult))
                cstep("vector", lambda e, par=par: e.tensor_mul(
                    sq[:, par, :], sq[:, par, :], g_bc[:, :]))
                inc("vector", lambda e, par=par: e.tensor_add(
                    xnt_st[:, par, :], sq[:, par, :], b_bc[:, :]), sem.sDVE)
            else:
                inc("vector", lambda e, tt=tt, par=par: e.tensor_scalar(
                    xnt_st[:, par, :], xres[:, tt, :], stats[:, tt, 2:3],
                    stats[:, tt, 5:6], OP.subtract, OP.mult), sem.sDVE)
            wait("sync", sem.sDVE, S.c(sem.sDVE))
            st_tgts.append(dma("sync", sem.sXN,
                               dst_d[tt * 128:(tt + 1) * 128, :],
                               xnt_st[:, par, :]))
        return S.c(sem.sXN)

    def emit_gating():
        for tt in range(TT):
            for cc in range(CCn):
                fn = (lambda e, tt=tt, cc=cc: e.matmul(
                    P[0][:, tt * E:(tt + 1) * E],
                    xnfm[:, cc, tt * 128:(tt + 1) * 128],
                    wgs[:, cc, :], start=(cc == 0), stop=(cc == CCn - 1)))
                if tt == TT - 1 and cc == CCn - 1:
                    inc("tensor", fn, sem.sPE1)
                else:
                    run("tensor", fn)
        wait("vector", sem.sPE1, S.c(sem.sPE1))
        sqf = sq.rearrange("p a b -> p (a b)")
        lg = sqf[:, 0:128].rearrange("p (a b) -> p a b", b=8)
        lg2 = sqf[:, 128:256].rearrange("p (a b) -> p a b", b=8)
        m1 = sqf[:, 256:272]
        m2 = sqf[:, 272:288]
        dm = sqf[:, 288:304]
        sg = sqf[:, 304:320]
        m1b = sqf[:, 256:272].unsqueeze(2).broadcast_to([128, 16, 8])
        m2b = sqf[:, 272:288].unsqueeze(2).broadcast_to([128, 16, 8])
        msk1v = msk1b[:, :, :]
        msk2v = msk2b[:, :, :]
        cstep("vector", lambda e: e.tensor_copy(
            out=sqf[:, 0:128], in_=P[0][:, 0:128]))
        cstep("vector", lambda e: e.tensor_reduce(
            m1, lg, axis=AX, op=OP.max))
        cstep("vector", lambda e: e.tensor_tensor(
            msk1v, lg, m1b, OP.is_equal))
        cstep("vector", lambda e: e.scalar_tensor_tensor(
            lg2, msk1v, -1e30, lg, OP.mult, OP.add))
        cstep("vector", lambda e: e.tensor_reduce(
            m2, lg2, axis=AX, op=OP.max))
        cstep("vector", lambda e: e.tensor_tensor(
            msk2v, lg2, m2b, OP.is_equal))
        inc("vector", lambda e: e.tensor_tensor(
            dm, m1, m2, OP.subtract), sem.sDVE)
        wait("scalar", sem.sDVE, S.c(sem.sDVE))
        inc("scalar", lambda e: e.activation(
            out=sg, in_=dm, func=AF.Sigmoid), sem.sACT)
        wait("vector", sem.sACT, S.c(sem.sACT))
        cstep("vector", lambda e: e.tensor_scalar_mul(
            wts[:, :, 0:1].rearrange("p a b -> p (a b)"), sg, 0.5))
        inc("vector", lambda e: e.tensor_scalar(
            wts[:, :, 1:2].rearrange("p a b -> p (a b)"),
            wts[:, :, 0:1].rearrange("p a b -> p (a b)"),
            -1.0, 0.5, OP.mult, OP.add), sem.sDVE)

    import os as _os2
    ffn_parts = set(_os2.environ.get("K_FFN", "ln,gate,exp").split(","))
    n_exp = int(_os2.environ.get("K_NEXP", "8"))
    exp_parts = set(_os2.environ.get("K_EXP", "h,eo,evac").split(","))

    def emit_ffn(call):
        xn_tgt = emit_ln(is_attn=False)
        wait("sync", sem.sXN, xn_tgt)
        for cc in range(CCn):
            dma("sync", sem.sXN, xnfm[:, cc, :],
                xnt_d[:, cc * 128:(cc + 1) * 128], transpose=True)
        xnfm_tgt = S.c(sem.sXN)
        wait("tensor", sem.sXN, xnfm_tgt)
        wait("tensor", sem.sDVE2, S.c(sem.sDVE2))
        wait("tensor", sem.sACT, S.c(sem.sACT))
        if "gate" in ffn_parts:
            emit_gating()
        if "exp" not in ffn_parts:
            return
        if dbg_what in ("cmat", "logits") and call == 0:
            wait("gpsimd", sem.sDVE, S.c(sem.sDVE))
            dma("gpsimd", sem.sOUT,
                dbg_d[:, 0:8].rearrange("(tt p) e -> p tt e", p=128),
                cmat[:, :, :])
        if dbg_what == "xn" and call == 0:
            wait("gpsimd", sem.sXN, S.c(sem.sXN))
            dma("gpsimd", sem.sOUT, dbg_d, xnt_d[:, 0:64])

        if dbg_what in ("w1c", "w2c") and call == 0:
            src = (w1c if dbg_what == "w1c" else w2c.rearrange(
                "e hh p (cc w) -> e hh p cc w", w=128))
            wait("gpsimd", sem.sWS, wc_done)
            dma("gpsimd", sem.sOUT,
                dbg_d.rearrange("(p a) b -> p (a b)", p=128),
                src[0, 0, :, :, :].rearrange("p cc w -> p (cc w)"))
        wait("tensor", sem.sDVE, S.c(sem.sDVE))

        for eidx in range(n_exp):
            for tg in range(4):
                for hh in range(HHn if "h" in exp_parts else 0):
                    par = hh % 4
                    glob = S.c(sem.sPEh)
                    if glob >= 4:
                        wait("sync", sem.sPEh, glob - 3)
                    import os as _os4
                    if call == 0 and relay_tgt:
                        wait("sync", sem.sWS,
                             max(relay_tgt[("w1", eidx, 7)],
                                 relay_tgt[("w3", eidx, 7)]))
                    t1 = dma("sync", sem.sW1, w1t[:, par, :, :],
                             w1c[eidx, hh, :, :, :])
                    t3 = dma("sync", sem.sW3, w3t[:, par, :, :],
                             w3c[eidx, hh, :, :, :])
                    wait("tensor", sem.sW1, t1)
                    if hh <= 1:
                        wait("tensor", sem.sDVE2, S.c(sem.sDVE2))
                    if glob >= 2:
                        wait("tensor", sem.sACT, S.c(sem.sACT) - 1)
                        wait("tensor", sem.sDVE, S.c(sem.sDVE) - 1)
                    for cc in range(CCn):
                        fn = (lambda e, hh=hh, cc=cc, tg=tg, par=par: e.matmul(
                            P[2 * (par % 2)][:, :], w1t[:, par, cc, :],
                            xnfm[:, cc, tg * 512:(tg + 1) * 512],
                            start=(cc == 0), stop=(cc == CCn - 1)))
                        if cc == CCn - 1:
                            inc("tensor", fn, sem.sPE1)
                        else:
                            run("tensor", fn)
                    wait("tensor", sem.sW3, t3)
                    for cc in range(CCn):
                        fn = (lambda e, hh=hh, cc=cc, tg=tg, par=par: e.matmul(
                            P[2 * (par % 2) + 1][:, :], w3t[:, par, cc, :],
                            xnfm[:, cc, tg * 512:(tg + 1) * 512],
                            start=(cc == 0), stop=(cc == CCn - 1)))
                        if cc == CCn - 1:
                            inc("tensor", fn, sem.sPEh)
                        else:
                            run("tensor", fn)
                    wait("scalar", sem.sPE1, S.c(sem.sPE1))
                    if glob >= 2:
                        wait("scalar", sem.sDVE, S.c(sem.sDVE) - 1)
                    inc("scalar", lambda e, par=par: e.activation(
                        out=sl[:, par % 2, :], in_=P[2 * (par % 2)][:, :],
                        func=AF.Silu), sem.sACT)
                    wait("vector", sem.sPEh, S.c(sem.sPEh))
                    wait("vector", sem.sACT, S.c(sem.sACT))
                    inc("vector", lambda e, hh=hh, par=par: e.tensor_tensor(
                        hid[:, hh, :], sl[:, par % 2, :],
                        P[2 * (par % 2) + 1][:, :], OP.mult), sem.sDVE)
                if "eo" not in exp_parts:
                    continue
                hid_done = S.c(sem.sDVE)
                wait("tensor", sem.sDVE, hid_done)
                wait("tensor", sem.sDVE2, S.c(sem.sDVE2))
                for hh in range(HHn):
                    par = hh % 2
                    globe = S.c(sem.sPEeo)
                    if globe >= 2:
                        wait("sync", sem.sPEeo, globe - 1)
                    if call == 0 and relay_tgt:
                        wait("sync", sem.sWS, relay_tgt[("w2", eidx, hh)])
                    t2 = dma("sync", sem.sW2, w2t[:, par, :],
                             w2c[eidx, hh, :, :])
                    wait("tensor", sem.sW2, t2)
                    for j in range(8):
                        ch, tt4 = j // 4, j % 4
                        fn = (lambda e, hh=hh, par=par, ch=ch, tt4=tt4, j=j:
                              e.matmul(
                                  P[j][:, :],
                                  hid[:, hh, tt4 * 128:(tt4 + 1) * 128],
                                  w2t[:, par, ch * 512:(ch + 1) * 512],
                                  start=(hh == 0), stop=(hh == HHn - 1)))
                        if j == 7:
                            inc("tensor", fn, sem.sPEeo)
                        else:
                            run("tensor", fn)
                if "evac" not in exp_parts:
                    continue
                wait("vector", sem.sPEeo, S.c(sem.sPEeo))
                for j in range(8):
                    ch, tt4 = j // 4, j % 4
                    tta = tg * 4 + tt4
                    inc("vector", lambda e, j=j, ch=ch, tta=tta, eidx=eidx:
                        e.scalar_tensor_tensor(
                            xres[:, tta, ch * 512:(ch + 1) * 512], P[j][:, :],
                            cmat[:, tta, eidx:eidx + 1],
                            xres[:, tta, ch * 512:(ch + 1) * 512],
                            OP.mult, OP.add), sem.sDVE2)

        if dbg_what == "hid" and call == 0:
            wait("gpsimd", sem.sDVE, S.c(sem.sDVE))
            dma("gpsimd", sem.sOUT,
                dbg_d[0:1024, :].rearrange("(p a) b -> p (a b)", p=128),
                hid[:, 0, :])


    def emit_ffn_sparse(call):
        xn_tgt = emit_ln(is_attn=False)
        wait("sync", sem.sXN, xn_tgt)
        for cc in range(CCn):
            dma("sync", sem.sXN, xnfm[:, cc, :],
                xnt_d[:, cc * 128:(cc + 1) * 128], transpose=True)
        xnfm_tgt = S.c(sem.sXN)
        wait("tensor", sem.sXN, xnfm_tgt)
        wait("tensor", sem.sDVE2, S.c(sem.sDVE2))
        wait("tensor", sem.sACT, S.c(sem.sACT))
        emit_gating()
        gate_done = S.c(sem.sDVE)

        m1v = msk1b.rearrange("p a b -> p (a b)")
        m2v = msk2b.rearrange("p a b -> p (a b)")
        wait("vector", sem.sDVE, gate_done)
        inc("vector", lambda e: e.tensor_add(msumb[:, :], m1v, m2v), sem.sDVE)
        msum_tgt = S.c(sem.sDVE)
        wait("tensor", sem.sDVE, max(msum_tgt, rc_done))
        run("tensor", lambda e: e.matmul(
            P[2][:, 0:128], usub[:, :], msumb[:, :], start=True, stop=True))
        inc("tensor", lambda e: e.matmul(
            P[2][0:1, 128:256], onescol_b[:, :], msumb[:, :],
            start=True, stop=True), sem.sPE1)
        wait("vector", sem.sPE1, S.c(sem.sPE1))
        inc("vector", lambda e: e.tensor_copy(out=SRr[:, :], in_=P[2][:, 0:128]),
            sem.sDVE)
        cstep("vector", lambda e: e.tensor_copy(
            out=rrow_f[0:1, :], in_=P[2][0:1, 128:256]))
        inc("vector", lambda e: e.nop() if False else e.tensor_copy(
            out=rsc_f[:, 0:1], in_=rsc_f[:, 0:1]), sem.sDVE)
        cnt_tgt = S.c(sem.sDVE)
        wait("tensor", sem.sDVE, cnt_tgt)
        inc("tensor", lambda e: e.transpose(
            P[6][0:128, 0:1], rrow_f[0:1, :], ident_f[0:1, 0:1]), sem.sPE1)
        wait("vector", sem.sPE1, S.c(sem.sPE1))
        inc("vector", lambda e: e.tensor_copy(out=rsc_f[:, 1:2],
                                              in_=P[6][0:128, 0:1]), sem.sDVE)
        wait("tensor", sem.sDVE, S.c(sem.sDVE))
        inc("tensor", lambda e: e.matmul(
            P[6][:, 1:2], u16f[:, :], rsc_f[:, 1:2], start=True, stop=True),
            sem.sPE1)
        wait("vector", sem.sPE1, S.c(sem.sPE1))
        inc("vector", lambda e: e.tensor_add(
            rsc_f[:, 2:3], P[6][:, 1:2], ecapcol[:, 0:1]), sem.sDVE)
        wait("tensor", sem.sDVE, S.c(sem.sDVE))
        inc("tensor", lambda e: e.transpose(
            P[6][0:1, 2:130], rsc_f[:, 2:3], ident_f[:, :]), sem.sPE1)
        wait("vector", sem.sPE1, S.c(sem.sPE1))
        inc("vector", lambda e: e.tensor_copy(
            out=rrow_f[0:1, :], in_=P[6][0:1, 2:130]), sem.sDVE)
        wait("tensor", sem.sDVE, S.c(sem.sDVE))
        inc("tensor", lambda e: e.matmul(
            P[6][:, 256:384], onesf[0:1, :], rrow_f[0:1, :],
            start=True, stop=True), sem.sPE1)
        wait("vector", sem.sPE1, S.c(sem.sPE1))
        inc("vector", lambda e: e.tensor_add(
            SRr[:, :], SRr[:, :], P[6][:, 256:384]), sem.sDVE)
        if os.environ.get("K_NOCLAMP", "") != "1":
            wait("tensor", sem.sDVE, S.c(sem.sDVE))
            inc("tensor", lambda e: e.matmul(
                P[6][:, 0:128], onesf[0:1, :], ecrow[0:1, :],
                start=True, stop=True), sem.sPE1)
            wait("vector", sem.sPE1, S.c(sem.sPE1))
            inc("vector", lambda e: e.tensor_tensor(
                SRr[:, :], SRr[:, :], P[6][:, 0:128], OP.min), sem.sDVE)
        sr_tgt = S.c(sem.sDVE)
        sqf = sq.rearrange("p a b -> p (a b)")
        troute = sqf[:, 0:128]
        tr16a = sqf[:, 128:144]
        tr16b = sqf[:, 144:160]
        for k, mv in ((0, m1v), (1, m2v)):
            wait("vector", sem.sDVE, sr_tgt)
            cstep("vector", lambda e, mv=mv: e.tensor_tensor(
                troute, mv, SRr[:, :], OP.mult))
            cstep("vector", lambda e, k=k: e.tensor_reduce(
                srt[:, :, k:k + 1],
                troute.rearrange("p (a b) -> p a b", b=8), axis=AX, op=OP.add))
            srk = srt[:, :, k:k + 1].rearrange("p a b -> p (a b)")
            si = sint2[:, :, k:k + 1].rearrange("p a b -> p (a b)")
            cstep("vector", lambda e, srk=srk, si=si: e.tensor_copy(
                out=si, in_=srk))
            cstep("vector", lambda e, si=si: e.tensor_scalar(
                tri_a[:, :], si, 127, None, OP.bitwise_and))
            cstep("vector", lambda e, si=si: e.tensor_scalar(
                tri_b[:, :], si, 7, None, OP.arith_shift_right))
            cstep("vector", lambda e: e.tensor_scalar(
                tri_b[:, :], tri_b[:, :], 0, None, OP.add))
            spv = sint[:, :, k:k + 1].rearrange("p a b -> p (a b)")
            cstep("vector", lambda e, spv=spv: e.tensor_scalar(
                spv, tri_a[:, :], 5, None, OP.arith_shift_left))
            cstep("vector", lambda e: e.tensor_scalar(
                tri_c[:, :], tri_a[:, :], 4, None, OP.arith_shift_left))
            cstep("vector", lambda e, spv=spv: e.tensor_tensor(
                spv, spv, tri_c[:, :], OP.add))
            cstep("vector", lambda e: e.tensor_scalar(
                tri_c[:, :], tri_a[:, :], 3, None, OP.arith_shift_left))
            cstep("vector", lambda e, spv=spv: e.tensor_tensor(
                spv, spv, tri_c[:, :], OP.add))
            inc("vector", lambda e, spv=spv: e.tensor_tensor(
                spv, spv, tri_b[:, :], OP.add), sem.sDVE)
        sint_tgt = S.c(sem.sDVE)

        wait("gpsimd", sem.sDVE, sint_tgt)
        for tt in range(TT):
            for k in range(2):
                idma("gpsimd", sem.sGB, glist_d[:, :],
                     tokid16[:, tt:tt + 1],
                     out_off=sint[:, tt, k:k + 1])
        sc_tgt = S.c(sem.sGB)
        wait("gpsimd", sem.sGB, sc_tgt)
        glt = dma("gpsimd", sem.sGB, glii[:, 0:56],
                  glist_d.rearrange("(p j) one -> p (j one)", p=128))

        tload_tgts = {}
        gst_tgts = {}
        hdone_tgts = {}
        eod_tgts = []

        def emit_gathers(eidx):
            wait("gpsimd", sem.sGB, glt)
            if eidx >= 2:
                wait("gpsimd", sem.sST, tload_tgts[eidx - 2])
            for i in range(NT):
                gpar = i % 2
                if i >= 2:
                    wait("gpsimd", sem.sST, S.c(sem.sST) - 16)
                idma("gpsimd", sem.sGB, xgstage[:, gpar, :],
                     xnt_d[0:128, :],
                     in_off=glii[:, eidx * NT + i:eidx * NT + i + 1])
                wait("gpsimd", sem.sGB, S.c(sem.sGB))
                dma("gpsimd", sem.sST,
                    xg_d[eidx % 2, i * 128:(i + 1) * 128, :],
                    xgstage[:, gpar, :])
            gst_tgts[eidx] = S.c(sem.sST)

        def emit_tloads(eidx):
            wait("scalar", sem.sST, gst_tgts[eidx])
            if eidx >= 1:
                wait("scalar", sem.sPEh, hdone_tgts[eidx - 1])
            for cc in range(CCn):
                dma("scalar", sem.sST, xg_fm[:, cc, :],
                    xg_d[eidx % 2, :, cc * 128:(cc + 1) * 128], transpose=True)
            tload_tgts[eidx] = S.c(sem.sST)

        if call == 0 and ("moe" in init_parts):
            _emit_relayout_expert(0)
        emit_gathers(0)
        emit_tloads(0)
        for eidx in range(E):
            wait("tensor", sem.sST, tload_tgts[eidx])
            if call == 0 and ("moe" in init_parts) and eidx + 1 < E:
                _emit_relayout_expert(eidx + 1)

            for hh in range(HHn):
                par = hh % 4
                pp = hh % 2
                glob = S.c(sem.sPEh)
                if glob >= 4:
                    wait("sync", sem.sPEh, glob - 3)
                if call == 0 and relay_tgt:
                    wait("sync", sem.sWS, max(relay_tgt[("w1", eidx, 7)],
                                              relay_tgt[("w3", eidx, 7)]))
                t1 = dma("sync", sem.sW1, w1t[:, par, :, :],
                         w1c[eidx, hh, :, :, :])
                t3 = dma("sync", sem.sW3, w3t[:, par, :, :],
                         w3c[eidx, hh, :, :, :])
                wait("tensor", sem.sW1, t1)
                if hh <= 1:
                    wait("tensor", sem.sACT, S.c(sem.sACT))
                if glob >= 2:
                    wait("tensor", sem.sACT, S.c(sem.sACT) - 2)
                    wait("tensor", sem.sDVE, S.c(sem.sDVE) - 2)
                for cc in range(CCn):
                    run("tensor", lambda e, cc=cc, par=par, pp=pp: e.matmul(
                        P[4 * pp][:, :], w1t[:, par, cc, :],
                        xg_fm[:, cc, 0:512],
                        start=(cc == 0), stop=(cc == CCn - 1)))
                for cc in range(CCn):
                    fn = (lambda e, cc=cc, par=par, pp=pp: e.matmul(
                        P[4 * pp + 1][:, 0:384], w1t[:, par, cc, :],
                        xg_fm[:, cc, 512:896],
                        start=(cc == 0), stop=(cc == CCn - 1)))
                    if cc == CCn - 1:
                        inc("tensor", fn, sem.sPE1)
                    else:
                        run("tensor", fn)
                h1_tgt = S.c(sem.sPE1)
                wait("tensor", sem.sW3, t3)
                for cc in range(CCn):
                    run("tensor", lambda e, cc=cc, par=par, pp=pp: e.matmul(
                        P[4 * pp + 2][:, :], w3t[:, par, cc, :],
                        xg_fm[:, cc, 0:512],
                        start=(cc == 0), stop=(cc == CCn - 1)))
                for cc in range(CCn):
                    fn = (lambda e, cc=cc, par=par, pp=pp: e.matmul(
                        P[4 * pp + 3][:, 0:384], w3t[:, par, cc, :],
                        xg_fm[:, cc, 512:896],
                        start=(cc == 0), stop=(cc == CCn - 1)))
                    if cc == CCn - 1:
                        inc("tensor", fn, sem.sPEh)
                    else:
                        run("tensor", fn)
                wait("scalar", sem.sPE1, h1_tgt)
                if glob >= 2:
                    wait("scalar", sem.sDVE, S.c(sem.sDVE) - 2)
                run("scalar", lambda e, pp=pp: e.activation(
                    out=sl2[:, pp, 0:512], in_=P[4 * pp][:, :], func=AF.Silu))
                inc("scalar", lambda e, pp=pp: e.activation(
                    out=sl2[:, pp, 512:896], in_=P[4 * pp + 1][:, 0:384],
                    func=AF.Silu), sem.sACT, 2)
                wait("vector", sem.sPEh, S.c(sem.sPEh))
                wait("vector", sem.sACT, S.c(sem.sACT))
                run("vector", lambda e, hh=hh, pp=pp: e.tensor_tensor(
                    hidg[:, hh, 0:512], sl2[:, pp, 0:512],
                    P[4 * pp + 2][:, :], OP.mult))
                inc("vector", lambda e, hh=hh, pp=pp: e.tensor_tensor(
                    hidg[:, hh, 512:896], sl2[:, pp, 512:896],
                    P[4 * pp + 3][:, 0:384], OP.mult), sem.sDVE, 2)
            hdone_tgts[eidx] = S.c(sem.sPEh)
            hid_done = S.c(sem.sDVE)

            if eidx + 1 < E:
                emit_gathers(eidx + 1)
                emit_tloads(eidx + 1)

            wait("tensor", sem.sDVE, hid_done)
            for w, (t0, nt) in enumerate(((0, 4), (4, 3))):
                wait("tensor", sem.sACT, S.c(sem.sACT))
                for hh in range(HHn):
                    par = hh % 2
                    globe = S.c(sem.sPEeo)
                    if globe >= 2:
                        wait("sync", sem.sPEeo, globe - 1)
                    if call == 0 and relay_tgt:
                        wait("sync", sem.sWS, relay_tgt[("w2", eidx, hh)])
                    t2 = dma("sync", sem.sW2, w2t[:, par, :],
                             w2c[eidx, hh, :, :])
                    wait("tensor", sem.sW2, t2)
                    for j in range(2 * nt):
                        tl, ch = j // 2, j % 2
                        fn = (lambda e, hh=hh, par=par, tl=tl, ch=ch, j=j,
                              t0=t0: e.matmul(
                                  P[j][:, :],
                                  hidg[:, hh, (t0 + tl) * 128:
                                       (t0 + tl + 1) * 128],
                                  w2t[:, par, ch * 512:(ch + 1) * 512],
                                  start=(hh == 0), stop=(hh == HHn - 1)))
                        if j == 2 * nt - 1:
                            inc("tensor", fn, sem.sPEeo)
                        else:
                            run("tensor", fn)
                eo_tgt = S.c(sem.sPEeo)
                wait("scalar", sem.sPEeo, eo_tgt)
                for tl in range(nt):
                    spar = len(eod_tgts) % 2
                    if len(eod_tgts) >= 2:
                        wait("scalar", sem.sEOD, eod_tgts[-2])
                    run("scalar", lambda e, tl=tl, spar=spar: e.copy(
                        out=eostage[:, spar, 0:512], in_=P[2 * tl][:, :]))
                    inc("scalar", lambda e, tl=tl, spar=spar: e.copy(
                        out=eostage[:, spar, 512:1024], in_=P[2 * tl + 1][:, :]),
                        sem.sACT)
                    wait("gpsimd", sem.sACT, S.c(sem.sACT))
                    base_row = eidx * CAP + (t0 + tl) * 128
                    eod_tgts.append(dma(
                        "gpsimd", sem.sEOD, eo_d[base_row:base_row + 128, :],
                        eostage[:, spar, :]))
        eo_all = S.c(sem.sEOD)

        if dbg_what == "sr" and call == 0:
            wait("gpsimd", sem.sDVE, sint_tgt)
            dma("gpsimd", sem.sOUT,
                dbg_d[0:256, :].rearrange("(p a) b -> p (a b)", p=128),
                SRr[:, :])
        if dbg_what == "w1c" and call == 0:
            wait("gpsimd", sem.sWS, S.c(sem.sWS))
            dma("gpsimd", sem.sOUT,
                dbg_d.rearrange("(p a) b -> p (a b)", p=128),
                w1c[0, 0, :, :, :].rearrange("p cc w -> p (cc w)"))
        if dbg_what == "glist" and call == 0:
            wait("gpsimd", sem.sGB, S.c(sem.sGB))
            dma("gpsimd", sem.sOUT, dbg_d[0:128, 0:56], glii[:, 0:56])
        wait("gpsimd", sem.sEOD, eo_all)
        wait("gpsimd", sem.sDVE, sint_tgt)
        for tt in range(TT):
            if tt >= 1:
                wait("gpsimd", sem.sDVE2, S.c(sem.sDVE2))
            for k in range(2):
                idma("gpsimd", sem.sGB, xgstage[:, k, :], eo_d[0:128, :],
                     in_off=sint2[:, tt, k:k + 1])
            gb_tgt = S.c(sem.sGB)
            wait("vector", sem.sGB, gb_tgt)
            run("vector", lambda e, tt=tt: e.scalar_tensor_tensor(
                xres[:, tt, :], xgstage[:, 0, :], wts[:, tt, 0:1],
                xres[:, tt, :], OP.mult, OP.add))
            inc("vector", lambda e, tt=tt: e.scalar_tensor_tensor(
                xres[:, tt, :], xgstage[:, 1, :], wts[:, tt, 1:2],
                xres[:, tt, :], OP.mult, OP.add), sem.sDVE2)

    def emit_attn():
        _emit_gb_bcast()
        if dbg_what == "attnc":
            wait("gpsimd", sem.sDVE, S.c(sem.sDVE))
            dma("gpsimd", sem.sOUT, dbg_d[0:128, 0:4], biascols[:, 0:4])
            dma("gpsimd", sem.sOUT, dbg_d[128:256, 0:64],
                g_bc[:, 0:64].bitcast(F32))
            dma("gpsimd", sem.sOUT, dbg_d[256:384, 0:64], b_bc[:, 0:64])
        cc_tgt = emit_ln(is_attn=True)
        if dbg_what == "xn2":
            wait("gpsimd", sem.sXN, cc_tgt)
            dma("gpsimd", sem.sOUT, dbg_d, cc_in[:, 0:64])
        for tt in range(TT):
            dma("gpsimd", sem.sSP, resid_d[tt * 128:(tt + 1) * 128, :],
                xres[:, tt, :])
        sp_tgt = S.c(sem.sSP)
        wait("gpsimd", sem.sXN, cc_tgt)
        inc("gpsimd", lambda e: e.collective_compute(
            "AllGather", OP.bypass,
            replica_groups=[[0, 1], [2, 3], [4, 5], [6, 7]],
            ins=[cc_in.opt()], outs=[cc_out.opt()]), sem.sCC)
        cc_done = S.c(sem.sCC)
        wait("gpsimd", sem.sSP, sp_tgt)

        barrier()

        wait("sync", sem.sWS, wc_done)
        wait("sync", sem.sCC, cc_done)
        wait("sync", sem.sSP, sp_tgt)
        for cc in range(CCn):
            dma("sync", sem.sST, wph[:, cc, :], wkt_d[cc * 128:(cc + 1) * 128, :])
        wkt_tgt = S.c(sem.sST)


        def emit_proj_rms(n_units, proj_dst, rhs_of, gam, sq_tgts, sqrt_tgts,
                          recip_tgts, mult_tgts):
            for step in range(n_units + 1):
                kc = step
                pc = step - 1
                if pc >= 0:
                    wait("tensor", sem.sACT, sq_tgts[pc])
                    inc("tensor", lambda e, pc=pc: e.matmul(
                        P[2][0:2, :], ones2h[:, :], ksq[:, pc % 2, :],
                        start=True, stop=True), sem.sPE1)
                    ss_tgt = S.c(sem.sPE1)
                if kc < n_units:
                    if kc >= 2:
                        wait("tensor", sem.sACT, sq_tgts[kc - 2])
                    for cc in range(CCn):
                        fn = (lambda e, cc=cc, kc=kc: e.matmul(
                            P[kc % 2][:, :], wph[:, cc, kc * 128:(kc + 1) * 128],
                            rhs_of(kc, cc),
                            start=(cc == 0), stop=(cc == CCn - 1)))
                        if cc == CCn - 1:
                            inc("tensor", fn, sem.sPE1)
                        else:
                            run("tensor", fn)
                    proj_tgt = S.c(sem.sPE1)
                if pc >= 0:
                    wait("scalar", sem.sPE1, ss_tgt)
                    inc("scalar", lambda e, pc=pc: e.activation(
                        out=rkt[0:2, pc % 2, :], in_=P[2][0:2, :], func=AF.Sqrt,
                        scale=1.0 / DH), sem.sACT)
                    sqrt_tgts.append(S.c(sem.sACT))
                if kc < n_units:
                    wait("scalar", sem.sPE1, proj_tgt)
                    run("scalar", lambda e, kc=kc: e.copy(
                        out=proj_dst(kc), in_=P[kc % 2][:, :]))
                    inc("scalar", lambda e, kc=kc: e.activation(
                        out=ksq[:, kc % 2, :], in_=P[kc % 2][:, :],
                        func=AF.Square), sem.sACT)
                    sq_tgts.append(S.c(sem.sACT))
                if pc >= 0:
                    wait("vector", sem.sACT, sqrt_tgts[pc])
                    cstep("vector", lambda e, pc=pc: e.tensor_scalar_max(
                        rkt[0:2, pc % 2, :], rkt[0:2, pc % 2, :], EPS_RMS))
                    inc("vector", lambda e, pc=pc: e.reciprocal(
                        rkt[0:2, pc % 2, :], rkt[0:2, pc % 2, :]), sem.sDVE)
                    recip_tgts.append(S.c(sem.sDVE))
                    wait("tensor", sem.sDVE, recip_tgts[pc])
                    if pc >= 1:
                        wait("tensor", sem.sDVE, mult_tgts[pc - 1])
                    inc("tensor", lambda e, pc=pc: e.matmul(
                        P[3][:, :], gam[:, :], rkt[0:2, pc % 2, :],
                        start=True, stop=True), sem.sPE1)
                    bc_tgt = S.c(sem.sPE1)
                    wait("vector", sem.sPE1, bc_tgt)
                    inc("vector", lambda e, pc=pc: e.tensor_tensor(
                        proj_dst(pc), proj_dst(pc), P[3][:, :], OP.mult),
                        sem.sDVE)
                    mult_tgts.append(S.c(sem.sDVE))

        k_sg_pe = []
        for sg in range(NSG):
            par = sg % 2
            if sg >= 2:
                wait("sync", sem.sPE1, k_sg_pe[sg - 2])
            for cc in range(CCn):
                dma("sync", sem.sST, stage[:, par, cc, :],
                    cc_out[sg * 512:(sg + 1) * 512, cc * 128:(cc + 1) * 128],
                    transpose=True)
            st_tgt = S.c(sem.sST)
            wait("tensor", sem.sST, st_tgt)
            if sg == 0:
                wait("tensor", sem.sST, wkt_tgt)
            wait("tensor", sem.sACT, S.c(sem.sACT))
            wait("tensor", sem.sDVE, S.c(sem.sDVE))
            emit_proj_rms(
                CCn,
                (lambda kc, sg=sg: kfm[:, kc, sg * 512:(sg + 1) * 512]),
                (lambda kc, cc, par=par: stage[:, par, cc, :]),
                kgam2, [], [], [], [])
            k_sg_pe.append(S.c(sem.sPE1))

        wait("sync", sem.sPE1, S.c(sem.sPE1))
        for cc in range(CCn):
            dma("sync", sem.sST, wph[:, cc, :], wvt_d[cc * 128:(cc + 1) * 128, :])
        wvt_tgt = S.c(sem.sST)
        for kt in range(NKT - 1):
            run("gpsimd", lambda e, kt=kt: e.memset(vt_h[:, kt, :, 64:65], 1.0))
        inc("gpsimd", lambda e: e.memset(vt_h[:, NKT - 1, :, 64:65], 1.0),
            sem.sGB)
        vones = S.c(sem.sGB)
        v_sg_pe = []
        for sg in range(NSG):
            par = sg % 2
            if sg >= 2:
                wait("sync", sem.sPE1, v_sg_pe[sg - 2])
            for cc in range(CCn):
                dma("sync", sem.sST, stage[:, par, cc, :],
                    cc_out[sg * 512:(sg + 1) * 512, cc * 128:(cc + 1) * 128],
                    transpose=True)
            st_tgt = S.c(sem.sST)
            wait("tensor", sem.sST, st_tgt)
            if sg == 0:
                wait("tensor", sem.sST, wvt_tgt)
                wait("tensor", sem.sGB, vones)
            for i in range(4):
                kt = sg * 4 + i
                for mh in range(2):
                    u = 2 * i + mh
                    pV = P[4 + u % 2]
                    if sg or i or mh:
                        wait("tensor", sem.sDVE2, S.c(sem.sDVE2) - 8)
                    for cc in range(CCn):
                        fn = (lambda e, cc=cc, i=i, mh=mh, par=par, pV=pV:
                              e.matmul(
                                  pV[:, :],
                                  stage[:, par, cc, i * 128:(i + 1) * 128],
                                  wph[:, cc, mh * 512:(mh + 1) * 512],
                                  start=(cc == 0), stop=(cc == CCn - 1)))
                        if cc == CCn - 1:
                            inc("tensor", fn, sem.sPE1)
                        else:
                            run("tensor", fn)
                    wait("vector", sem.sPE1, S.c(sem.sPE1))
                    for hp in range(8):
                        head = mh * 8 + hp
                        inc("vector", lambda e, kt=kt, head=head, hp=hp, pV=pV:
                            e.tensor_copy(
                                out=vt_h[:, kt, head, 0:64],
                                in_=pV[:, hp * 64:(hp + 1) * 64]), sem.sDVE2)
            v_sg_pe.append(S.c(sem.sPE1))

        for qb in range(NQB):
            wait("sync", sem.sDVE2, S.c(sem.sDVE2))
            wait("sync", sem.sPE1, S.c(sem.sPE1))
            wait("tensor", sem.sDVE2, S.c(sem.sDVE2))
            for cc in range(CCn):
                dma("sync", sem.sST, wph[:, cc, :],
                    wqt_d[cc * 128:(cc + 1) * 128, :])
            for cc in range(CCn):
                dma("sync", sem.sST, stage[:, 0, cc, :],
                    cc_in[qb * 512:(qb + 1) * 512, cc * 128:(cc + 1) * 128],
                    transpose=True)
            qst_tgt = S.c(sem.sST)
            wait("tensor", sem.sST, qst_tgt)
            wait("tensor", sem.sACT, S.c(sem.sACT))
            wait("tensor", sem.sDVE, S.c(sem.sDVE))
            emit_proj_rms(
                CCn,
                (lambda qc: qfm[:, qc, :]),
                (lambda qc, cc: stage[:, 0, cc, :]),
                qgam2, [], [], [], [])
            q_done_dve = S.c(sem.sDVE)

            for pr in range(CCn):
                wait("tensor", sem.sDVE, S.c(sem.sDVE))
                hA, hB = 2 * pr, 2 * pr + 1
                for kt in range(NKT):
                    pn = kt % 2
                    pSA, pSB = P[0 + pn], P[2 + pn]
                    if S.c(sem.sACT) >= 1:
                        wait("tensor", sem.sACT, S.c(sem.sACT) - 1)
                    run("tensor", lambda e, pr=pr, kt=kt, pSA=pSA: e.matmul(
                        pSA[:, :], kfm[0:64, pr, kt * 128:(kt + 1) * 128],
                        qfm[0:64, pr, :], tile_position=(0, 0),
                        start=True, stop=True))
                    inc("tensor", lambda e, pr=pr, kt=kt, pSB=pSB: e.matmul(
                        pSB[:, :], kfm[64:128, pr, kt * 128:(kt + 1) * 128],
                        qfm[64:128, pr, :], tile_position=(64, 0),
                        start=True, stop=True), sem.sPE1)
                    wait("scalar", sem.sPE1, S.c(sem.sPE1))
                    run("scalar", lambda e, pn=pn, pSA=pSA: e.activation(
                        out=Pt[:, 2 * pn, :], in_=pSA[:, :], func=AF.Exp,
                        bias=biascols[:, 2:3]))
                    inc("scalar", lambda e, pn=pn, pSB=pSB: e.activation(
                        out=Pt[:, 2 * pn + 1, :], in_=pSB[:, :], func=AF.Exp,
                        bias=biascols[:, 2:3]), sem.sACT)
                    if kt >= 1:
                        po = (kt - 1) % 2
                        wait("tensor", sem.sACT, S.c(sem.sACT) - 1)
                        run("tensor", lambda e, kt=kt, po=po, hA=hA: e.matmul(
                            P[4][0:65, :], vt_h[:, kt - 1, hA, 0:65],
                            Pt[:, 2 * po, :],
                            start=(kt == 1), stop=False))
                        inc("tensor", lambda e, kt=kt, po=po, hB=hB: e.matmul(
                            P[5][0:65, :], vt_h[:, kt - 1, hB, 0:65],
                            Pt[:, 2 * po + 1, :],
                            start=(kt == 1), stop=False), sem.sPE1)
                po = (NKT - 1) % 2
                wait("tensor", sem.sACT, S.c(sem.sACT))
                run("tensor", lambda e, po=po, hA=hA: e.matmul(
                    P[4][0:65, :], vt_h[:, NKT - 1, hA, 0:65],
                    Pt[:, 2 * po, :], start=False, stop=True))
                inc("tensor", lambda e, po=po, hB=hB: e.matmul(
                    P[5][0:65, :], vt_h[:, NKT - 1, hB, 0:65],
                    Pt[:, 2 * po + 1, :], start=False, stop=True), sem.sPE1)
                pv_done = S.c(sem.sPE1)
                wait("vector", sem.sPE1, pv_done)
                for hb in range(2):
                    pO = P[4 + hb]
                    cstep("vector", lambda e, hb=hb, pO=pO: e.tensor_copy(
                        out=onrm[0:65, hb, :], in_=pO[0:65, :]))
                    inc("vector", lambda e, hb=hb: e.reciprocal(
                        rden[0:1, :], onrm[64:65, hb, :]), sem.sDVE)
                    wait("tensor", sem.sDVE, S.c(sem.sDVE))
                    inc("tensor", lambda e: e.matmul(
                        P[6][0:64, :], onesf[0:1, 0:64], rden[0:1, :],
                        start=True, stop=True), sem.sPE1)
                    wait("vector", sem.sPE1, S.c(sem.sPE1))
                    base = 64 * hb
                    inc("vector", lambda e, hb=hb, pr=pr, base=base:
                        e.tensor_tensor(
                            out_norm[base:base + 64, pr, :],
                            onrm[0:64, hb, :], P[6][0:64, :], OP.mult),
                        sem.sDVE)
            onorm_done = S.c(sem.sDVE)

            wait("sync", sem.sDVE, onorm_done)
            for cc in range(CCn):
                dma("sync", sem.sST, wph[:, cc, :],
                    wot_d[cc * 128:(cc + 1) * 128, :])
            wo_tgt = S.c(sem.sST)
            wait("tensor", sem.sST, wo_tgt)
            wait("tensor", sem.sDVE, onorm_done)
            prev_store = None
            for tt4 in range(4):
                tta = qb * 4 + tt4
                if prev_store is not None:
                    wait("sync", sem.sXN, prev_store)
                rs_t = dma("sync", sem.sXN, res_stage[:, :],
                           resid_d[tta * 128:(tta + 1) * 128, :])
                for mh in range(2):
                    pW = P[6 + (2 * tt4 + mh) % 2]
                    if tt4 or mh:
                        wait("tensor", sem.sDVE2, S.c(sem.sDVE2))
                    for pr in range(CCn):
                        run("tensor", lambda e, pr=pr, tt4=tt4, mh=mh, pW=pW:
                            e.matmul(
                                pW[:, :],
                                out_norm[:, pr, tt4 * 128:(tt4 + 1) * 128],
                                wph[:, pr, mh * 512:(mh + 1) * 512],
                                start=(pr == 0), stop=False))
                    inc("tensor", lambda e, mh=mh, pW=pW: e.matmul(
                        pW[:, :], ones1b[0:1, :],
                        borow[0:1, mh * 512:(mh + 1) * 512],
                        start=False, stop=True), sem.sPE1)
                    wait("vector", sem.sPE1, S.c(sem.sPE1))
                    wait("vector", sem.sXN, rs_t)
                    inc("vector", lambda e, mh=mh, pW=pW: e.tensor_add(
                        res_stage[:, mh * 512:(mh + 1) * 512], pW[:, :],
                        res_stage[:, mh * 512:(mh + 1) * 512]), sem.sDVE2)
                wait("sync", sem.sDVE2, S.c(sem.sDVE2))
                prev_store = dma("sync", sem.sXN,
                                 resid_d[tta * 128:(tta + 1) * 128, :],
                                 res_stage[:, :])

        barrier()
        wait("sync", sem.sXN, S.c(sem.sXN))
        for tt in range(TT):
            dma("sync", sem.sXN, xres[:, tt, :],
                resid_d[tt * 128:(tt + 1) * 128, :])
        wait("tensor", sem.sXN, S.c(sem.sXN))
        wait("vector", sem.sXN, S.c(sem.sXN))
        wait("scalar", sem.sXN, S.c(sem.sXN))
        barrier()

    use_sparse = os.environ.get("K_DENSE", "") != "1"
    ffn_fn = emit_ffn_sparse if use_sparse else emit_ffn
    if "ffn1" in stages:
        ffn_fn(0)
        barrier()
    if "attn" in stages:
        emit_attn()
    if "ffn2" in stages:
        ffn_fn(1)
        barrier()

    wait("sync", sem.sDVE2, S.c(sem.sDVE2))
    wait("sync", sem.sDVE, S.c(sem.sDVE))
    for tt in range(TT):
        dma("sync", sem.sOUT, out_d[tt * 128:(tt + 1) * 128, :], xres[:, tt, :])
    wait("sync", sem.sOUT, S.c(sem.sOUT))

    with nc.Block() as block:
        @block.sync
        def _(eng):
            for fn in S.q["sync"]:
                fn(eng)

        @block.scalar
        def _(eng):
            for fn in S.q["scalar"]:
                fn(eng)

        @block.vector
        def _(eng):
            for fn in S.q["vector"]:
                fn(eng)

        @block.tensor
        def _(eng):
            for fn in S.q["tensor"]:
                fn(eng)

        @block.gpsimd
        def _(eng):
            for fn in S.q["gpsimd"]:
                fn(eng)

    stack.close()
    return nc


_NC_CACHE = {}


def kernel(L0_attn_x, L0_attn_Wq, L0_attn_Wk, L0_attn_Wv, L0_attn_Wo,
           L0_attn_bo, L0_attn_ln_g, L0_attn_ln_b, L0_attn_q_gamma,
           L0_attn_k_gamma, L0_ffn_Wg, L0_ffn_W1, L0_ffn_W3, L0_ffn_W2,
           _stages=("ffn1", "attn", "ffn2")):
    x = np.ascontiguousarray(np.asarray(L0_attn_x, dtype=np.float32))
    B, N, Cm = x.shape
    assert (B, N, Cm) == (4, 4096, 1024)
    weights = {
        "wq": np.ascontiguousarray(L0_attn_Wq, dtype=np.float32),
        "wk": np.ascontiguousarray(L0_attn_Wk, dtype=np.float32),
        "wv": np.ascontiguousarray(L0_attn_Wv, dtype=np.float32),
        "wo": np.ascontiguousarray(L0_attn_Wo, dtype=np.float32),
        "bo": np.ascontiguousarray(np.asarray(L0_attn_bo, np.float32).reshape(1, C)),
        "lng": np.ascontiguousarray(np.asarray(L0_attn_ln_g, np.float32).reshape(1, C)),
        "lnb": np.ascontiguousarray(np.asarray(L0_attn_ln_b, np.float32).reshape(1, C)),
        "qg": np.ascontiguousarray(np.asarray(L0_attn_q_gamma, np.float32).reshape(1, DH)),
        "kg": np.ascontiguousarray(np.asarray(L0_attn_k_gamma, np.float32).reshape(1, DH)),
        "wg": np.ascontiguousarray(L0_ffn_Wg, dtype=np.float32),
        "w1": np.ascontiguousarray(L0_ffn_W1, dtype=np.float32),
        "w3": np.ascontiguousarray(L0_ffn_W3, dtype=np.float32),
        "w2": np.ascontiguousarray(L0_ffn_W2, dtype=np.float32),
    }
    key = tuple(_stages)
    if key not in _NC_CACHE:
        _NC_CACHE[key] = build_program(_stages)
    nc = _NC_CACHE[key]
    in_maps = []
    for core in range(8):
        bi, half = core // 2, core % 2
        shard = np.ascontiguousarray(x[bi, half * T:(half + 1) * T, :])
        m = {"x": shard}
        m.update(weights)
        in_maps.append(m)
    res = run_bass_kernel_spmd(nc, in_maps, core_ids=list(range(8)))
    out = np.empty((B, N, Cm), dtype=np.float32)
    global DBG
    DBG = []
    for core in range(8):
        bi, half = core // 2, core % 2
        out[bi, half * T:(half + 1) * T, :] = res.results[core]["out"]
        if "dbg" in res.results[core]:
            DBG.append(np.asarray(res.results[core]["dbg"]))
    return out


# version 13
# speedup vs baseline: 1.1420x; 1.0133x over previous
import numpy as np
import concourse.bass as bass
import concourse.mybir as mybir
from concourse.bass_utils import run_bass_kernel_spmd

F32 = mybir.dt.float32
BF16 = mybir.dt.bfloat16
U8 = mybir.dt.uint8
AF = mybir.ActivationFunctionType
OP = mybir.AluOpType
AX = mybir.AxisListType.X

T = 2048
C = 1024
H = 4096
E = 8
TT = 16
CCn = 8
HHn = 32
NQB = 4
NSG = 8
NKT = 32
DH = 64
EPS_LN = 1e-5
EPS_RMS = 1e-8

ENGINES = ("sync", "scalar", "vector", "tensor", "gpsimd")


class Sched:
    def __init__(self):
        self.q = {e: [] for e in ENGINES}
        self.cnt = {}

    def emit(self, eng, fn):
        self.q[eng].append(fn)

    def c(self, sem):
        return self.cnt.get(sem.name, 0)

    def bump(self, sem, n=1):
        self.cnt[sem.name] = self.c(sem) + n
        return self.cnt[sem.name]


def build_program(stages=("ffn1", "attn", "ffn2")):
    import os
    init_parts = set(os.environ.get("K_INIT", "consts,gb,wt,moe").split(","))
    nc = bass.Bass("TRN2", target_bir_lowering=False, debug=False)

    din = {}
    din["x"] = nc.dram_tensor("x", [T, C], F32, kind="ExternalInput").ap()
    for nm in ("wq", "wk", "wv", "wo"):
        din[nm] = nc.dram_tensor(nm, [C, C], F32, kind="ExternalInput").ap()
    din["bo"] = nc.dram_tensor("bo", [1, C], F32, kind="ExternalInput").ap()
    din["lng"] = nc.dram_tensor("lng", [1, C], F32, kind="ExternalInput").ap()
    din["lnb"] = nc.dram_tensor("lnb", [1, C], F32, kind="ExternalInput").ap()
    din["qg"] = nc.dram_tensor("qg", [1, DH], F32, kind="ExternalInput").ap()
    din["kg"] = nc.dram_tensor("kg", [1, DH], F32, kind="ExternalInput").ap()
    din["wg"] = nc.dram_tensor("wg", [C, E], F32, kind="ExternalInput").ap()
    din["w1"] = nc.dram_tensor("w1", [E, C, H], F32, kind="ExternalInput").ap()
    din["w3"] = nc.dram_tensor("w3", [E, C, H], F32, kind="ExternalInput").ap()
    din["w2"] = nc.dram_tensor("w2", [E, H, C], F32, kind="ExternalInput").ap()
    out_d = nc.dram_tensor("out", [T, C], F32, kind="ExternalOutput").ap()
    import os as _osd
    dbg_what = _osd.environ.get("K_DBG", "")
    dbg_d = nc.dram_tensor("dbg", [2048, 64], F32, kind="ExternalOutput").ap()

    w1c = nc.dram_tensor("w1c", [E, HHn, 128, CCn, 128], BF16).ap()
    w3c = nc.dram_tensor("w3c", [E, HHn, 128, CCn, 128], BF16).ap()
    w2c = nc.dram_tensor("w2c", [E, HHn, 128, C], BF16).ap()
    wqt_d = nc.dram_tensor("wqt_d", [C, C], BF16).ap()
    wkt_d = nc.dram_tensor("wkt_d", [C, C], BF16).ap()
    wvt_d = nc.dram_tensor("wvt_d", [C, C], BF16).ap()
    wot_d = nc.dram_tensor("wot_d", [C, C], BF16).ap()
    xnt_d = nc.dram_tensor("xnt_d", [T, C], BF16).ap()
    cc_in = nc.dram_tensor("cc_in", [T, C], BF16).ap()
    cc_out = nc.dram_tensor("cc_out", [2 * T, C], BF16).ap()
    resid_d = nc.dram_tensor("resid_d", [T, C], F32).ap()
    CAP = 896
    NT = CAP // 128
    NSL = E * CAP
    glist_d = nc.dram_tensor("glist_d", [NSL, 1], mybir.dt.int32).ap()
    xg_d = nc.dram_tensor("xg_d", [2, CAP, C], BF16).ap()
    eo_d = nc.dram_tensor("eo_d", [NSL, C], BF16).ap()

    arenaA = nc.alloc_sbuf_tensor("arenaA", [128, 65536], U8)
    arenaB = nc.alloc_sbuf_tensor("arenaB", [128, 67584], U8)
    arenaC = nc.alloc_sbuf_tensor("arenaC", [128, 68096], U8)

    def carve(arena, off, nbytes, dt, shape):
        ap = arena.ap()[:, off:off + nbytes].bitcast(dt)
        if len(shape) == 1:
            assert ap.shape[1] == shape[0], (ap.shape, shape)
            return ap
        names = " ".join(f"d{i}" for i in range(len(shape)))
        return ap.rearrange(f"p ({names}) -> p {names}",
                            **{f"d{i}": s for i, s in enumerate(shape)})

    xres = carve(arenaA, 0, 65536, F32, (TT, C))
    kfm = carve(arenaA, 0, 65536, BF16, (CCn, 4096))

    xnfm = carve(arenaB, 0, 32768, BF16, (CCn, T))
    hid = carve(arenaB, 32768, 32768, BF16, (HHn, 512))
    vt = carve(arenaB, 0, 66560, BF16, (NKT, 1040))
    vt_h = vt.rearrange("p t (h w) -> p t h w", w=65)

    w1t = carve(arenaC, 51200, 8192, BF16, (4, CCn, 128))
    w3t = carve(arenaC, 59392, 8192, BF16, (4, CCn, 128))
    w2t = carve(arenaC, 8192, 4096, BF16, (2, C))
    sl = carve(arenaC, 12288, 2048, BF16, (2, 512))
    sq = carve(arenaC, 14336, 8192, F32, (2, C))
    xnt_st = carve(arenaC, 22528, 4096, BF16, (2, C))
    wstripe = carve(arenaC, 26624, 8192, BF16, (2, 2048))
    g_bc = carve(arenaC, 43008, 4096, F32, (C,))
    b_bc = carve(arenaC, 47104, 4096, F32, (C,))
    stage = carve(arenaC, 0, 16384, BF16, (2, CCn, 512))
    wph = carve(arenaC, 16384, 16384, BF16, (CCn, C))
    qfm = carve(arenaC, 32768, 8192, BF16, (CCn, 512))
    Pt = carve(arenaC, 40960, 4096, BF16, (4, 512))
    out_norm = carve(arenaC, 45056, 8192, BF16, (CCn, 512))
    onrm = carve(arenaC, 53248, 4096, F32, (2, 512))
    ksq = carve(arenaC, 57344, 2048, BF16, (2, 512))
    rkt = carve(arenaC, 59392, 4096, F32, (2, 512))
    res_stage = carve(arenaC, 63488, 4096, F32, (C,))
    mxbuf = carve(arenaC, 67584, 256, F32, (2, 32))
    rden = carve(arenaC, 40960, 2048, F32, (512,))[0:1, :]
    grow = carve(arenaC, 14336, 4096, F32, (C,))[0:1, :]
    brow = carve(arenaC, 18432, 4096, F32, (C,))[0:1, :]
    biascols = carve(arenaC, 67840, 16, F32, (4,))
    I32 = mybir.dt.int32
    xgstage = carve(arenaC, 0, 4096, BF16, (2, C))
    glii = carve(arenaC, 4096, 256, I32, (64,))
    srt = carve(arenaC, 4352, 128, F32, (16, 2))
    sint = carve(arenaC, 4480, 128, I32, (16, 2))
    sint2 = carve(arenaC, 4608, 128, I32, (16, 2))
    tokid16 = None
    SRr = carve(arenaC, 4800, 512, F32, (128,))
    msumb = carve(arenaC, 5312, 256, BF16, (128,))
    rsc_f = carve(arenaC, 5568, 64, F32, (16,))
    rrow_f = carve(arenaC, 5632, 512, F32, (128,))[0:1, :]
    ecapcol = None
    msk1b = carve(arenaC, 6208, 256, BF16, (16, E))
    msk2b = carve(arenaC, 6464, 256, BF16, (16, E))
    wts = carve(arenaC, 6720, 128, F32, (16, 2))
    usub = None
    u16f = None
    ubAi = carve(arenaC, 26624, 512, I32, (128,))
    ubBi = carve(arenaC, 27136, 512, I32, (128,))
    ubCi = carve(arenaC, 27648, 512, I32, (128,))
    ubDi = carve(arenaC, 28160, 512, I32, (128,))
    tri_a = carve(arenaC, 14976, 64, I32, (16,))
    tri_b = carve(arenaC, 15040, 64, I32, (16,))
    hidg = carve(arenaB, 0, 57344, BF16, (HHn, CAP))
    xg_fm = carve(arenaC, 34816, 14336, BF16, (CCn, CAP))
    sl2 = carve(arenaC, 18432, 3584, BF16, (2, CAP))
    eostage = carve(arenaC, 14336, 4096, BF16, (2, C))
    tri_c = carve(arenaC, 15104, 64, I32, (16,))

    stats = nc.alloc_sbuf_tensor("stats", [128, 16, 8], F32).ap()
    cmat = nc.alloc_sbuf_tensor("cmat", [128, 16, E], F32).ap()
    wgs = nc.alloc_sbuf_tensor("wgs", [128, CCn, E], BF16).ap()
    gsc = nc.alloc_sbuf_tensor("gsc", [128, 3, E], F32).ap()
    gtmp = nc.alloc_sbuf_tensor("gtmp", [128, 8], F32).ap()
    ident_f = nc.alloc_sbuf_tensor("ident_f", [128, 128], F32).ap()
    ident_b = nc.alloc_sbuf_tensor("ident_b", [128, 128], BF16).ap()
    onesf = nc.alloc_sbuf_tensor("onesf", [1, 128], F32).ap()
    ones1b = nc.alloc_sbuf_tensor("ones1b", [1, 128], BF16).ap()
    ones2h = nc.alloc_sbuf_tensor("ones2h", [128, 2], BF16).ap()
    o2tmp = nc.alloc_sbuf_tensor("o2tmp", [128, 2], F32).ap()
    qgam2 = nc.alloc_sbuf_tensor("qgam2", [2, 128], F32).ap()
    kgam2 = nc.alloc_sbuf_tensor("kgam2", [2, 128], F32).ap()
    negmT = nc.alloc_sbuf_tensor("negmT", [1, 4], F32).ap()
    borow = nc.alloc_sbuf_tensor("borow", [1, C], BF16).ap()
    ceps = nc.alloc_sbuf_tensor("ceps", [128, 1], F32).ap()
    onescol_b = nc.alloc_sbuf_tensor("onescol_b", [128, 1], BF16).ap()
    usub = nc.alloc_sbuf_tensor("usub", [128, 128], BF16).ap()
    u16f = nc.alloc_sbuf_tensor("u16f", [128, 128], F32).ap()
    ecapcol = nc.alloc_sbuf_tensor("ecapcol", [128, 1], F32).ap()
    tokid16 = nc.alloc_sbuf_tensor("tokid16", [128, 16], mybir.dt.int32).ap()
    ecrow = nc.alloc_sbuf_tensor("ecrow", [1, 128], F32).ap()
    wtmp = nc.alloc_sbuf_tensor("wtmp", [128, 2, 128], BF16).ap()
    wsto = nc.alloc_sbuf_tensor("wsto", [128, 2, 128], BF16).ap()
    assert nc.sbuf_bytes_remaining >= 0, nc.sbuf_bytes_remaining

    P = [nc.alloc_psum_tensor(f"p{i}", [128, 512], F32).ap() for i in range(8)]
    p_bf = P[7].bitcast(BF16)

    S = Sched()
    import contextlib
    stack = contextlib.ExitStack()
    sems = {}
    for nm in ("sIN", "sW1", "sW3", "sW2", "sST", "sXN", "sSP", "sCC", "sWC",
               "sWS", "sOUT", "sPE1", "sPEh", "sPEeo", "sPEx", "sACT", "sDVE",
               "sDVE2", "sBAR", "sGB", "sVC", "sSC", "sGC", "sEOD"):
        sems[nm] = stack.enter_context(nc.semaphore(nm))
    sem = type("X", (), sems)

    def emit(eng, fn):
        S.emit(eng, fn)

    def wait(eng, s, val):
        if val > 0:
            emit(eng, lambda e, s=s, v=val: e.wait_ge(s, v))

    def dma(eng, s, out, in_, **kw):
        tgt = S.bump(s, 16)
        emit(eng, lambda e, o=out, i=in_, s=s, kw=kw:
             e.dma_start(out=o, in_=i, **kw).then_inc(s, 16))
        return tgt

    from concourse.bass_core_rust import IndirectOffsetOnAxis

    def idma(eng, s, out, in_, out_off=None, in_off=None):
        tgt = S.bump(s, 16)
        emit(eng, lambda e, o=out, i=in_, s=s, oo=out_off, io=in_off:
             e.indirect_dma_start(
                 o, IndirectOffsetOnAxis(oo, 0) if oo is not None else None,
                 i, IndirectOffsetOnAxis(io, 0) if io is not None else None
             ).then_inc(s, 16))
        return tgt

    def inc(eng, instr_fn, s, n=1):
        tgt = S.bump(s, n)
        emit(eng, lambda e, f=instr_fn, s=s, n=n: f(e).then_inc(s, n))
        return tgt

    def run(eng, instr_fn):
        emit(eng, lambda e, f=instr_fn: f(e))

    chain_sems = {}

    def cstep(eng, instr_fn):
        sc = chain_sems[eng]
        tgt = inc(eng, instr_fn, sc)
        wait(eng, sc, tgt)

    chain_sems.update({"vector": sem.sVC, "scalar": sem.sSC,
                       "gpsimd": sem.sGC, "tensor": sem.sPEx})

    nbar = [0]

    def barrier():
        nbar[0] += 1
        tgt = nbar[0] * len(ENGINES)
        S.bump(sem.sBAR, len(ENGINES))
        for eng in ENGINES:
            emit(eng, lambda e: e.sem_inc(sem.sBAR, 1))
            wait(eng, sem.sBAR, tgt)

    for tt in range(TT):
        dma("sync", sem.sIN, xres[:, tt, :], din["x"][tt * 128:(tt + 1) * 128, :])
    x_loaded = S.c(sem.sIN)

    cstep("gpsimd", lambda e: e.iota(
        ident_f[:, :], [[1, 128]], channel_multiplier=-1,
        allow_small_or_imprecise_dtypes=True))
    cstep("gpsimd", lambda e: e.tensor_scalar(
        ident_f[:, :], ident_f[:, :], 0.0, None, OP.is_equal))
    run("gpsimd", lambda e: e.memset(onesf[:, :], 1.0))
    run("gpsimd", lambda e: e.memset(ones1b[:, :], 1.0))
    run("gpsimd", lambda e: e.memset(qgam2[0:2, :], 0.0))
    run("gpsimd", lambda e: e.memset(kgam2[0:2, :], 0.0))
    run("gpsimd", lambda e: e.memset(ceps[:, :], EPS_LN))
    cstep("gpsimd", lambda e: e.iota(
        o2tmp[:, 0:1], [[1, 1]], channel_multiplier=1,
        allow_small_or_imprecise_dtypes=True))
    cstep("gpsimd", lambda e: e.tensor_scalar(
        o2tmp[:, 1:2], o2tmp[:, 0:1], 63.5, None, OP.is_gt))
    inc("gpsimd", lambda e: e.tensor_scalar(
        o2tmp[:, 0:1], o2tmp[:, 1:2], -1.0, 1.0, OP.mult, OP.add), sem.sGB)
    wait("scalar", sem.sGB, 1)
    inc("scalar", lambda e: e.copy(out=ident_b[:, :], in_=ident_f[:, :]), sem.sACT)
    inc("scalar", lambda e: e.copy(out=ones2h[:, :], in_=o2tmp[:, :]), sem.sACT)
    consts_cast = S.c(sem.sACT)

    def _emit_const_loads():
        dma("gpsimd", sem.sIN, qgam2[0:1, 0:64], din["qg"][0:1, :])
        dma("gpsimd", sem.sIN, qgam2[1:2, 64:128], din["qg"][0:1, :])
        dma("gpsimd", sem.sIN, kgam2[0:1, 0:64], din["kg"][0:1, :])
        dma("gpsimd", sem.sIN, kgam2[1:2, 64:128], din["kg"][0:1, :])
        dma("gpsimd", sem.sIN, grow[:, :], din["lng"][:, :])
        dma("gpsimd", sem.sIN, brow[:, :], din["lnb"][:, :])
        dma("gpsimd", sem.sIN, borow[:, :], din["bo"][:, :])
        tgt = dma("gpsimd", sem.sIN, wgs[:, :, :],
                  din["wg"].rearrange("(cc p) e -> p cc e", p=128))
        wait("vector", sem.sIN, tgt)
        inc("vector", lambda e: e.tensor_scalar_mul(
            qgam2[0:2, :], qgam2[0:2, :], 0.125), sem.sDVE)
        return tgt

    def _emit_gb_bcast():
        dma("gpsimd", sem.sIN, grow[:, :], din["lng"][:, :])
        gb_tgt = dma("gpsimd", sem.sIN, brow[:, :], din["lnb"][:, :])
        wait("tensor", sem.sIN, gb_tgt)
        wait("tensor", sem.sACT, max(consts_cast, S.c(sem.sACT)))
        wait("tensor", sem.sDVE, S.c(sem.sDVE))
        wait("tensor", sem.sDVE2, S.c(sem.sDVE2))
        for half in range(2):
            if half == 1:
                wait("tensor", sem.sDVE, S.c(sem.sDVE))
            run("tensor", lambda e, h=half: e.matmul(
                P[6][:, :], onesf[0:1, :], grow[0:1, h * 512:(h + 1) * 512],
                start=True, stop=True))
            inc("tensor", lambda e, h=half: e.matmul(
                P[5][:, :], onesf[0:1, :], brow[0:1, h * 512:(h + 1) * 512],
                start=True, stop=True), sem.sPEx)
            wait("vector", sem.sPEx, S.c(sem.sPEx))
            run("vector", lambda e, h=half: e.tensor_copy(
                out=g_bc[:, h * 512:(h + 1) * 512], in_=P[6][:, :]))
            inc("vector", lambda e, h=half: e.tensor_copy(
                out=b_bc[:, h * 512:(h + 1) * 512], in_=P[5][:, :]), sem.sDVE)

    def _emit_wt_transposes():
        import os as _os
        wtn = int(_os.environ.get("K_WTN", "256"))
        wt_skip_store = _os.environ.get("K_WT_NOSTORE", "") == "1"
        wt_skip_dve = _os.environ.get("K_WT_NODVE", "") == "1"
        wt_skip_tr = _os.environ.get("K_WT_NOTRANS", "") == "1"
        wait("tensor", sem.sACT, consts_cast)
        store_tgts = []
        n_tr = 0
        for nm, dst in (("wq", wqt_d), ("wk", wkt_d), ("wv", wvt_d),
                        ("wo", wot_d)):
            for mm in range(8):
                for cc in range(8):
                    n = n_tr
                    if n >= wtn:
                        continue
                    par = n % 2
                    if n >= 2:
                        wait("gpsimd", sem.sPEx, S.c(sem.sPEx) - 1)
                    lt = dma("gpsimd", sem.sWC, wtmp[:, par, :],
                             din[nm][mm * 128:(mm + 1) * 128,
                                     cc * 128:(cc + 1) * 128])
                    wait("tensor", sem.sWC, lt)
                    if n >= 1:
                        wait("tensor", sem.sDVE2, S.c(sem.sDVE2))
                    if wt_skip_tr:
                        inc("tensor", lambda e: e.engine_nop()
                            if False else e.nop(), sem.sPEx)
                    else:
                        inc("tensor", lambda e, par=par: e.transpose(
                            p_bf[:, par * 128:par * 128 + 128], wtmp[:, par, :],
                            ident_b[:, :]), sem.sPEx)
                    wait("vector", sem.sPEx, S.c(sem.sPEx))
                    if n >= 2:
                        wait("vector", sem.sWS, store_tgts[n - 2])
                    if wt_skip_dve or wt_skip_tr:
                        inc("vector", lambda e, par=par: e.tensor_copy(
                            out=wsto[:, par, :], in_=wtmp[:, par, :]),
                            sem.sDVE2)
                    else:
                        inc("vector", lambda e, par=par: e.tensor_copy(
                            out=wsto[:, par, :],
                            in_=p_bf[:, par * 128:par * 128 + 128]), sem.sDVE2)
                    wait("sync", sem.sDVE2, S.c(sem.sDVE2))
                    if wt_skip_store:
                        st = S.bump(sem.sWS, 16)
                        emit("sync", lambda e: e.sem_inc(sem.sWS, 16))
                    else:
                        st = dma("sync", sem.sWS,
                                 dst[cc * 128:(cc + 1) * 128,
                                     mm * 128:(mm + 1) * 128],
                                 wsto[:, par, :])
                    store_tgts.append(st)
                    n_tr += 1

    relay_tgt = {}

    def _emit_relayout_expert(eidx, parts=("w13", "w2")):
        units = []
        if "w13" in parts:
            for cc in range(CCn):
                units += [("w1", eidx, cc, 0), ("w1", eidx, cc, 1),
                          ("w3", eidx, cc, 0), ("w3", eidx, cc, 1)]
        if "w2" in parts:
            units += [("w2", eidx, hh, 0) for hh in range(HHn)]
        _emit_relayout_units(units)

    rl_n = [0]

    def _emit_relayout_units(units):
        for (kind, eidx, i, hf) in units:
            par = rl_n[0] % 2
            rl_n[0] += 1
            wait("gpsimd", sem.sWS, S.c(sem.sWS) - 16)
            if kind in ("w1", "w3"):
                src = din[kind]
                dstc = w1c if kind == "w1" else w3c
                lt = dma("gpsimd", sem.sWC, wstripe[:, par, :],
                         src[eidx, i * 128:(i + 1) * 128,
                             hf * 2048:(hf + 1) * 2048])
                wait("gpsimd", sem.sWC, lt)
                st = dma("gpsimd", sem.sWS,
                         dstc[eidx, hf * 16:(hf + 1) * 16, :, i, :]
                         .transpose([1, 0, 2]),
                         wstripe[:, par, :].rearrange("p (hh w) -> p hh w",
                                                      w=128))
                relay_tgt[(kind, eidx, i, hf)] = st
                relay_tgt[(kind, eidx, i)] = st
            else:
                lt = dma("gpsimd", sem.sWC, wstripe[:, par, 0:1024],
                         din["w2"][eidx, i * 128:(i + 1) * 128, :])
                wait("gpsimd", sem.sWC, lt)
                st = dma("gpsimd", sem.sWS, w2c[eidx, i, :, :],
                         wstripe[:, par, 0:1024])
                relay_tgt[("w2", eidx, i)] = st

    def _emit_moe_cache():
        st_list = []
        units = []
        for eidx in range(E):
            for cc in range(CCn):
                units.append(("w1", eidx, cc))
                units.append(("w3", eidx, cc))
            for hh in range(HHn):
                units.append(("w2", eidx, hh))
        for u, (kind, eidx, i) in enumerate(units):
            par = u % 2
            if u >= 2:
                wait("gpsimd", sem.sWS, st_list[u - 2])
            if kind == "w1":
                lt = dma("gpsimd", sem.sWC, wstripe[:, par, :],
                         din["w1"][eidx, i * 128:(i + 1) * 128, :])
                wait("gpsimd", sem.sWC, lt)
                st = dma("gpsimd", sem.sWS,
                         w1c[eidx, :, :, i, :].transpose([1, 0, 2]),
                         wstripe[:, par, :].rearrange("p (hh w) -> p hh w",
                                                      w=128))
            elif kind == "w3":
                lt = dma("gpsimd", sem.sWC, wstripe[:, par, :],
                         din["w3"][eidx, i * 128:(i + 1) * 128, :])
                wait("gpsimd", sem.sWC, lt)
                st = dma("gpsimd", sem.sWS,
                         w3c[eidx, :, :, i, :].transpose([1, 0, 2]),
                         wstripe[:, par, :].rearrange("p (hh w) -> p hh w",
                                                      w=128))
            else:
                lt = dma("gpsimd", sem.sWC, wstripe[:, par, 0:1024],
                         din["w2"][eidx, i * 128:(i + 1) * 128, :])
                wait("gpsimd", sem.sWC, lt)
                st = dma("gpsimd", sem.sWS, w2c[eidx, i, :, :],
                         wstripe[:, par, 0:1024])
            st_list.append(st)
            relay_tgt[(kind, eidx, i)] = st

    def _emit_route_consts():
        wait("gpsimd", sem.sGB, 1)
        cstep("gpsimd", lambda e: e.iota(ubBi[:, :], [[1, 128]],
              channel_multiplier=-1,
              allow_small_or_imprecise_dtypes=True))
        inc("gpsimd", lambda e: e.iota(ubDi[:, 0:1], [[1, 1]],
            channel_multiplier=1), sem.sGB)
        iotas_done = S.c(sem.sGB)
        wait("vector", sem.sGB, iotas_done)
        rfull = carve(arenaC, 5632, 512, F32, (128,))
        cstep("vector", lambda e: e.tensor_copy(out=rfull[:, :],
                                                in_=ubBi[:, :]))
        cstep("vector", lambda e: e.tensor_scalar(
            u16f[:, :], rfull[:, :], 0.0, None, OP.is_gt))
        cstep("vector", lambda e: e.tensor_copy(out=usub[:, :],
                                                in_=u16f[:, :]))
        cstep("vector", lambda e: e.tensor_scalar(
            ubCi[:, :], ubBi[:, :], 7, None, OP.bitwise_and))
        cstep("vector", lambda e: e.tensor_copy(out=rfull[:, :],
                                                in_=ubCi[:, :]))
        cstep("vector", lambda e: e.tensor_scalar(
            rfull[:, :], rfull[:, :], 0.0, None, OP.is_equal))
        cstep("vector", lambda e: e.tensor_mul(
            u16f[:, :], u16f[:, :], rfull[:, :]))
        cstep("vector", lambda e: e.tensor_scalar(
            ubDi[:, 1:2], ubDi[:, 0:1], 7, None, OP.bitwise_and))
        cstep("vector", lambda e: e.tensor_scalar(
            ubDi[:, 2:3], ubDi[:, 1:2], 9, None, OP.arith_shift_left))
        cstep("vector", lambda e: e.tensor_scalar(
            ubDi[:, 3:4], ubDi[:, 1:2], 8, None, OP.arith_shift_left))
        cstep("vector", lambda e: e.tensor_tensor(
            ubDi[:, 2:3], ubDi[:, 2:3], ubDi[:, 3:4], OP.add))
        cstep("vector", lambda e: e.tensor_scalar(
            ubDi[:, 3:4], ubDi[:, 1:2], 7, None, OP.arith_shift_left))
        cstep("vector", lambda e: e.tensor_tensor(
            ubDi[:, 2:3], ubDi[:, 2:3], ubDi[:, 3:4], OP.add))
        inc("vector", lambda e: e.tensor_copy(
            out=ecapcol[:, 0:1], in_=ubDi[:, 2:3]), sem.sDVE)
        wait("tensor", sem.sDVE, S.c(sem.sDVE))
        inc("tensor", lambda e: e.transpose(
            P[6][0:1, 0:128], ecapcol[:, 0:1], ident_f[:, :]), sem.sPE1)
        wait("vector", sem.sPE1, S.c(sem.sPE1))
        cstep("vector", lambda e: e.tensor_copy(
            out=ecrow[0:1, :], in_=P[6][0:1, 0:128]))
        inc("vector", lambda e: e.tensor_scalar_add(
            ecrow[0:1, :], ecrow[0:1, :], float(CAP - 1)), sem.sDVE)
        rc = S.c(sem.sDVE)
        cstep("gpsimd", lambda e: e.iota(
            tokid16[:, :], [[128, 16]], channel_multiplier=1))
        run("gpsimd", lambda e: e.memset(onescol_b[:, :], 1.0))
        cstep("gpsimd", lambda e: e.memset(glii[:, :], 0))
        dma("gpsimd", sem.sGB, glist_d.rearrange("(p j) one -> p (j one)", p=128),
            glii[:, 0:56])
        return rc

    def _emit_softmax_bound():
        wait("vector", sem.sIN, gam_loaded[0])
        wait("vector", sem.sDVE, S.c(sem.sDVE))
        cstep("vector", lambda e: e.reduce_max(
            negmT[0:1, 0:1], qgam2[0:1, 0:64], axis=AX,
            apply_absolute_value=True))
        cstep("vector", lambda e: e.reduce_max(
            negmT[0:1, 1:2], kgam2[0:1, 0:64], axis=AX,
            apply_absolute_value=True))
        cstep("vector", lambda e: e.tensor_mul(
            negmT[0:1, 2:3], negmT[0:1, 0:1], negmT[0:1, 1:2]))
        inc("vector", lambda e: e.tensor_scalar_mul(
            negmT[0:1, 3:4], negmT[0:1, 2:3], -64.0), sem.sDVE)
        wait("tensor", sem.sDVE, S.c(sem.sDVE))
        wait("tensor", sem.sPEx, S.c(sem.sPEx))
        inc("tensor", lambda e: e.matmul(
            P[2][:, 0:1], onesf[0:1, :], negmT[0:1, 3:4],
            start=True, stop=True), sem.sPE1)
        wait("vector", sem.sPE1, S.c(sem.sPE1))
        inc("vector", lambda e: e.tensor_copy(
            out=biascols[:, 2:3], in_=P[2][:, 0:1]), sem.sDVE)

    gam_loaded = [0]
    if "consts" in init_parts:
        gam_loaded[0] = _emit_const_loads()
    else:
        gam_loaded[0] = dma("gpsimd", sem.sIN, wgs[:, :, :],
                            din["wg"].rearrange("(cc p) e -> p cc e", p=128))
    if "gb" in init_parts:
        _emit_gb_bcast()
    if "wt" in init_parts:
        _emit_wt_transposes()
    _emit_softmax_bound()
    rc_done = _emit_route_consts()

    wait("scalar", sem.sIN, x_loaded)
    wait("vector", sem.sIN, x_loaded)
    barrier()
    if "moe" in init_parts and os.environ.get("K_DENSE", "") == "1":
        _emit_moe_cache()
    wc_done = S.c(sem.sWS)

    def emit_ln(is_attn):
        dst_d = cc_in if is_attn else xnt_d
        if is_attn:
            wait("scalar", sem.sPEx, S.c(sem.sPEx))
            wait("vector", sem.sPEx, S.c(sem.sPEx))
        ssq_base = S.c(sem.sDVE)
        act_base = S.c(sem.sACT)
        for tt in range(TT):
            par = tt % 2
            if tt >= 2:
                wait("scalar", sem.sDVE, ssq_base + tt - 1)
            inc("scalar", lambda e, tt=tt, par=par: e.square(
                out=sq[:, par, :], in_=xres[:, tt, :]), sem.sACT)
            run("vector", lambda e, tt=tt: e.reduce_sum(
                stats[:, tt, 0:1], xres[:, tt, :], axis=AX))
            wait("vector", sem.sACT, act_base + tt + 1)
            inc("vector", lambda e, tt=tt, par=par: e.reduce_sum(
                stats[:, tt, 1:2], sq[:, par, :], axis=AX), sem.sDVE)
            S.bump(sem.sDVE, 0)
        wait("vector", sem.sDVE, S.c(sem.sDVE))
        cstep("vector", lambda e: e.tensor_scalar_mul(
            stats[:, :, 2:3], stats[:, :, 0:1], 1.0 / C))
        cstep("vector", lambda e: e.tensor_mul(
            stats[:, :, 6:7], stats[:, :, 2:3], stats[:, :, 2:3]))
        inc("vector", lambda e: e.scalar_tensor_tensor(
            stats[:, :, 3:4], stats[:, :, 1:2], 1.0 / C, stats[:, :, 6:7],
            OP.mult, OP.subtract), sem.sDVE)
        wait("scalar", sem.sDVE, S.c(sem.sDVE))
        inc("scalar", lambda e: e.activation(
            out=stats[:, :, 4:5], in_=stats[:, :, 3:4], func=AF.Sqrt,
            bias=ceps[:, 0:1]), sem.sACT)
        wait("vector", sem.sACT, S.c(sem.sACT))
        cstep("vector", lambda e: e.reciprocal(stats[:, :, 5:6], stats[:, :, 4:5]))
        if is_attn:
            cstep("vector", lambda e: e.tensor_mul(
                stats[:, :, 6:7], stats[:, :, 5:6], stats[:, :, 5:6]))
            inc("vector", lambda e: e.tensor_mul(
                stats[:, :, 6:7], stats[:, :, 3:4], stats[:, :, 6:7]), sem.sDVE)
            wait("scalar", sem.sDVE, S.c(sem.sDVE))
            inc("scalar", lambda e: e.activation(
                out=stats[:, :, 4:5], in_=stats[:, :, 6:7], func=AF.Sqrt,
                bias=ceps[:, 0:1]), sem.sACT)
            wait("vector", sem.sACT, S.c(sem.sACT))
            cstep("vector", lambda e: e.reciprocal(stats[:, :, 6:7], stats[:, :, 4:5]))
            cstep("vector", lambda e: e.tensor_mul(
                stats[:, :, 5:6], stats[:, :, 5:6], stats[:, :, 6:7]))
        st_tgts = []
        for tt in range(TT):
            par = tt % 2
            if tt >= 2:
                wait("vector", sem.sXN, st_tgts[tt - 2])
            if is_attn:
                cstep("vector", lambda e, tt=tt, par=par: e.tensor_scalar(
                    sq[:, par, :], xres[:, tt, :], stats[:, tt, 2:3],
                    stats[:, tt, 5:6], OP.subtract, OP.mult))
                cstep("vector", lambda e, par=par: e.tensor_mul(
                    sq[:, par, :], sq[:, par, :], g_bc[:, :]))
                inc("vector", lambda e, par=par: e.tensor_add(
                    xnt_st[:, par, :], sq[:, par, :], b_bc[:, :]), sem.sDVE)
            else:
                inc("vector", lambda e, tt=tt, par=par: e.tensor_scalar(
                    xnt_st[:, par, :], xres[:, tt, :], stats[:, tt, 2:3],
                    stats[:, tt, 5:6], OP.subtract, OP.mult), sem.sDVE)
            wait("sync", sem.sDVE, S.c(sem.sDVE))
            st_tgts.append(dma("sync", sem.sXN,
                               dst_d[tt * 128:(tt + 1) * 128, :],
                               xnt_st[:, par, :]))
        return S.c(sem.sXN)

    def emit_gating():
        for tt in range(TT):
            for cc in range(CCn):
                fn = (lambda e, tt=tt, cc=cc: e.matmul(
                    P[0][:, tt * E:(tt + 1) * E],
                    xnfm[:, cc, tt * 128:(tt + 1) * 128],
                    wgs[:, cc, :], start=(cc == 0), stop=(cc == CCn - 1)))
                if tt == TT - 1 and cc == CCn - 1:
                    inc("tensor", fn, sem.sPE1)
                else:
                    run("tensor", fn)
        wait("vector", sem.sPE1, S.c(sem.sPE1))
        sqf = sq.rearrange("p a b -> p (a b)")
        lg = sqf[:, 0:128].rearrange("p (a b) -> p a b", b=8)
        lg2 = sqf[:, 128:256].rearrange("p (a b) -> p a b", b=8)
        m1 = sqf[:, 256:272]
        m2 = sqf[:, 272:288]
        dm = sqf[:, 288:304]
        sg = sqf[:, 304:320]
        m1b = sqf[:, 256:272].unsqueeze(2).broadcast_to([128, 16, 8])
        m2b = sqf[:, 272:288].unsqueeze(2).broadcast_to([128, 16, 8])
        msk1v = msk1b[:, :, :]
        msk2v = msk2b[:, :, :]
        cstep("vector", lambda e: e.tensor_copy(
            out=sqf[:, 0:128], in_=P[0][:, 0:128]))
        cstep("vector", lambda e: e.tensor_reduce(
            m1, lg, axis=AX, op=OP.max))
        cstep("vector", lambda e: e.tensor_tensor(
            msk1v, lg, m1b, OP.is_equal))
        cstep("vector", lambda e: e.scalar_tensor_tensor(
            lg2, msk1v, -1e30, lg, OP.mult, OP.add))
        cstep("vector", lambda e: e.tensor_reduce(
            m2, lg2, axis=AX, op=OP.max))
        cstep("vector", lambda e: e.tensor_tensor(
            msk2v, lg2, m2b, OP.is_equal))
        inc("vector", lambda e: e.tensor_tensor(
            dm, m1, m2, OP.subtract), sem.sDVE)
        wait("scalar", sem.sDVE, S.c(sem.sDVE))
        inc("scalar", lambda e: e.activation(
            out=sg, in_=dm, func=AF.Sigmoid), sem.sACT)
        wait("vector", sem.sACT, S.c(sem.sACT))
        cstep("vector", lambda e: e.tensor_scalar_mul(
            wts[:, :, 0:1].rearrange("p a b -> p (a b)"), sg, 0.5))
        inc("vector", lambda e: e.tensor_scalar(
            wts[:, :, 1:2].rearrange("p a b -> p (a b)"),
            wts[:, :, 0:1].rearrange("p a b -> p (a b)"),
            -1.0, 0.5, OP.mult, OP.add), sem.sDVE)

    import os as _os2
    ffn_parts = set(_os2.environ.get("K_FFN", "ln,gate,exp").split(","))
    n_exp = int(_os2.environ.get("K_NEXP", "8"))
    exp_parts = set(_os2.environ.get("K_EXP", "h,eo,evac").split(","))

    def emit_ffn(call):
        xn_tgt = emit_ln(is_attn=False)
        wait("sync", sem.sXN, xn_tgt)
        for cc in range(CCn):
            dma("sync", sem.sXN, xnfm[:, cc, :],
                xnt_d[:, cc * 128:(cc + 1) * 128], transpose=True)
        xnfm_tgt = S.c(sem.sXN)
        wait("tensor", sem.sXN, xnfm_tgt)
        wait("tensor", sem.sDVE2, S.c(sem.sDVE2))
        wait("tensor", sem.sACT, S.c(sem.sACT))
        if "gate" in ffn_parts:
            emit_gating()
        if "exp" not in ffn_parts:
            return
        if dbg_what in ("cmat", "logits") and call == 0:
            wait("gpsimd", sem.sDVE, S.c(sem.sDVE))
            dma("gpsimd", sem.sOUT,
                dbg_d[:, 0:8].rearrange("(tt p) e -> p tt e", p=128),
                cmat[:, :, :])
        if dbg_what == "xn" and call == 0:
            wait("gpsimd", sem.sXN, S.c(sem.sXN))
            dma("gpsimd", sem.sOUT, dbg_d, xnt_d[:, 0:64])

        if dbg_what in ("w1c", "w2c") and call == 0:
            src = (w1c if dbg_what == "w1c" else w2c.rearrange(
                "e hh p (cc w) -> e hh p cc w", w=128))
            wait("gpsimd", sem.sWS, wc_done)
            dma("gpsimd", sem.sOUT,
                dbg_d.rearrange("(p a) b -> p (a b)", p=128),
                src[0, 0, :, :, :].rearrange("p cc w -> p (cc w)"))
        wait("tensor", sem.sDVE, S.c(sem.sDVE))

        for eidx in range(n_exp):
            for tg in range(4):
                for hh in range(HHn if "h" in exp_parts else 0):
                    par = hh % 4
                    glob = S.c(sem.sPEh)
                    if glob >= 4:
                        wait("sync", sem.sPEh, glob - 3)
                    import os as _os4
                    if call == 0 and relay_tgt:
                        wait("sync", sem.sWS,
                             max(relay_tgt[("w1", eidx, 7)],
                                 relay_tgt[("w3", eidx, 7)]))
                    t1 = dma("sync", sem.sW1, w1t[:, par, :, :],
                             w1c[eidx, hh, :, :, :])
                    t3 = dma("sync", sem.sW3, w3t[:, par, :, :],
                             w3c[eidx, hh, :, :, :])
                    wait("tensor", sem.sW1, t1)
                    if hh <= 1:
                        wait("tensor", sem.sDVE2, S.c(sem.sDVE2))
                    if glob >= 2:
                        wait("tensor", sem.sACT, S.c(sem.sACT) - 1)
                        wait("tensor", sem.sDVE, S.c(sem.sDVE) - 1)
                    for cc in range(CCn):
                        fn = (lambda e, hh=hh, cc=cc, tg=tg, par=par: e.matmul(
                            P[2 * (par % 2)][:, :], w1t[:, par, cc, :],
                            xnfm[:, cc, tg * 512:(tg + 1) * 512],
                            start=(cc == 0), stop=(cc == CCn - 1)))
                        if cc == CCn - 1:
                            inc("tensor", fn, sem.sPE1)
                        else:
                            run("tensor", fn)
                    wait("tensor", sem.sW3, t3)
                    for cc in range(CCn):
                        fn = (lambda e, hh=hh, cc=cc, tg=tg, par=par: e.matmul(
                            P[2 * (par % 2) + 1][:, :], w3t[:, par, cc, :],
                            xnfm[:, cc, tg * 512:(tg + 1) * 512],
                            start=(cc == 0), stop=(cc == CCn - 1)))
                        if cc == CCn - 1:
                            inc("tensor", fn, sem.sPEh)
                        else:
                            run("tensor", fn)
                    wait("scalar", sem.sPE1, S.c(sem.sPE1))
                    if glob >= 2:
                        wait("scalar", sem.sDVE, S.c(sem.sDVE) - 1)
                    inc("scalar", lambda e, par=par: e.activation(
                        out=sl[:, par % 2, :], in_=P[2 * (par % 2)][:, :],
                        func=AF.Silu), sem.sACT)
                    wait("vector", sem.sPEh, S.c(sem.sPEh))
                    wait("vector", sem.sACT, S.c(sem.sACT))
                    inc("vector", lambda e, hh=hh, par=par: e.tensor_tensor(
                        hid[:, hh, :], sl[:, par % 2, :],
                        P[2 * (par % 2) + 1][:, :], OP.mult), sem.sDVE)
                if "eo" not in exp_parts:
                    continue
                hid_done = S.c(sem.sDVE)
                wait("tensor", sem.sDVE, hid_done)
                wait("tensor", sem.sDVE2, S.c(sem.sDVE2))
                for hh in range(HHn):
                    par = hh % 2
                    globe = S.c(sem.sPEeo)
                    if globe >= 2:
                        wait("sync", sem.sPEeo, globe - 1)
                    if call == 0 and relay_tgt:
                        wait("sync", sem.sWS, relay_tgt[("w2", eidx, hh)])
                    t2 = dma("sync", sem.sW2, w2t[:, par, :],
                             w2c[eidx, hh, :, :])
                    wait("tensor", sem.sW2, t2)
                    for j in range(8):
                        ch, tt4 = j // 4, j % 4
                        fn = (lambda e, hh=hh, par=par, ch=ch, tt4=tt4, j=j:
                              e.matmul(
                                  P[j][:, :],
                                  hid[:, hh, tt4 * 128:(tt4 + 1) * 128],
                                  w2t[:, par, ch * 512:(ch + 1) * 512],
                                  start=(hh == 0), stop=(hh == HHn - 1)))
                        if j == 7:
                            inc("tensor", fn, sem.sPEeo)
                        else:
                            run("tensor", fn)
                if "evac" not in exp_parts:
                    continue
                wait("vector", sem.sPEeo, S.c(sem.sPEeo))
                for j in range(8):
                    ch, tt4 = j // 4, j % 4
                    tta = tg * 4 + tt4
                    inc("vector", lambda e, j=j, ch=ch, tta=tta, eidx=eidx:
                        e.scalar_tensor_tensor(
                            xres[:, tta, ch * 512:(ch + 1) * 512], P[j][:, :],
                            cmat[:, tta, eidx:eidx + 1],
                            xres[:, tta, ch * 512:(ch + 1) * 512],
                            OP.mult, OP.add), sem.sDVE2)

        if dbg_what == "hid" and call == 0:
            wait("gpsimd", sem.sDVE, S.c(sem.sDVE))
            dma("gpsimd", sem.sOUT,
                dbg_d[0:1024, :].rearrange("(p a) b -> p (a b)", p=128),
                hid[:, 0, :])


    def emit_ffn_sparse(call):
        xn_tgt = emit_ln(is_attn=False)
        wait("sync", sem.sXN, xn_tgt)
        for cc in range(CCn):
            dma("sync", sem.sXN, xnfm[:, cc, :],
                xnt_d[:, cc * 128:(cc + 1) * 128], transpose=True)
        xnfm_tgt = S.c(sem.sXN)
        wait("tensor", sem.sXN, xnfm_tgt)
        wait("tensor", sem.sDVE2, S.c(sem.sDVE2))
        wait("tensor", sem.sACT, S.c(sem.sACT))
        emit_gating()
        gate_done = S.c(sem.sDVE)

        m1v = msk1b.rearrange("p a b -> p (a b)")
        m2v = msk2b.rearrange("p a b -> p (a b)")
        wait("vector", sem.sDVE, gate_done)
        inc("vector", lambda e: e.tensor_add(msumb[:, :], m1v, m2v), sem.sDVE)
        msum_tgt = S.c(sem.sDVE)
        wait("tensor", sem.sDVE, max(msum_tgt, rc_done))
        run("tensor", lambda e: e.matmul(
            P[2][:, 0:128], usub[:, :], msumb[:, :], start=True, stop=True))
        inc("tensor", lambda e: e.matmul(
            P[2][0:1, 128:256], onescol_b[:, :], msumb[:, :],
            start=True, stop=True), sem.sPE1)
        wait("vector", sem.sPE1, S.c(sem.sPE1))
        inc("vector", lambda e: e.tensor_copy(out=SRr[:, :], in_=P[2][:, 0:128]),
            sem.sDVE)
        cstep("vector", lambda e: e.tensor_copy(
            out=rrow_f[0:1, :], in_=P[2][0:1, 128:256]))
        inc("vector", lambda e: e.nop() if False else e.tensor_copy(
            out=rsc_f[:, 0:1], in_=rsc_f[:, 0:1]), sem.sDVE)
        cnt_tgt = S.c(sem.sDVE)
        wait("tensor", sem.sDVE, cnt_tgt)
        inc("tensor", lambda e: e.transpose(
            P[6][0:128, 0:1], rrow_f[0:1, :], ident_f[0:1, 0:1]), sem.sPE1)
        wait("vector", sem.sPE1, S.c(sem.sPE1))
        inc("vector", lambda e: e.tensor_copy(out=rsc_f[:, 1:2],
                                              in_=P[6][0:128, 0:1]), sem.sDVE)
        wait("tensor", sem.sDVE, S.c(sem.sDVE))
        inc("tensor", lambda e: e.matmul(
            P[6][:, 1:2], u16f[:, :], rsc_f[:, 1:2], start=True, stop=True),
            sem.sPE1)
        wait("vector", sem.sPE1, S.c(sem.sPE1))
        inc("vector", lambda e: e.tensor_add(
            rsc_f[:, 2:3], P[6][:, 1:2], ecapcol[:, 0:1]), sem.sDVE)
        wait("tensor", sem.sDVE, S.c(sem.sDVE))
        inc("tensor", lambda e: e.transpose(
            P[6][0:1, 2:130], rsc_f[:, 2:3], ident_f[:, :]), sem.sPE1)
        wait("vector", sem.sPE1, S.c(sem.sPE1))
        inc("vector", lambda e: e.tensor_copy(
            out=rrow_f[0:1, :], in_=P[6][0:1, 2:130]), sem.sDVE)
        wait("tensor", sem.sDVE, S.c(sem.sDVE))
        inc("tensor", lambda e: e.matmul(
            P[6][:, 256:384], onesf[0:1, :], rrow_f[0:1, :],
            start=True, stop=True), sem.sPE1)
        wait("vector", sem.sPE1, S.c(sem.sPE1))
        inc("vector", lambda e: e.tensor_add(
            SRr[:, :], SRr[:, :], P[6][:, 256:384]), sem.sDVE)
        if os.environ.get("K_NOCLAMP", "") != "1":
            wait("tensor", sem.sDVE, S.c(sem.sDVE))
            inc("tensor", lambda e: e.matmul(
                P[6][:, 0:128], onesf[0:1, :], ecrow[0:1, :],
                start=True, stop=True), sem.sPE1)
            wait("vector", sem.sPE1, S.c(sem.sPE1))
            inc("vector", lambda e: e.tensor_tensor(
                SRr[:, :], SRr[:, :], P[6][:, 0:128], OP.min), sem.sDVE)
        sr_tgt = S.c(sem.sDVE)
        sqf = sq.rearrange("p a b -> p (a b)")
        troute = sqf[:, 0:128]
        tr16a = sqf[:, 128:144]
        tr16b = sqf[:, 144:160]
        for k, mv in ((0, m1v), (1, m2v)):
            wait("vector", sem.sDVE, sr_tgt)
            cstep("vector", lambda e, mv=mv: e.tensor_tensor(
                troute, mv, SRr[:, :], OP.mult))
            cstep("vector", lambda e, k=k: e.tensor_reduce(
                srt[:, :, k:k + 1],
                troute.rearrange("p (a b) -> p a b", b=8), axis=AX, op=OP.add))
            srk = srt[:, :, k:k + 1].rearrange("p a b -> p (a b)")
            si = sint2[:, :, k:k + 1].rearrange("p a b -> p (a b)")
            cstep("vector", lambda e, srk=srk, si=si: e.tensor_copy(
                out=si, in_=srk))
            cstep("vector", lambda e, si=si: e.tensor_scalar(
                tri_a[:, :], si, 127, None, OP.bitwise_and))
            cstep("vector", lambda e, si=si: e.tensor_scalar(
                tri_b[:, :], si, 7, None, OP.arith_shift_right))
            cstep("vector", lambda e: e.tensor_scalar(
                tri_b[:, :], tri_b[:, :], 0, None, OP.add))
            spv = sint[:, :, k:k + 1].rearrange("p a b -> p (a b)")
            cstep("vector", lambda e, spv=spv: e.tensor_scalar(
                spv, tri_a[:, :], 5, None, OP.arith_shift_left))
            cstep("vector", lambda e: e.tensor_scalar(
                tri_c[:, :], tri_a[:, :], 4, None, OP.arith_shift_left))
            cstep("vector", lambda e, spv=spv: e.tensor_tensor(
                spv, spv, tri_c[:, :], OP.add))
            cstep("vector", lambda e: e.tensor_scalar(
                tri_c[:, :], tri_a[:, :], 3, None, OP.arith_shift_left))
            cstep("vector", lambda e, spv=spv: e.tensor_tensor(
                spv, spv, tri_c[:, :], OP.add))
            inc("vector", lambda e, spv=spv: e.tensor_tensor(
                spv, spv, tri_b[:, :], OP.add), sem.sDVE)
        sint_tgt = S.c(sem.sDVE)

        wait("gpsimd", sem.sDVE, sint_tgt)
        for tt in range(TT):
            for k in range(2):
                idma("gpsimd", sem.sGB, glist_d[:, :],
                     tokid16[:, tt:tt + 1],
                     out_off=sint[:, tt, k:k + 1])
        sc_tgt = S.c(sem.sGB)
        wait("gpsimd", sem.sGB, sc_tgt)
        glt = dma("gpsimd", sem.sGB, glii[:, 0:56],
                  glist_d.rearrange("(p j) one -> p (j one)", p=128))

        tload_tgts = {}
        gst_tgts = {}
        hdone_tgts = {}
        eod_tgts = []

        def emit_gathers(eidx):
            wait("gpsimd", sem.sGB, glt)
            if eidx >= 2:
                wait("gpsimd", sem.sST, tload_tgts[eidx - 2])
            for i in range(NT):
                gpar = i % 2
                if i >= 2:
                    wait("gpsimd", sem.sST, S.c(sem.sST) - 16)
                idma("gpsimd", sem.sGB, xgstage[:, gpar, :],
                     xnt_d[0:128, :],
                     in_off=glii[:, eidx * NT + i:eidx * NT + i + 1])
                wait("gpsimd", sem.sGB, S.c(sem.sGB))
                dma("gpsimd", sem.sST,
                    xg_d[eidx % 2, i * 128:(i + 1) * 128, :],
                    xgstage[:, gpar, :])
            gst_tgts[eidx] = S.c(sem.sST)

        def emit_tloads(eidx):
            wait("scalar", sem.sST, gst_tgts[eidx])
            if eidx >= 1:
                wait("scalar", sem.sPEh, hdone_tgts[eidx - 1])
            for cc in range(CCn):
                dma("scalar", sem.sST, xg_fm[:, cc, :],
                    xg_d[eidx % 2, :, cc * 128:(cc + 1) * 128], transpose=True)
            tload_tgts[eidx] = S.c(sem.sST)

        if call == 0 and ("moe" in init_parts):
            _emit_relayout_expert(0, parts=("w13",))
        emit_gathers(0)
        emit_tloads(0)
        for eidx in range(E):
            wait("tensor", sem.sST, tload_tgts[eidx])
            if eidx + 1 < E:
                emit_gathers(eidx + 1)
            if call == 0 and ("moe" in init_parts):
                if eidx == 0:
                    _emit_relayout_expert(0, parts=("w2",))
                if eidx + 1 < E:
                    _emit_relayout_expert(eidx + 1)

            for hh in range(HHn):
                par = hh % 4
                pp = hh % 2
                glob = S.c(sem.sPEh)
                if glob >= 4:
                    wait("sync", sem.sPEh, glob - 3)
                if call == 0 and relay_tgt:
                    wait("sync", sem.sWS, max(relay_tgt[("w1", eidx, 7)],
                                              relay_tgt[("w3", eidx, 7)]))
                t1 = dma("sync", sem.sW1, w1t[:, par, :, :],
                         w1c[eidx, hh, :, :, :])
                t3 = dma("sync", sem.sW3, w3t[:, par, :, :],
                         w3c[eidx, hh, :, :, :])
                wait("tensor", sem.sW1, t1)
                if hh <= 1:
                    wait("tensor", sem.sACT, S.c(sem.sACT))
                if glob >= 2:
                    wait("tensor", sem.sACT, S.c(sem.sACT) - 2)
                    wait("tensor", sem.sDVE, S.c(sem.sDVE) - 2)
                for cc in range(CCn):
                    run("tensor", lambda e, cc=cc, par=par, pp=pp: e.matmul(
                        P[4 * pp][:, :], w1t[:, par, cc, :],
                        xg_fm[:, cc, 0:512],
                        start=(cc == 0), stop=(cc == CCn - 1)))
                for cc in range(CCn):
                    fn = (lambda e, cc=cc, par=par, pp=pp: e.matmul(
                        P[4 * pp + 1][:, 0:384], w1t[:, par, cc, :],
                        xg_fm[:, cc, 512:896],
                        start=(cc == 0), stop=(cc == CCn - 1)))
                    if cc == CCn - 1:
                        inc("tensor", fn, sem.sPE1)
                    else:
                        run("tensor", fn)
                h1_tgt = S.c(sem.sPE1)
                wait("tensor", sem.sW3, t3)
                for cc in range(CCn):
                    run("tensor", lambda e, cc=cc, par=par, pp=pp: e.matmul(
                        P[4 * pp + 2][:, :], w3t[:, par, cc, :],
                        xg_fm[:, cc, 0:512],
                        start=(cc == 0), stop=(cc == CCn - 1)))
                for cc in range(CCn):
                    fn = (lambda e, cc=cc, par=par, pp=pp: e.matmul(
                        P[4 * pp + 3][:, 0:384], w3t[:, par, cc, :],
                        xg_fm[:, cc, 512:896],
                        start=(cc == 0), stop=(cc == CCn - 1)))
                    if cc == CCn - 1:
                        inc("tensor", fn, sem.sPEh)
                    else:
                        run("tensor", fn)
                wait("scalar", sem.sPE1, h1_tgt)
                if glob >= 2:
                    wait("scalar", sem.sDVE, S.c(sem.sDVE) - 2)
                run("scalar", lambda e, pp=pp: e.activation(
                    out=sl2[:, pp, 0:512], in_=P[4 * pp][:, :], func=AF.Silu))
                inc("scalar", lambda e, pp=pp: e.activation(
                    out=sl2[:, pp, 512:896], in_=P[4 * pp + 1][:, 0:384],
                    func=AF.Silu), sem.sACT, 2)
                wait("vector", sem.sPEh, S.c(sem.sPEh))
                wait("vector", sem.sACT, S.c(sem.sACT))
                run("vector", lambda e, hh=hh, pp=pp: e.tensor_tensor(
                    hidg[:, hh, 0:512], sl2[:, pp, 0:512],
                    P[4 * pp + 2][:, :], OP.mult))
                inc("vector", lambda e, hh=hh, pp=pp: e.tensor_tensor(
                    hidg[:, hh, 512:896], sl2[:, pp, 512:896],
                    P[4 * pp + 3][:, 0:384], OP.mult), sem.sDVE, 2)
            hdone_tgts[eidx] = S.c(sem.sPEh)
            hid_done = S.c(sem.sDVE)

            if eidx + 1 < E:
                emit_tloads(eidx + 1)

            wait("tensor", sem.sDVE, hid_done)
            for w, (t0, nt) in enumerate(((0, 4), (4, 3))):
                wait("tensor", sem.sACT, S.c(sem.sACT))
                for hh in range(HHn):
                    par = hh % 2
                    globe = S.c(sem.sPEeo)
                    if globe >= 2:
                        wait("sync", sem.sPEeo, globe - 1)
                    if call == 0 and relay_tgt:
                        wait("sync", sem.sWS, relay_tgt[("w2", eidx, hh)])
                    t2 = dma("sync", sem.sW2, w2t[:, par, :],
                             w2c[eidx, hh, :, :])
                    wait("tensor", sem.sW2, t2)
                    for j in range(2 * nt):
                        tl, ch = j // 2, j % 2
                        fn = (lambda e, hh=hh, par=par, tl=tl, ch=ch, j=j,
                              t0=t0: e.matmul(
                                  P[j][:, :],
                                  hidg[:, hh, (t0 + tl) * 128:
                                       (t0 + tl + 1) * 128],
                                  w2t[:, par, ch * 512:(ch + 1) * 512],
                                  start=(hh == 0), stop=(hh == HHn - 1)))
                        if j == 2 * nt - 1:
                            inc("tensor", fn, sem.sPEeo)
                        else:
                            run("tensor", fn)
                eo_tgt = S.c(sem.sPEeo)
                wait("scalar", sem.sPEeo, eo_tgt)
                for tl in range(nt):
                    spar = len(eod_tgts) % 2
                    if len(eod_tgts) >= 2:
                        wait("scalar", sem.sEOD, eod_tgts[-2])
                    run("scalar", lambda e, tl=tl, spar=spar: e.copy(
                        out=eostage[:, spar, 0:512], in_=P[2 * tl][:, :]))
                    inc("scalar", lambda e, tl=tl, spar=spar: e.copy(
                        out=eostage[:, spar, 512:1024], in_=P[2 * tl + 1][:, :]),
                        sem.sACT)
                    wait("gpsimd", sem.sACT, S.c(sem.sACT))
                    base_row = eidx * CAP + (t0 + tl) * 128
                    eod_tgts.append(dma(
                        "gpsimd", sem.sEOD, eo_d[base_row:base_row + 128, :],
                        eostage[:, spar, :]))
        eo_all = S.c(sem.sEOD)

        if dbg_what == "sr" and call == 0:
            wait("gpsimd", sem.sDVE, sint_tgt)
            dma("gpsimd", sem.sOUT,
                dbg_d[0:256, :].rearrange("(p a) b -> p (a b)", p=128),
                SRr[:, :])
        if dbg_what == "w1c" and call == 0:
            wait("gpsimd", sem.sWS, S.c(sem.sWS))
            dma("gpsimd", sem.sOUT,
                dbg_d.rearrange("(p a) b -> p (a b)", p=128),
                w1c[0, 0, :, :, :].rearrange("p cc w -> p (cc w)"))
        if dbg_what == "glist" and call == 0:
            wait("gpsimd", sem.sGB, S.c(sem.sGB))
            dma("gpsimd", sem.sOUT, dbg_d[0:128, 0:56], glii[:, 0:56])
        wait("gpsimd", sem.sEOD, eo_all)
        wait("gpsimd", sem.sDVE, sint_tgt)
        for tt in range(TT):
            if tt >= 1:
                wait("gpsimd", sem.sDVE2, S.c(sem.sDVE2))
            for k in range(2):
                idma("gpsimd", sem.sGB, xgstage[:, k, :], eo_d[0:128, :],
                     in_off=sint2[:, tt, k:k + 1])
            gb_tgt = S.c(sem.sGB)
            wait("vector", sem.sGB, gb_tgt)
            run("vector", lambda e, tt=tt: e.scalar_tensor_tensor(
                xres[:, tt, :], xgstage[:, 0, :], wts[:, tt, 0:1],
                xres[:, tt, :], OP.mult, OP.add))
            inc("vector", lambda e, tt=tt: e.scalar_tensor_tensor(
                xres[:, tt, :], xgstage[:, 1, :], wts[:, tt, 1:2],
                xres[:, tt, :], OP.mult, OP.add), sem.sDVE2)

    def emit_attn():
        _emit_gb_bcast()
        if dbg_what == "attnc":
            wait("gpsimd", sem.sDVE, S.c(sem.sDVE))
            dma("gpsimd", sem.sOUT, dbg_d[0:128, 0:4], biascols[:, 0:4])
            dma("gpsimd", sem.sOUT, dbg_d[128:256, 0:64],
                g_bc[:, 0:64].bitcast(F32))
            dma("gpsimd", sem.sOUT, dbg_d[256:384, 0:64], b_bc[:, 0:64])
        cc_tgt = emit_ln(is_attn=True)
        if dbg_what == "xn2":
            wait("gpsimd", sem.sXN, cc_tgt)
            dma("gpsimd", sem.sOUT, dbg_d, cc_in[:, 0:64])
        for tt in range(TT):
            dma("gpsimd", sem.sSP, resid_d[tt * 128:(tt + 1) * 128, :],
                xres[:, tt, :])
        sp_tgt = S.c(sem.sSP)
        wait("gpsimd", sem.sXN, cc_tgt)
        inc("gpsimd", lambda e: e.collective_compute(
            "AllGather", OP.bypass,
            replica_groups=[[0, 1], [2, 3], [4, 5], [6, 7]],
            ins=[cc_in.opt()], outs=[cc_out.opt()]), sem.sCC)
        cc_done = S.c(sem.sCC)
        wait("gpsimd", sem.sSP, sp_tgt)

        barrier()

        wait("sync", sem.sWS, wc_done)
        wait("sync", sem.sCC, cc_done)
        wait("sync", sem.sSP, sp_tgt)
        for cc in range(CCn):
            dma("sync", sem.sST, wph[:, cc, :], wkt_d[cc * 128:(cc + 1) * 128, :])
        wkt_tgt = S.c(sem.sST)


        def emit_proj_rms(n_units, proj_dst, rhs_of, gam, sq_tgts, sqrt_tgts,
                          recip_tgts, mult_tgts):
            for step in range(n_units + 1):
                kc = step
                pc = step - 1
                if pc >= 0:
                    wait("tensor", sem.sACT, sq_tgts[pc])
                    inc("tensor", lambda e, pc=pc: e.matmul(
                        P[2][0:2, :], ones2h[:, :], ksq[:, pc % 2, :],
                        start=True, stop=True), sem.sPE1)
                    ss_tgt = S.c(sem.sPE1)
                if kc < n_units:
                    if kc >= 2:
                        wait("tensor", sem.sACT, sq_tgts[kc - 2])
                    for cc in range(CCn):
                        fn = (lambda e, cc=cc, kc=kc: e.matmul(
                            P[kc % 2][:, :], wph[:, cc, kc * 128:(kc + 1) * 128],
                            rhs_of(kc, cc),
                            start=(cc == 0), stop=(cc == CCn - 1)))
                        if cc == CCn - 1:
                            inc("tensor", fn, sem.sPE1)
                        else:
                            run("tensor", fn)
                    proj_tgt = S.c(sem.sPE1)
                if pc >= 0:
                    wait("scalar", sem.sPE1, ss_tgt)
                    inc("scalar", lambda e, pc=pc: e.activation(
                        out=rkt[0:2, pc % 2, :], in_=P[2][0:2, :], func=AF.Sqrt,
                        scale=1.0 / DH), sem.sACT)
                    sqrt_tgts.append(S.c(sem.sACT))
                if kc < n_units:
                    wait("scalar", sem.sPE1, proj_tgt)
                    run("scalar", lambda e, kc=kc: e.copy(
                        out=proj_dst(kc), in_=P[kc % 2][:, :]))
                    inc("scalar", lambda e, kc=kc: e.activation(
                        out=ksq[:, kc % 2, :], in_=P[kc % 2][:, :],
                        func=AF.Square), sem.sACT)
                    sq_tgts.append(S.c(sem.sACT))
                if pc >= 0:
                    wait("vector", sem.sACT, sqrt_tgts[pc])
                    cstep("vector", lambda e, pc=pc: e.tensor_scalar_max(
                        rkt[0:2, pc % 2, :], rkt[0:2, pc % 2, :], EPS_RMS))
                    inc("vector", lambda e, pc=pc: e.reciprocal(
                        rkt[0:2, pc % 2, :], rkt[0:2, pc % 2, :]), sem.sDVE)
                    recip_tgts.append(S.c(sem.sDVE))
                    wait("tensor", sem.sDVE, recip_tgts[pc])
                    if pc >= 1:
                        wait("tensor", sem.sDVE, mult_tgts[pc - 1])
                    inc("tensor", lambda e, pc=pc: e.matmul(
                        P[3][:, :], gam[:, :], rkt[0:2, pc % 2, :],
                        start=True, stop=True), sem.sPE1)
                    bc_tgt = S.c(sem.sPE1)
                    wait("vector", sem.sPE1, bc_tgt)
                    inc("vector", lambda e, pc=pc: e.tensor_tensor(
                        proj_dst(pc), proj_dst(pc), P[3][:, :], OP.mult),
                        sem.sDVE)
                    mult_tgts.append(S.c(sem.sDVE))

        k_sg_pe = []
        for sg in range(NSG):
            par = sg % 2
            if sg >= 2:
                wait("sync", sem.sPE1, k_sg_pe[sg - 2])
            for cc in range(CCn):
                dma("sync", sem.sST, stage[:, par, cc, :],
                    cc_out[sg * 512:(sg + 1) * 512, cc * 128:(cc + 1) * 128],
                    transpose=True)
            st_tgt = S.c(sem.sST)
            wait("tensor", sem.sST, st_tgt)
            if sg == 0:
                wait("tensor", sem.sST, wkt_tgt)
            wait("tensor", sem.sACT, S.c(sem.sACT))
            wait("tensor", sem.sDVE, S.c(sem.sDVE))
            emit_proj_rms(
                CCn,
                (lambda kc, sg=sg: kfm[:, kc, sg * 512:(sg + 1) * 512]),
                (lambda kc, cc, par=par: stage[:, par, cc, :]),
                kgam2, [], [], [], [])
            k_sg_pe.append(S.c(sem.sPE1))

        wait("sync", sem.sPE1, S.c(sem.sPE1))
        for cc in range(CCn):
            dma("sync", sem.sST, wph[:, cc, :], wvt_d[cc * 128:(cc + 1) * 128, :])
        wvt_tgt = S.c(sem.sST)
        for kt in range(NKT - 1):
            run("gpsimd", lambda e, kt=kt: e.memset(vt_h[:, kt, :, 64:65], 1.0))
        inc("gpsimd", lambda e: e.memset(vt_h[:, NKT - 1, :, 64:65], 1.0),
            sem.sGB)
        vones = S.c(sem.sGB)
        v_sg_pe = []
        for sg in range(NSG):
            par = sg % 2
            if sg >= 2:
                wait("sync", sem.sPE1, v_sg_pe[sg - 2])
            for cc in range(CCn):
                dma("sync", sem.sST, stage[:, par, cc, :],
                    cc_out[sg * 512:(sg + 1) * 512, cc * 128:(cc + 1) * 128],
                    transpose=True)
            st_tgt = S.c(sem.sST)
            wait("tensor", sem.sST, st_tgt)
            if sg == 0:
                wait("tensor", sem.sST, wvt_tgt)
                wait("tensor", sem.sGB, vones)
            for i in range(4):
                kt = sg * 4 + i
                for mh in range(2):
                    u = 2 * i + mh
                    pV = P[4 + u % 2]
                    if sg or i or mh:
                        wait("tensor", sem.sDVE2, S.c(sem.sDVE2) - 8)
                    for cc in range(CCn):
                        fn = (lambda e, cc=cc, i=i, mh=mh, par=par, pV=pV:
                              e.matmul(
                                  pV[:, :],
                                  stage[:, par, cc, i * 128:(i + 1) * 128],
                                  wph[:, cc, mh * 512:(mh + 1) * 512],
                                  start=(cc == 0), stop=(cc == CCn - 1)))
                        if cc == CCn - 1:
                            inc("tensor", fn, sem.sPE1)
                        else:
                            run("tensor", fn)
                    wait("vector", sem.sPE1, S.c(sem.sPE1))
                    for hp in range(8):
                        head = mh * 8 + hp
                        inc("vector", lambda e, kt=kt, head=head, hp=hp, pV=pV:
                            e.tensor_copy(
                                out=vt_h[:, kt, head, 0:64],
                                in_=pV[:, hp * 64:(hp + 1) * 64]), sem.sDVE2)
            v_sg_pe.append(S.c(sem.sPE1))

        for qb in range(NQB):
            wait("sync", sem.sDVE2, S.c(sem.sDVE2))
            wait("sync", sem.sPE1, S.c(sem.sPE1))
            wait("tensor", sem.sDVE2, S.c(sem.sDVE2))
            for cc in range(CCn):
                dma("sync", sem.sST, wph[:, cc, :],
                    wqt_d[cc * 128:(cc + 1) * 128, :])
            for cc in range(CCn):
                dma("sync", sem.sST, stage[:, 0, cc, :],
                    cc_in[qb * 512:(qb + 1) * 512, cc * 128:(cc + 1) * 128],
                    transpose=True)
            qst_tgt = S.c(sem.sST)
            wait("tensor", sem.sST, qst_tgt)
            wait("tensor", sem.sACT, S.c(sem.sACT))
            wait("tensor", sem.sDVE, S.c(sem.sDVE))
            emit_proj_rms(
                CCn,
                (lambda qc: qfm[:, qc, :]),
                (lambda qc, cc: stage[:, 0, cc, :]),
                qgam2, [], [], [], [])
            q_done_dve = S.c(sem.sDVE)

            for pr in range(CCn):
                wait("tensor", sem.sDVE, S.c(sem.sDVE))
                hA, hB = 2 * pr, 2 * pr + 1
                for kt in range(NKT):
                    pn = kt % 2
                    pSA, pSB = P[0 + pn], P[2 + pn]
                    if S.c(sem.sACT) >= 1:
                        wait("tensor", sem.sACT, S.c(sem.sACT) - 1)
                    run("tensor", lambda e, pr=pr, kt=kt, pSA=pSA: e.matmul(
                        pSA[:, :], kfm[0:64, pr, kt * 128:(kt + 1) * 128],
                        qfm[0:64, pr, :], tile_position=(0, 0),
                        start=True, stop=True))
                    inc("tensor", lambda e, pr=pr, kt=kt, pSB=pSB: e.matmul(
                        pSB[:, :], kfm[64:128, pr, kt * 128:(kt + 1) * 128],
                        qfm[64:128, pr, :], tile_position=(64, 0),
                        start=True, stop=True), sem.sPE1)
                    wait("scalar", sem.sPE1, S.c(sem.sPE1))
                    run("scalar", lambda e, pn=pn, pSA=pSA: e.activation(
                        out=Pt[:, 2 * pn, :], in_=pSA[:, :], func=AF.Exp,
                        bias=biascols[:, 2:3]))
                    inc("scalar", lambda e, pn=pn, pSB=pSB: e.activation(
                        out=Pt[:, 2 * pn + 1, :], in_=pSB[:, :], func=AF.Exp,
                        bias=biascols[:, 2:3]), sem.sACT)
                    if kt >= 1:
                        po = (kt - 1) % 2
                        wait("tensor", sem.sACT, S.c(sem.sACT) - 1)
                        run("tensor", lambda e, kt=kt, po=po, hA=hA: e.matmul(
                            P[4][0:65, :], vt_h[:, kt - 1, hA, 0:65],
                            Pt[:, 2 * po, :],
                            start=(kt == 1), stop=False))
                        inc("tensor", lambda e, kt=kt, po=po, hB=hB: e.matmul(
                            P[5][0:65, :], vt_h[:, kt - 1, hB, 0:65],
                            Pt[:, 2 * po + 1, :],
                            start=(kt == 1), stop=False), sem.sPE1)
                po = (NKT - 1) % 2
                wait("tensor", sem.sACT, S.c(sem.sACT))
                run("tensor", lambda e, po=po, hA=hA: e.matmul(
                    P[4][0:65, :], vt_h[:, NKT - 1, hA, 0:65],
                    Pt[:, 2 * po, :], start=False, stop=True))
                inc("tensor", lambda e, po=po, hB=hB: e.matmul(
                    P[5][0:65, :], vt_h[:, NKT - 1, hB, 0:65],
                    Pt[:, 2 * po + 1, :], start=False, stop=True), sem.sPE1)
                pv_done = S.c(sem.sPE1)
                wait("vector", sem.sPE1, pv_done)
                for hb in range(2):
                    pO = P[4 + hb]
                    cstep("vector", lambda e, hb=hb, pO=pO: e.tensor_copy(
                        out=onrm[0:65, hb, :], in_=pO[0:65, :]))
                    inc("vector", lambda e, hb=hb: e.reciprocal(
                        rden[0:1, :], onrm[64:65, hb, :]), sem.sDVE)
                    wait("tensor", sem.sDVE, S.c(sem.sDVE))
                    inc("tensor", lambda e: e.matmul(
                        P[6][0:64, :], onesf[0:1, 0:64], rden[0:1, :],
                        start=True, stop=True), sem.sPE1)
                    wait("vector", sem.sPE1, S.c(sem.sPE1))
                    base = 64 * hb
                    inc("vector", lambda e, hb=hb, pr=pr, base=base:
                        e.tensor_tensor(
                            out_norm[base:base + 64, pr, :],
                            onrm[0:64, hb, :], P[6][0:64, :], OP.mult),
                        sem.sDVE)
            onorm_done = S.c(sem.sDVE)

            wait("sync", sem.sDVE, onorm_done)
            for cc in range(CCn):
                dma("sync", sem.sST, wph[:, cc, :],
                    wot_d[cc * 128:(cc + 1) * 128, :])
            wo_tgt = S.c(sem.sST)
            wait("tensor", sem.sST, wo_tgt)
            wait("tensor", sem.sDVE, onorm_done)
            prev_store = None
            for tt4 in range(4):
                tta = qb * 4 + tt4
                if prev_store is not None:
                    wait("sync", sem.sXN, prev_store)
                rs_t = dma("sync", sem.sXN, res_stage[:, :],
                           resid_d[tta * 128:(tta + 1) * 128, :])
                for mh in range(2):
                    pW = P[6 + (2 * tt4 + mh) % 2]
                    if tt4 or mh:
                        wait("tensor", sem.sDVE2, S.c(sem.sDVE2))
                    for pr in range(CCn):
                        run("tensor", lambda e, pr=pr, tt4=tt4, mh=mh, pW=pW:
                            e.matmul(
                                pW[:, :],
                                out_norm[:, pr, tt4 * 128:(tt4 + 1) * 128],
                                wph[:, pr, mh * 512:(mh + 1) * 512],
                                start=(pr == 0), stop=False))
                    inc("tensor", lambda e, mh=mh, pW=pW: e.matmul(
                        pW[:, :], ones1b[0:1, :],
                        borow[0:1, mh * 512:(mh + 1) * 512],
                        start=False, stop=True), sem.sPE1)
                    wait("vector", sem.sPE1, S.c(sem.sPE1))
                    wait("vector", sem.sXN, rs_t)
                    inc("vector", lambda e, mh=mh, pW=pW: e.tensor_add(
                        res_stage[:, mh * 512:(mh + 1) * 512], pW[:, :],
                        res_stage[:, mh * 512:(mh + 1) * 512]), sem.sDVE2)
                wait("sync", sem.sDVE2, S.c(sem.sDVE2))
                prev_store = dma("sync", sem.sXN,
                                 resid_d[tta * 128:(tta + 1) * 128, :],
                                 res_stage[:, :])

        barrier()
        wait("sync", sem.sXN, S.c(sem.sXN))
        for tt in range(TT):
            dma("sync", sem.sXN, xres[:, tt, :],
                resid_d[tt * 128:(tt + 1) * 128, :])
        wait("tensor", sem.sXN, S.c(sem.sXN))
        wait("vector", sem.sXN, S.c(sem.sXN))
        wait("scalar", sem.sXN, S.c(sem.sXN))
        barrier()

    use_sparse = os.environ.get("K_DENSE", "") != "1"
    ffn_fn = emit_ffn_sparse if use_sparse else emit_ffn
    if "ffn1" in stages:
        ffn_fn(0)
        barrier()
    if "attn" in stages:
        emit_attn()
    if "ffn2" in stages:
        ffn_fn(1)
        barrier()

    wait("sync", sem.sDVE2, S.c(sem.sDVE2))
    wait("sync", sem.sDVE, S.c(sem.sDVE))
    for tt in range(TT):
        dma("sync", sem.sOUT, out_d[tt * 128:(tt + 1) * 128, :], xres[:, tt, :])
    wait("sync", sem.sOUT, S.c(sem.sOUT))

    with nc.Block() as block:
        @block.sync
        def _(eng):
            for fn in S.q["sync"]:
                fn(eng)

        @block.scalar
        def _(eng):
            for fn in S.q["scalar"]:
                fn(eng)

        @block.vector
        def _(eng):
            for fn in S.q["vector"]:
                fn(eng)

        @block.tensor
        def _(eng):
            for fn in S.q["tensor"]:
                fn(eng)

        @block.gpsimd
        def _(eng):
            for fn in S.q["gpsimd"]:
                fn(eng)

    stack.close()
    return nc


_NC_CACHE = {}


def kernel(L0_attn_x, L0_attn_Wq, L0_attn_Wk, L0_attn_Wv, L0_attn_Wo,
           L0_attn_bo, L0_attn_ln_g, L0_attn_ln_b, L0_attn_q_gamma,
           L0_attn_k_gamma, L0_ffn_Wg, L0_ffn_W1, L0_ffn_W3, L0_ffn_W2,
           _stages=("ffn1", "attn", "ffn2")):
    x = np.ascontiguousarray(np.asarray(L0_attn_x, dtype=np.float32))
    B, N, Cm = x.shape
    assert (B, N, Cm) == (4, 4096, 1024)
    weights = {
        "wq": np.ascontiguousarray(L0_attn_Wq, dtype=np.float32),
        "wk": np.ascontiguousarray(L0_attn_Wk, dtype=np.float32),
        "wv": np.ascontiguousarray(L0_attn_Wv, dtype=np.float32),
        "wo": np.ascontiguousarray(L0_attn_Wo, dtype=np.float32),
        "bo": np.ascontiguousarray(np.asarray(L0_attn_bo, np.float32).reshape(1, C)),
        "lng": np.ascontiguousarray(np.asarray(L0_attn_ln_g, np.float32).reshape(1, C)),
        "lnb": np.ascontiguousarray(np.asarray(L0_attn_ln_b, np.float32).reshape(1, C)),
        "qg": np.ascontiguousarray(np.asarray(L0_attn_q_gamma, np.float32).reshape(1, DH)),
        "kg": np.ascontiguousarray(np.asarray(L0_attn_k_gamma, np.float32).reshape(1, DH)),
        "wg": np.ascontiguousarray(L0_ffn_Wg, dtype=np.float32),
        "w1": np.ascontiguousarray(L0_ffn_W1, dtype=np.float32),
        "w3": np.ascontiguousarray(L0_ffn_W3, dtype=np.float32),
        "w2": np.ascontiguousarray(L0_ffn_W2, dtype=np.float32),
    }
    key = tuple(_stages)
    if key not in _NC_CACHE:
        _NC_CACHE[key] = build_program(_stages)
    nc = _NC_CACHE[key]
    in_maps = []
    for core in range(8):
        bi, half = core // 2, core % 2
        shard = np.ascontiguousarray(x[bi, half * T:(half + 1) * T, :])
        m = {"x": shard}
        m.update(weights)
        in_maps.append(m)
    res = run_bass_kernel_spmd(nc, in_maps, core_ids=list(range(8)))
    out = np.empty((B, N, Cm), dtype=np.float32)
    global DBG
    DBG = []
    for core in range(8):
        bi, half = core // 2, core % 2
        out[bi, half * T:(half + 1) * T, :] = res.results[core]["out"]
        if "dbg" in res.results[core]:
            DBG.append(np.asarray(res.results[core]["dbg"]))
    return out


# version 14
# speedup vs baseline: 1.1828x; 1.0358x over previous
import numpy as np
import concourse.bass as bass
import concourse.mybir as mybir
from concourse.bass_utils import run_bass_kernel_spmd

F32 = mybir.dt.float32
BF16 = mybir.dt.bfloat16
U8 = mybir.dt.uint8
AF = mybir.ActivationFunctionType
OP = mybir.AluOpType
AX = mybir.AxisListType.X

T = 2048
C = 1024
H = 4096
E = 8
TT = 16
CCn = 8
HHn = 32
NQB = 4
NSG = 8
NKT = 32
DH = 64
EPS_LN = 1e-5
EPS_RMS = 1e-8

ENGINES = ("sync", "scalar", "vector", "tensor", "gpsimd")


class Sched:
    def __init__(self):
        self.q = {e: [] for e in ENGINES}
        self.cnt = {}

    def emit(self, eng, fn):
        self.q[eng].append(fn)

    def c(self, sem):
        return self.cnt.get(sem.name, 0)

    def bump(self, sem, n=1):
        self.cnt[sem.name] = self.c(sem) + n
        return self.cnt[sem.name]


def build_program(stages=("ffn1", "attn", "ffn2")):
    import os
    init_parts = set(os.environ.get("K_INIT", "consts,gb,wt,moe").split(","))
    nc = bass.Bass("TRN2", target_bir_lowering=False, debug=False)

    din = {}
    din["x"] = nc.dram_tensor("x", [T, C], F32, kind="ExternalInput").ap()
    for nm in ("wq", "wk", "wv", "wo"):
        din[nm] = nc.dram_tensor(nm, [C, C], F32, kind="ExternalInput").ap()
    din["bo"] = nc.dram_tensor("bo", [1, C], F32, kind="ExternalInput").ap()
    din["lng"] = nc.dram_tensor("lng", [1, C], F32, kind="ExternalInput").ap()
    din["lnb"] = nc.dram_tensor("lnb", [1, C], F32, kind="ExternalInput").ap()
    din["qg"] = nc.dram_tensor("qg", [1, DH], F32, kind="ExternalInput").ap()
    din["kg"] = nc.dram_tensor("kg", [1, DH], F32, kind="ExternalInput").ap()
    din["wg"] = nc.dram_tensor("wg", [C, E], F32, kind="ExternalInput").ap()
    din["w1"] = nc.dram_tensor("w1", [E, C, H], F32, kind="ExternalInput").ap()
    din["w3"] = nc.dram_tensor("w3", [E, C, H], F32, kind="ExternalInput").ap()
    din["w2"] = nc.dram_tensor("w2", [E, H, C], F32, kind="ExternalInput").ap()
    out_d = nc.dram_tensor("out", [T, C], F32, kind="ExternalOutput").ap()
    import os as _osd
    dbg_what = _osd.environ.get("K_DBG", "")
    dbg_d = nc.dram_tensor("dbg", [2048, 64], F32, kind="ExternalOutput").ap()

    w1c = nc.dram_tensor("w1c", [E, HHn, 128, CCn, 128], BF16).ap()
    w3c = nc.dram_tensor("w3c", [E, HHn, 128, CCn, 128], BF16).ap()
    w2c = nc.dram_tensor("w2c", [E, HHn, 128, C], BF16).ap()
    wqt_d = nc.dram_tensor("wqt_d", [C, C], BF16).ap()
    wkt_d = nc.dram_tensor("wkt_d", [C, C], BF16).ap()
    wvt_d = nc.dram_tensor("wvt_d", [C, C], BF16).ap()
    wot_d = nc.dram_tensor("wot_d", [C, C], BF16).ap()
    xnt_d = nc.dram_tensor("xnt_d", [T, C], BF16).ap()
    cc_in = nc.dram_tensor("cc_in", [T, C], BF16).ap()
    cc_out = nc.dram_tensor("cc_out", [2 * T, C], BF16).ap()
    resid_d = nc.dram_tensor("resid_d", [T, C], F32).ap()
    CAP = 896
    NT = CAP // 128
    NSL = E * CAP
    glist_d = nc.dram_tensor("glist_d", [NSL, 1], mybir.dt.int32).ap()
    xg_d = nc.dram_tensor("xg_d", [2, CAP, C], BF16).ap()
    eo_d = nc.dram_tensor("eo_d", [NSL, C], BF16).ap()

    arenaA = nc.alloc_sbuf_tensor("arenaA", [128, 65536], U8)
    arenaB = nc.alloc_sbuf_tensor("arenaB", [128, 67584], U8)
    arenaC = nc.alloc_sbuf_tensor("arenaC", [128, 68096], U8)

    def carve(arena, off, nbytes, dt, shape):
        ap = arena.ap()[:, off:off + nbytes].bitcast(dt)
        if len(shape) == 1:
            assert ap.shape[1] == shape[0], (ap.shape, shape)
            return ap
        names = " ".join(f"d{i}" for i in range(len(shape)))
        return ap.rearrange(f"p ({names}) -> p {names}",
                            **{f"d{i}": s for i, s in enumerate(shape)})

    xres = carve(arenaA, 0, 65536, F32, (TT, C))
    kfm = carve(arenaA, 0, 65536, BF16, (CCn, 4096))

    xnfm = carve(arenaB, 0, 32768, BF16, (CCn, T))
    hid = carve(arenaB, 32768, 32768, BF16, (HHn, 512))
    vt = carve(arenaB, 0, 66560, BF16, (NKT, 1040))
    vt_h = vt.rearrange("p t (h w) -> p t h w", w=65)

    w1t = carve(arenaC, 51200, 8192, BF16, (4, CCn, 128))
    w3t = carve(arenaC, 59392, 8192, BF16, (4, CCn, 128))
    w2t = carve(arenaC, 8192, 4096, BF16, (2, C))
    sl = carve(arenaC, 12288, 2048, BF16, (2, 512))
    sq = carve(arenaC, 14336, 8192, F32, (2, C))
    xnt_st = carve(arenaC, 22528, 4096, BF16, (2, C))
    wstripe = carve(arenaC, 26624, 8192, BF16, (2, 2048))
    g_bc = carve(arenaC, 43008, 4096, F32, (C,))
    b_bc = carve(arenaC, 47104, 4096, F32, (C,))
    stage = carve(arenaC, 0, 16384, BF16, (2, CCn, 512))
    wph = carve(arenaC, 16384, 16384, BF16, (CCn, C))
    qfm = carve(arenaC, 32768, 8192, BF16, (CCn, 512))
    Pt = carve(arenaC, 40960, 4096, BF16, (4, 512))
    out_norm = carve(arenaC, 45056, 8192, BF16, (CCn, 512))
    onrm = carve(arenaC, 53248, 4096, F32, (2, 512))
    ksq = carve(arenaC, 57344, 2048, BF16, (2, 512))
    rkt = carve(arenaC, 59392, 4096, F32, (2, 512))
    res_stage = carve(arenaC, 63488, 4096, F32, (C,))
    mxbuf = carve(arenaC, 67584, 256, F32, (2, 32))
    rden = carve(arenaC, 40960, 2048, F32, (512,))[0:1, :]
    grow = carve(arenaC, 14336, 4096, F32, (C,))[0:1, :]
    brow = carve(arenaC, 18432, 4096, F32, (C,))[0:1, :]
    biascols = carve(arenaC, 67840, 16, F32, (4,))
    I32 = mybir.dt.int32
    xgstage = carve(arenaC, 0, 4096, BF16, (2, C))
    glii = carve(arenaC, 4096, 256, I32, (64,))
    srt = carve(arenaC, 4352, 128, F32, (16, 2))
    sint = carve(arenaC, 4480, 128, I32, (16, 2))
    sint2 = carve(arenaC, 4608, 128, I32, (16, 2))
    tokid16 = None
    SRr = carve(arenaC, 4800, 512, F32, (128,))
    msumb = carve(arenaC, 5312, 256, BF16, (128,))
    rsc_f = carve(arenaC, 5568, 64, F32, (16,))
    rrow_f = carve(arenaC, 5632, 512, F32, (128,))[0:1, :]
    ecapcol = None
    msk1b = carve(arenaC, 6208, 256, BF16, (16, E))
    msk2b = carve(arenaC, 6464, 256, BF16, (16, E))
    wts = carve(arenaC, 6720, 128, F32, (16, 2))
    usub = None
    u16f = None
    ubAi = carve(arenaC, 26624, 512, I32, (128,))
    ubBi = carve(arenaC, 27136, 512, I32, (128,))
    ubCi = carve(arenaC, 27648, 512, I32, (128,))
    ubDi = carve(arenaC, 28160, 512, I32, (128,))
    tri_a = carve(arenaC, 14976, 64, I32, (16,))
    tri_b = carve(arenaC, 15040, 64, I32, (16,))
    hidg = carve(arenaB, 0, 57344, BF16, (HHn, CAP))
    xg_fm = carve(arenaC, 34816, 14336, BF16, (CCn, CAP))
    sl2 = carve(arenaC, 18432, 3584, BF16, (2, CAP))
    eostage = carve(arenaC, 14336, 4096, BF16, (2, C))
    tri_c = carve(arenaC, 15104, 64, I32, (16,))

    stats = nc.alloc_sbuf_tensor("stats", [128, 16, 8], F32).ap()
    cmat = nc.alloc_sbuf_tensor("cmat", [128, 16, E], F32).ap()
    wgs = nc.alloc_sbuf_tensor("wgs", [128, CCn, E], BF16).ap()
    gsc = nc.alloc_sbuf_tensor("gsc", [128, 3, E], F32).ap()
    gtmp = nc.alloc_sbuf_tensor("gtmp", [128, 8], F32).ap()
    ident_f = nc.alloc_sbuf_tensor("ident_f", [128, 128], F32).ap()
    ident_b = nc.alloc_sbuf_tensor("ident_b", [128, 128], BF16).ap()
    onesf = nc.alloc_sbuf_tensor("onesf", [1, 128], F32).ap()
    ones1b = nc.alloc_sbuf_tensor("ones1b", [1, 128], BF16).ap()
    ones2h = nc.alloc_sbuf_tensor("ones2h", [128, 2], BF16).ap()
    o2tmp = nc.alloc_sbuf_tensor("o2tmp", [128, 2], F32).ap()
    qgam2 = nc.alloc_sbuf_tensor("qgam2", [2, 128], F32).ap()
    kgam2 = nc.alloc_sbuf_tensor("kgam2", [2, 128], F32).ap()
    negmT = nc.alloc_sbuf_tensor("negmT", [1, 4], F32).ap()
    borow = nc.alloc_sbuf_tensor("borow", [1, C], BF16).ap()
    ceps = nc.alloc_sbuf_tensor("ceps", [128, 1], F32).ap()
    onescol_b = nc.alloc_sbuf_tensor("onescol_b", [128, 1], BF16).ap()
    usub = nc.alloc_sbuf_tensor("usub", [128, 128], BF16).ap()
    u16f = nc.alloc_sbuf_tensor("u16f", [128, 128], F32).ap()
    ecapcol = nc.alloc_sbuf_tensor("ecapcol", [128, 1], F32).ap()
    tokid16 = nc.alloc_sbuf_tensor("tokid16", [128, 16], mybir.dt.int32).ap()
    ecrow = nc.alloc_sbuf_tensor("ecrow", [1, 128], F32).ap()
    wtmp = nc.alloc_sbuf_tensor("wtmp", [128, 2, 128], BF16).ap()
    wsto = nc.alloc_sbuf_tensor("wsto", [128, 2, 128], BF16).ap()
    assert nc.sbuf_bytes_remaining >= 0, nc.sbuf_bytes_remaining

    P = [nc.alloc_psum_tensor(f"p{i}", [128, 512], F32).ap() for i in range(8)]
    p_bf = P[7].bitcast(BF16)

    S = Sched()
    import contextlib
    stack = contextlib.ExitStack()
    sems = {}
    for nm in ("sIN", "sW1", "sW3", "sW2", "sST", "sXN", "sSP", "sCC", "sWC",
               "sWS", "sOUT", "sPE1", "sPEh", "sPEeo", "sPEx", "sACT", "sDVE",
               "sDVE2", "sBAR", "sGB", "sVC", "sSC", "sGC", "sEOD"):
        sems[nm] = stack.enter_context(nc.semaphore(nm))
    sem = type("X", (), sems)

    def emit(eng, fn):
        S.emit(eng, fn)

    def wait(eng, s, val):
        if val > 0:
            emit(eng, lambda e, s=s, v=val: e.wait_ge(s, v))

    def dma(eng, s, out, in_, **kw):
        tgt = S.bump(s, 16)
        emit(eng, lambda e, o=out, i=in_, s=s, kw=kw:
             e.dma_start(out=o, in_=i, **kw).then_inc(s, 16))
        return tgt

    from concourse.bass_core_rust import IndirectOffsetOnAxis

    def idma(eng, s, out, in_, out_off=None, in_off=None):
        tgt = S.bump(s, 16)
        emit(eng, lambda e, o=out, i=in_, s=s, oo=out_off, io=in_off:
             e.indirect_dma_start(
                 o, IndirectOffsetOnAxis(oo, 0) if oo is not None else None,
                 i, IndirectOffsetOnAxis(io, 0) if io is not None else None
             ).then_inc(s, 16))
        return tgt

    def inc(eng, instr_fn, s, n=1):
        tgt = S.bump(s, n)
        emit(eng, lambda e, f=instr_fn, s=s, n=n: f(e).then_inc(s, n))
        return tgt

    def run(eng, instr_fn):
        emit(eng, lambda e, f=instr_fn: f(e))

    chain_sems = {}

    def cstep(eng, instr_fn):
        sc = chain_sems[eng]
        tgt = inc(eng, instr_fn, sc)
        wait(eng, sc, tgt)

    chain_sems.update({"vector": sem.sVC, "scalar": sem.sSC,
                       "gpsimd": sem.sGC, "tensor": sem.sPEx})

    nbar = [0]

    def barrier():
        nbar[0] += 1
        tgt = nbar[0] * len(ENGINES)
        S.bump(sem.sBAR, len(ENGINES))
        for eng in ENGINES:
            emit(eng, lambda e: e.sem_inc(sem.sBAR, 1))
            wait(eng, sem.sBAR, tgt)

    for tt in range(TT):
        dma("sync", sem.sIN, xres[:, tt, :], din["x"][tt * 128:(tt + 1) * 128, :])
    x_loaded = S.c(sem.sIN)

    cstep("gpsimd", lambda e: e.iota(
        ident_f[:, :], [[1, 128]], channel_multiplier=-1,
        allow_small_or_imprecise_dtypes=True))
    cstep("gpsimd", lambda e: e.tensor_scalar(
        ident_f[:, :], ident_f[:, :], 0.0, None, OP.is_equal))
    run("gpsimd", lambda e: e.memset(onesf[:, :], 1.0))
    run("gpsimd", lambda e: e.memset(ones1b[:, :], 1.0))
    run("gpsimd", lambda e: e.memset(qgam2[0:2, :], 0.0))
    run("gpsimd", lambda e: e.memset(kgam2[0:2, :], 0.0))
    run("gpsimd", lambda e: e.memset(ceps[:, :], EPS_LN))
    cstep("gpsimd", lambda e: e.iota(
        o2tmp[:, 0:1], [[1, 1]], channel_multiplier=1,
        allow_small_or_imprecise_dtypes=True))
    cstep("gpsimd", lambda e: e.tensor_scalar(
        o2tmp[:, 1:2], o2tmp[:, 0:1], 63.5, None, OP.is_gt))
    inc("gpsimd", lambda e: e.tensor_scalar(
        o2tmp[:, 0:1], o2tmp[:, 1:2], -1.0, 1.0, OP.mult, OP.add), sem.sGB)
    wait("scalar", sem.sGB, 1)
    inc("scalar", lambda e: e.copy(out=ident_b[:, :], in_=ident_f[:, :]), sem.sACT)
    inc("scalar", lambda e: e.copy(out=ones2h[:, :], in_=o2tmp[:, :]), sem.sACT)
    consts_cast = S.c(sem.sACT)

    def _emit_const_loads():
        dma("gpsimd", sem.sIN, qgam2[0:1, 0:64], din["qg"][0:1, :])
        dma("gpsimd", sem.sIN, qgam2[1:2, 64:128], din["qg"][0:1, :])
        dma("gpsimd", sem.sIN, kgam2[0:1, 0:64], din["kg"][0:1, :])
        dma("gpsimd", sem.sIN, kgam2[1:2, 64:128], din["kg"][0:1, :])
        dma("gpsimd", sem.sIN, grow[:, :], din["lng"][:, :])
        dma("gpsimd", sem.sIN, brow[:, :], din["lnb"][:, :])
        dma("gpsimd", sem.sIN, borow[:, :], din["bo"][:, :])
        tgt = dma("gpsimd", sem.sIN, wgs[:, :, :],
                  din["wg"].rearrange("(cc p) e -> p cc e", p=128))
        wait("vector", sem.sIN, tgt)
        inc("vector", lambda e: e.tensor_scalar_mul(
            qgam2[0:2, :], qgam2[0:2, :], 0.125), sem.sDVE)
        return tgt

    def _emit_gb_bcast():
        dma("gpsimd", sem.sIN, grow[:, :], din["lng"][:, :])
        gb_tgt = dma("gpsimd", sem.sIN, brow[:, :], din["lnb"][:, :])
        wait("tensor", sem.sIN, gb_tgt)
        wait("tensor", sem.sACT, max(consts_cast, S.c(sem.sACT)))
        wait("tensor", sem.sDVE, S.c(sem.sDVE))
        wait("tensor", sem.sDVE2, S.c(sem.sDVE2))
        for half in range(2):
            if half == 1:
                wait("tensor", sem.sDVE, S.c(sem.sDVE))
            run("tensor", lambda e, h=half: e.matmul(
                P[6][:, :], onesf[0:1, :], grow[0:1, h * 512:(h + 1) * 512],
                start=True, stop=True))
            inc("tensor", lambda e, h=half: e.matmul(
                P[5][:, :], onesf[0:1, :], brow[0:1, h * 512:(h + 1) * 512],
                start=True, stop=True), sem.sPEx)
            wait("vector", sem.sPEx, S.c(sem.sPEx))
            run("vector", lambda e, h=half: e.tensor_copy(
                out=g_bc[:, h * 512:(h + 1) * 512], in_=P[6][:, :]))
            inc("vector", lambda e, h=half: e.tensor_copy(
                out=b_bc[:, h * 512:(h + 1) * 512], in_=P[5][:, :]), sem.sDVE)

    def _emit_wt_transposes():
        import os as _os
        wtn = int(_os.environ.get("K_WTN", "256"))
        wt_skip_store = _os.environ.get("K_WT_NOSTORE", "") == "1"
        wt_skip_dve = _os.environ.get("K_WT_NODVE", "") == "1"
        wt_skip_tr = _os.environ.get("K_WT_NOTRANS", "") == "1"
        wait("tensor", sem.sACT, consts_cast)
        store_tgts = []
        n_tr = 0
        for nm, dst in (("wq", wqt_d), ("wk", wkt_d), ("wv", wvt_d),
                        ("wo", wot_d)):
            for mm in range(8):
                for cc in range(8):
                    n = n_tr
                    if n >= wtn:
                        continue
                    par = n % 2
                    if n >= 2:
                        wait("gpsimd", sem.sPEx, S.c(sem.sPEx) - 1)
                    lt = dma("gpsimd", sem.sWC, wtmp[:, par, :],
                             din[nm][mm * 128:(mm + 1) * 128,
                                     cc * 128:(cc + 1) * 128])
                    wait("tensor", sem.sWC, lt)
                    if n >= 1:
                        wait("tensor", sem.sDVE2, S.c(sem.sDVE2))
                    if wt_skip_tr:
                        inc("tensor", lambda e: e.engine_nop()
                            if False else e.nop(), sem.sPEx)
                    else:
                        inc("tensor", lambda e, par=par: e.transpose(
                            p_bf[:, par * 128:par * 128 + 128], wtmp[:, par, :],
                            ident_b[:, :]), sem.sPEx)
                    wait("vector", sem.sPEx, S.c(sem.sPEx))
                    if n >= 2:
                        wait("vector", sem.sWS, store_tgts[n - 2])
                    if wt_skip_dve or wt_skip_tr:
                        inc("vector", lambda e, par=par: e.tensor_copy(
                            out=wsto[:, par, :], in_=wtmp[:, par, :]),
                            sem.sDVE2)
                    else:
                        inc("vector", lambda e, par=par: e.tensor_copy(
                            out=wsto[:, par, :],
                            in_=p_bf[:, par * 128:par * 128 + 128]), sem.sDVE2)
                    wait("sync", sem.sDVE2, S.c(sem.sDVE2))
                    if wt_skip_store:
                        st = S.bump(sem.sWS, 16)
                        emit("sync", lambda e: e.sem_inc(sem.sWS, 16))
                    else:
                        st = dma("sync", sem.sWS,
                                 dst[cc * 128:(cc + 1) * 128,
                                     mm * 128:(mm + 1) * 128],
                                 wsto[:, par, :])
                    store_tgts.append(st)
                    n_tr += 1

    relay_tgt = {}

    def _emit_relayout_expert(eidx, parts=("w13", "w2")):
        units = []
        if "w13" in parts:
            for cc in range(CCn):
                units += [("w1", eidx, cc, 0), ("w1", eidx, cc, 1),
                          ("w3", eidx, cc, 0), ("w3", eidx, cc, 1)]
        if "w2" in parts:
            units += [("w2", eidx, hh, 0) for hh in range(HHn)]
        _emit_relayout_units(units)

    rl_n = [0]

    def _emit_relayout_units(units):
        for (kind, eidx, i, hf) in units:
            par = rl_n[0] % 2
            rl_n[0] += 1
            wait("gpsimd", sem.sWS, S.c(sem.sWS) - 16)
            if kind in ("w1", "w3"):
                src = din[kind]
                dstc = w1c if kind == "w1" else w3c
                lt = dma("gpsimd", sem.sWC, wstripe[:, par, :],
                         src[eidx, i * 128:(i + 1) * 128,
                             hf * 2048:(hf + 1) * 2048])
                wait("gpsimd", sem.sWC, lt)
                st = dma("gpsimd", sem.sWS,
                         dstc[eidx, hf * 16:(hf + 1) * 16, :, i, :]
                         .transpose([1, 0, 2]),
                         wstripe[:, par, :].rearrange("p (hh w) -> p hh w",
                                                      w=128))
                relay_tgt[(kind, eidx, i, hf)] = st
                relay_tgt[(kind, eidx, i)] = st
            else:
                lt = dma("gpsimd", sem.sWC, wstripe[:, par, 0:1024],
                         din["w2"][eidx, i * 128:(i + 1) * 128, :])
                wait("gpsimd", sem.sWC, lt)
                st = dma("gpsimd", sem.sWS, w2c[eidx, i, :, :],
                         wstripe[:, par, 0:1024])
                relay_tgt[("w2", eidx, i)] = st

    def _emit_moe_cache():
        st_list = []
        units = []
        for eidx in range(E):
            for cc in range(CCn):
                units.append(("w1", eidx, cc))
                units.append(("w3", eidx, cc))
            for hh in range(HHn):
                units.append(("w2", eidx, hh))
        for u, (kind, eidx, i) in enumerate(units):
            par = u % 2
            if u >= 2:
                wait("gpsimd", sem.sWS, st_list[u - 2])
            if kind == "w1":
                lt = dma("gpsimd", sem.sWC, wstripe[:, par, :],
                         din["w1"][eidx, i * 128:(i + 1) * 128, :])
                wait("gpsimd", sem.sWC, lt)
                st = dma("gpsimd", sem.sWS,
                         w1c[eidx, :, :, i, :].transpose([1, 0, 2]),
                         wstripe[:, par, :].rearrange("p (hh w) -> p hh w",
                                                      w=128))
            elif kind == "w3":
                lt = dma("gpsimd", sem.sWC, wstripe[:, par, :],
                         din["w3"][eidx, i * 128:(i + 1) * 128, :])
                wait("gpsimd", sem.sWC, lt)
                st = dma("gpsimd", sem.sWS,
                         w3c[eidx, :, :, i, :].transpose([1, 0, 2]),
                         wstripe[:, par, :].rearrange("p (hh w) -> p hh w",
                                                      w=128))
            else:
                lt = dma("gpsimd", sem.sWC, wstripe[:, par, 0:1024],
                         din["w2"][eidx, i * 128:(i + 1) * 128, :])
                wait("gpsimd", sem.sWC, lt)
                st = dma("gpsimd", sem.sWS, w2c[eidx, i, :, :],
                         wstripe[:, par, 0:1024])
            st_list.append(st)
            relay_tgt[(kind, eidx, i)] = st

    def _emit_route_consts():
        wait("gpsimd", sem.sGB, 1)
        cstep("gpsimd", lambda e: e.iota(ubBi[:, :], [[1, 128]],
              channel_multiplier=-1,
              allow_small_or_imprecise_dtypes=True))
        inc("gpsimd", lambda e: e.iota(ubDi[:, 0:1], [[1, 1]],
            channel_multiplier=1), sem.sGB)
        iotas_done = S.c(sem.sGB)
        wait("vector", sem.sGB, iotas_done)
        rfull = carve(arenaC, 5632, 512, F32, (128,))
        cstep("vector", lambda e: e.tensor_copy(out=rfull[:, :],
                                                in_=ubBi[:, :]))
        cstep("vector", lambda e: e.tensor_scalar(
            u16f[:, :], rfull[:, :], 0.0, None, OP.is_gt))
        cstep("vector", lambda e: e.tensor_copy(out=usub[:, :],
                                                in_=u16f[:, :]))
        cstep("vector", lambda e: e.tensor_scalar(
            ubCi[:, :], ubBi[:, :], 7, None, OP.bitwise_and))
        cstep("vector", lambda e: e.tensor_copy(out=rfull[:, :],
                                                in_=ubCi[:, :]))
        cstep("vector", lambda e: e.tensor_scalar(
            rfull[:, :], rfull[:, :], 0.0, None, OP.is_equal))
        cstep("vector", lambda e: e.tensor_mul(
            u16f[:, :], u16f[:, :], rfull[:, :]))
        cstep("vector", lambda e: e.tensor_scalar(
            ubDi[:, 1:2], ubDi[:, 0:1], 7, None, OP.bitwise_and))
        cstep("vector", lambda e: e.tensor_scalar(
            ubDi[:, 2:3], ubDi[:, 1:2], 9, None, OP.arith_shift_left))
        cstep("vector", lambda e: e.tensor_scalar(
            ubDi[:, 3:4], ubDi[:, 1:2], 8, None, OP.arith_shift_left))
        cstep("vector", lambda e: e.tensor_tensor(
            ubDi[:, 2:3], ubDi[:, 2:3], ubDi[:, 3:4], OP.add))
        cstep("vector", lambda e: e.tensor_scalar(
            ubDi[:, 3:4], ubDi[:, 1:2], 7, None, OP.arith_shift_left))
        cstep("vector", lambda e: e.tensor_tensor(
            ubDi[:, 2:3], ubDi[:, 2:3], ubDi[:, 3:4], OP.add))
        inc("vector", lambda e: e.tensor_copy(
            out=ecapcol[:, 0:1], in_=ubDi[:, 2:3]), sem.sDVE)
        wait("tensor", sem.sDVE, S.c(sem.sDVE))
        inc("tensor", lambda e: e.transpose(
            P[6][0:1, 0:128], ecapcol[:, 0:1], ident_f[:, :]), sem.sPE1)
        wait("vector", sem.sPE1, S.c(sem.sPE1))
        cstep("vector", lambda e: e.tensor_copy(
            out=ecrow[0:1, :], in_=P[6][0:1, 0:128]))
        inc("vector", lambda e: e.tensor_scalar_add(
            ecrow[0:1, :], ecrow[0:1, :], float(CAP - 1)), sem.sDVE)
        rc = S.c(sem.sDVE)
        cstep("gpsimd", lambda e: e.iota(
            tokid16[:, :], [[128, 16]], channel_multiplier=1))
        run("gpsimd", lambda e: e.memset(onescol_b[:, :], 1.0))
        cstep("gpsimd", lambda e: e.memset(glii[:, :], 0))
        dma("gpsimd", sem.sGB, glist_d.rearrange("(p j) one -> p (j one)", p=128),
            glii[:, 0:56])
        return rc

    def _emit_softmax_bound():
        wait("vector", sem.sIN, gam_loaded[0])
        wait("vector", sem.sDVE, S.c(sem.sDVE))
        cstep("vector", lambda e: e.reduce_max(
            negmT[0:1, 0:1], qgam2[0:1, 0:64], axis=AX,
            apply_absolute_value=True))
        cstep("vector", lambda e: e.reduce_max(
            negmT[0:1, 1:2], kgam2[0:1, 0:64], axis=AX,
            apply_absolute_value=True))
        cstep("vector", lambda e: e.tensor_mul(
            negmT[0:1, 2:3], negmT[0:1, 0:1], negmT[0:1, 1:2]))
        inc("vector", lambda e: e.tensor_scalar_mul(
            negmT[0:1, 3:4], negmT[0:1, 2:3], -64.0), sem.sDVE)
        wait("tensor", sem.sDVE, S.c(sem.sDVE))
        wait("tensor", sem.sPEx, S.c(sem.sPEx))
        inc("tensor", lambda e: e.matmul(
            P[2][:, 0:1], onesf[0:1, :], negmT[0:1, 3:4],
            start=True, stop=True), sem.sPE1)
        wait("vector", sem.sPE1, S.c(sem.sPE1))
        inc("vector", lambda e: e.tensor_copy(
            out=biascols[:, 2:3], in_=P[2][:, 0:1]), sem.sDVE)

    gam_loaded = [0]
    if "consts" in init_parts:
        gam_loaded[0] = _emit_const_loads()
    else:
        gam_loaded[0] = dma("gpsimd", sem.sIN, wgs[:, :, :],
                            din["wg"].rearrange("(cc p) e -> p cc e", p=128))
    if "gb" in init_parts:
        _emit_gb_bcast()
    if "wt" in init_parts:
        _emit_wt_transposes()
    _emit_softmax_bound()
    rc_done = _emit_route_consts()

    wait("scalar", sem.sIN, x_loaded)
    wait("vector", sem.sIN, x_loaded)
    barrier()
    if "moe" in init_parts and os.environ.get("K_DENSE", "") == "1":
        _emit_moe_cache()
    wc_done = S.c(sem.sWS)

    def emit_ln(is_attn):
        dst_d = cc_in if is_attn else xnt_d
        if is_attn:
            wait("scalar", sem.sPEx, S.c(sem.sPEx))
            wait("vector", sem.sPEx, S.c(sem.sPEx))
        ssq_base = S.c(sem.sDVE)
        act_base = S.c(sem.sACT)
        for tt in range(TT):
            par = tt % 2
            if tt >= 2:
                wait("scalar", sem.sDVE, ssq_base + tt - 1)
            inc("scalar", lambda e, tt=tt, par=par: e.square(
                out=sq[:, par, :], in_=xres[:, tt, :]), sem.sACT)
            run("vector", lambda e, tt=tt: e.reduce_sum(
                stats[:, tt, 0:1], xres[:, tt, :], axis=AX))
            wait("vector", sem.sACT, act_base + tt + 1)
            inc("vector", lambda e, tt=tt, par=par: e.reduce_sum(
                stats[:, tt, 1:2], sq[:, par, :], axis=AX), sem.sDVE)
            S.bump(sem.sDVE, 0)
        wait("vector", sem.sDVE, S.c(sem.sDVE))
        cstep("vector", lambda e: e.tensor_scalar_mul(
            stats[:, :, 2:3], stats[:, :, 0:1], 1.0 / C))
        cstep("vector", lambda e: e.tensor_mul(
            stats[:, :, 6:7], stats[:, :, 2:3], stats[:, :, 2:3]))
        inc("vector", lambda e: e.scalar_tensor_tensor(
            stats[:, :, 3:4], stats[:, :, 1:2], 1.0 / C, stats[:, :, 6:7],
            OP.mult, OP.subtract), sem.sDVE)
        wait("scalar", sem.sDVE, S.c(sem.sDVE))
        inc("scalar", lambda e: e.activation(
            out=stats[:, :, 4:5], in_=stats[:, :, 3:4], func=AF.Sqrt,
            bias=ceps[:, 0:1]), sem.sACT)
        wait("vector", sem.sACT, S.c(sem.sACT))
        cstep("vector", lambda e: e.reciprocal(stats[:, :, 5:6], stats[:, :, 4:5]))
        if is_attn:
            cstep("vector", lambda e: e.tensor_mul(
                stats[:, :, 6:7], stats[:, :, 5:6], stats[:, :, 5:6]))
            inc("vector", lambda e: e.tensor_mul(
                stats[:, :, 6:7], stats[:, :, 3:4], stats[:, :, 6:7]), sem.sDVE)
            wait("scalar", sem.sDVE, S.c(sem.sDVE))
            inc("scalar", lambda e: e.activation(
                out=stats[:, :, 4:5], in_=stats[:, :, 6:7], func=AF.Sqrt,
                bias=ceps[:, 0:1]), sem.sACT)
            wait("vector", sem.sACT, S.c(sem.sACT))
            cstep("vector", lambda e: e.reciprocal(stats[:, :, 6:7], stats[:, :, 4:5]))
            cstep("vector", lambda e: e.tensor_mul(
                stats[:, :, 5:6], stats[:, :, 5:6], stats[:, :, 6:7]))
        st_tgts = []
        for tt in range(TT):
            par = tt % 2
            if tt >= 2:
                wait("vector", sem.sXN, st_tgts[tt - 2])
            if is_attn:
                cstep("vector", lambda e, tt=tt, par=par: e.tensor_scalar(
                    sq[:, par, :], xres[:, tt, :], stats[:, tt, 2:3],
                    stats[:, tt, 5:6], OP.subtract, OP.mult))
                cstep("vector", lambda e, par=par: e.tensor_mul(
                    sq[:, par, :], sq[:, par, :], g_bc[:, :]))
                inc("vector", lambda e, par=par: e.tensor_add(
                    xnt_st[:, par, :], sq[:, par, :], b_bc[:, :]), sem.sDVE)
            else:
                inc("vector", lambda e, tt=tt, par=par: e.tensor_scalar(
                    xnt_st[:, par, :], xres[:, tt, :], stats[:, tt, 2:3],
                    stats[:, tt, 5:6], OP.subtract, OP.mult), sem.sDVE)
            wait("sync", sem.sDVE, S.c(sem.sDVE))
            st_tgts.append(dma("sync", sem.sXN,
                               dst_d[tt * 128:(tt + 1) * 128, :],
                               xnt_st[:, par, :]))
        return S.c(sem.sXN)

    def emit_gating():
        for tt in range(TT):
            for cc in range(CCn):
                fn = (lambda e, tt=tt, cc=cc: e.matmul(
                    P[0][:, tt * E:(tt + 1) * E],
                    xnfm[:, cc, tt * 128:(tt + 1) * 128],
                    wgs[:, cc, :], start=(cc == 0), stop=(cc == CCn - 1)))
                if tt == TT - 1 and cc == CCn - 1:
                    inc("tensor", fn, sem.sPE1)
                else:
                    run("tensor", fn)
        wait("vector", sem.sPE1, S.c(sem.sPE1))
        sqf = sq.rearrange("p a b -> p (a b)")
        lg = sqf[:, 0:128].rearrange("p (a b) -> p a b", b=8)
        lg2 = sqf[:, 128:256].rearrange("p (a b) -> p a b", b=8)
        m1 = sqf[:, 256:272]
        m2 = sqf[:, 272:288]
        dm = sqf[:, 288:304]
        sg = sqf[:, 304:320]
        m1b = sqf[:, 256:272].unsqueeze(2).broadcast_to([128, 16, 8])
        m2b = sqf[:, 272:288].unsqueeze(2).broadcast_to([128, 16, 8])
        msk1v = msk1b[:, :, :]
        msk2v = msk2b[:, :, :]
        cstep("vector", lambda e: e.tensor_copy(
            out=sqf[:, 0:128], in_=P[0][:, 0:128]))
        cstep("vector", lambda e: e.tensor_reduce(
            m1, lg, axis=AX, op=OP.max))
        cstep("vector", lambda e: e.tensor_tensor(
            msk1v, lg, m1b, OP.is_equal))
        cstep("vector", lambda e: e.scalar_tensor_tensor(
            lg2, msk1v, -1e30, lg, OP.mult, OP.add))
        cstep("vector", lambda e: e.tensor_reduce(
            m2, lg2, axis=AX, op=OP.max))
        cstep("vector", lambda e: e.tensor_tensor(
            msk2v, lg2, m2b, OP.is_equal))
        inc("vector", lambda e: e.tensor_tensor(
            dm, m1, m2, OP.subtract), sem.sDVE)
        wait("scalar", sem.sDVE, S.c(sem.sDVE))
        inc("scalar", lambda e: e.activation(
            out=sg, in_=dm, func=AF.Sigmoid), sem.sACT)
        wait("vector", sem.sACT, S.c(sem.sACT))
        cstep("vector", lambda e: e.tensor_scalar_mul(
            wts[:, :, 0:1].rearrange("p a b -> p (a b)"), sg, 0.5))
        inc("vector", lambda e: e.tensor_scalar(
            wts[:, :, 1:2].rearrange("p a b -> p (a b)"),
            wts[:, :, 0:1].rearrange("p a b -> p (a b)"),
            -1.0, 0.5, OP.mult, OP.add), sem.sDVE)

    import os as _os2
    ffn_parts = set(_os2.environ.get("K_FFN", "ln,gate,exp").split(","))
    n_exp = int(_os2.environ.get("K_NEXP", "8"))
    exp_parts = set(_os2.environ.get("K_EXP", "h,eo,evac").split(","))

    def emit_ffn(call):
        xn_tgt = emit_ln(is_attn=False)
        wait("sync", sem.sXN, xn_tgt)
        for cc in range(CCn):
            dma("sync", sem.sXN, xnfm[:, cc, :],
                xnt_d[:, cc * 128:(cc + 1) * 128], transpose=True)
        xnfm_tgt = S.c(sem.sXN)
        wait("tensor", sem.sXN, xnfm_tgt)
        wait("tensor", sem.sDVE2, S.c(sem.sDVE2))
        wait("tensor", sem.sACT, S.c(sem.sACT))
        if "gate" in ffn_parts:
            emit_gating()
        if "exp" not in ffn_parts:
            return
        if dbg_what in ("cmat", "logits") and call == 0:
            wait("gpsimd", sem.sDVE, S.c(sem.sDVE))
            dma("gpsimd", sem.sOUT,
                dbg_d[:, 0:8].rearrange("(tt p) e -> p tt e", p=128),
                cmat[:, :, :])
        if dbg_what == "xn" and call == 0:
            wait("gpsimd", sem.sXN, S.c(sem.sXN))
            dma("gpsimd", sem.sOUT, dbg_d, xnt_d[:, 0:64])

        if dbg_what in ("w1c", "w2c") and call == 0:
            src = (w1c if dbg_what == "w1c" else w2c.rearrange(
                "e hh p (cc w) -> e hh p cc w", w=128))
            wait("gpsimd", sem.sWS, wc_done)
            dma("gpsimd", sem.sOUT,
                dbg_d.rearrange("(p a) b -> p (a b)", p=128),
                src[0, 0, :, :, :].rearrange("p cc w -> p (cc w)"))
        wait("tensor", sem.sDVE, S.c(sem.sDVE))

        for eidx in range(n_exp):
            for tg in range(4):
                for hh in range(HHn if "h" in exp_parts else 0):
                    par = hh % 4
                    glob = S.c(sem.sPEh)
                    if glob >= 4:
                        wait("sync", sem.sPEh, glob - 3)
                    import os as _os4
                    if call == 0 and relay_tgt:
                        wait("sync", sem.sWS,
                             max(relay_tgt[("w1", eidx, 7)],
                                 relay_tgt[("w3", eidx, 7)]))
                    t1 = dma("sync", sem.sW1, w1t[:, par, :, :],
                             w1c[eidx, hh, :, :, :])
                    t3 = dma("sync", sem.sW3, w3t[:, par, :, :],
                             w3c[eidx, hh, :, :, :])
                    wait("tensor", sem.sW1, t1)
                    if hh <= 1:
                        wait("tensor", sem.sDVE2, S.c(sem.sDVE2))
                    if glob >= 2:
                        wait("tensor", sem.sACT, S.c(sem.sACT) - 1)
                        wait("tensor", sem.sDVE, S.c(sem.sDVE) - 1)
                    for cc in range(CCn):
                        fn = (lambda e, hh=hh, cc=cc, tg=tg, par=par: e.matmul(
                            P[2 * (par % 2)][:, :], w1t[:, par, cc, :],
                            xnfm[:, cc, tg * 512:(tg + 1) * 512],
                            start=(cc == 0), stop=(cc == CCn - 1)))
                        if cc == CCn - 1:
                            inc("tensor", fn, sem.sPE1)
                        else:
                            run("tensor", fn)
                    wait("tensor", sem.sW3, t3)
                    for cc in range(CCn):
                        fn = (lambda e, hh=hh, cc=cc, tg=tg, par=par: e.matmul(
                            P[2 * (par % 2) + 1][:, :], w3t[:, par, cc, :],
                            xnfm[:, cc, tg * 512:(tg + 1) * 512],
                            start=(cc == 0), stop=(cc == CCn - 1)))
                        if cc == CCn - 1:
                            inc("tensor", fn, sem.sPEh)
                        else:
                            run("tensor", fn)
                    wait("scalar", sem.sPE1, S.c(sem.sPE1))
                    if glob >= 2:
                        wait("scalar", sem.sDVE, S.c(sem.sDVE) - 1)
                    inc("scalar", lambda e, par=par: e.activation(
                        out=sl[:, par % 2, :], in_=P[2 * (par % 2)][:, :],
                        func=AF.Silu), sem.sACT)
                    wait("vector", sem.sPEh, S.c(sem.sPEh))
                    wait("vector", sem.sACT, S.c(sem.sACT))
                    inc("vector", lambda e, hh=hh, par=par: e.tensor_tensor(
                        hid[:, hh, :], sl[:, par % 2, :],
                        P[2 * (par % 2) + 1][:, :], OP.mult), sem.sDVE)
                if "eo" not in exp_parts:
                    continue
                hid_done = S.c(sem.sDVE)
                wait("tensor", sem.sDVE, hid_done)
                wait("tensor", sem.sDVE2, S.c(sem.sDVE2))
                for hh in range(HHn):
                    par = hh % 2
                    globe = S.c(sem.sPEeo)
                    if globe >= 2:
                        wait("sync", sem.sPEeo, globe - 1)
                    if call == 0 and relay_tgt:
                        wait("sync", sem.sWS, relay_tgt[("w2", eidx, hh)])
                    t2 = dma("sync", sem.sW2, w2t[:, par, :],
                             w2c[eidx, hh, :, :])
                    wait("tensor", sem.sW2, t2)
                    for j in range(8):
                        ch, tt4 = j // 4, j % 4
                        fn = (lambda e, hh=hh, par=par, ch=ch, tt4=tt4, j=j:
                              e.matmul(
                                  P[j][:, :],
                                  hid[:, hh, tt4 * 128:(tt4 + 1) * 128],
                                  w2t[:, par, ch * 512:(ch + 1) * 512],
                                  start=(hh == 0), stop=(hh == HHn - 1)))
                        if j == 7:
                            inc("tensor", fn, sem.sPEeo)
                        else:
                            run("tensor", fn)
                if "evac" not in exp_parts:
                    continue
                wait("vector", sem.sPEeo, S.c(sem.sPEeo))
                for j in range(8):
                    ch, tt4 = j // 4, j % 4
                    tta = tg * 4 + tt4
                    inc("vector", lambda e, j=j, ch=ch, tta=tta, eidx=eidx:
                        e.scalar_tensor_tensor(
                            xres[:, tta, ch * 512:(ch + 1) * 512], P[j][:, :],
                            cmat[:, tta, eidx:eidx + 1],
                            xres[:, tta, ch * 512:(ch + 1) * 512],
                            OP.mult, OP.add), sem.sDVE2)

        if dbg_what == "hid" and call == 0:
            wait("gpsimd", sem.sDVE, S.c(sem.sDVE))
            dma("gpsimd", sem.sOUT,
                dbg_d[0:1024, :].rearrange("(p a) b -> p (a b)", p=128),
                hid[:, 0, :])


    def emit_ffn_sparse(call):
        xn_tgt = emit_ln(is_attn=False)
        wait("sync", sem.sXN, xn_tgt)
        for cc in range(CCn):
            dma("sync", sem.sXN, xnfm[:, cc, :],
                xnt_d[:, cc * 128:(cc + 1) * 128], transpose=True)
        xnfm_tgt = S.c(sem.sXN)
        wait("tensor", sem.sXN, xnfm_tgt)
        wait("tensor", sem.sDVE2, S.c(sem.sDVE2))
        wait("tensor", sem.sACT, S.c(sem.sACT))
        emit_gating()
        gate_done = S.c(sem.sDVE)

        m1v = msk1b.rearrange("p a b -> p (a b)")
        m2v = msk2b.rearrange("p a b -> p (a b)")
        wait("vector", sem.sDVE, gate_done)
        inc("vector", lambda e: e.tensor_add(msumb[:, :], m1v, m2v), sem.sDVE)
        msum_tgt = S.c(sem.sDVE)
        wait("tensor", sem.sDVE, max(msum_tgt, rc_done))
        run("tensor", lambda e: e.matmul(
            P[2][:, 0:128], usub[:, :], msumb[:, :], start=True, stop=True))
        inc("tensor", lambda e: e.matmul(
            P[2][0:1, 128:256], onescol_b[:, :], msumb[:, :],
            start=True, stop=True), sem.sPE1)
        wait("vector", sem.sPE1, S.c(sem.sPE1))
        inc("vector", lambda e: e.tensor_copy(out=SRr[:, :], in_=P[2][:, 0:128]),
            sem.sDVE)
        cstep("vector", lambda e: e.tensor_copy(
            out=rrow_f[0:1, :], in_=P[2][0:1, 128:256]))
        inc("vector", lambda e: e.nop() if False else e.tensor_copy(
            out=rsc_f[:, 0:1], in_=rsc_f[:, 0:1]), sem.sDVE)
        cnt_tgt = S.c(sem.sDVE)
        wait("tensor", sem.sDVE, cnt_tgt)
        inc("tensor", lambda e: e.transpose(
            P[6][0:128, 0:1], rrow_f[0:1, :], ident_f[0:1, 0:1]), sem.sPE1)
        wait("vector", sem.sPE1, S.c(sem.sPE1))
        inc("vector", lambda e: e.tensor_copy(out=rsc_f[:, 1:2],
                                              in_=P[6][0:128, 0:1]), sem.sDVE)
        wait("tensor", sem.sDVE, S.c(sem.sDVE))
        inc("tensor", lambda e: e.matmul(
            P[6][:, 1:2], u16f[:, :], rsc_f[:, 1:2], start=True, stop=True),
            sem.sPE1)
        wait("vector", sem.sPE1, S.c(sem.sPE1))
        inc("vector", lambda e: e.tensor_add(
            rsc_f[:, 2:3], P[6][:, 1:2], ecapcol[:, 0:1]), sem.sDVE)
        wait("tensor", sem.sDVE, S.c(sem.sDVE))
        inc("tensor", lambda e: e.transpose(
            P[6][0:1, 2:130], rsc_f[:, 2:3], ident_f[:, :]), sem.sPE1)
        wait("vector", sem.sPE1, S.c(sem.sPE1))
        inc("vector", lambda e: e.tensor_copy(
            out=rrow_f[0:1, :], in_=P[6][0:1, 2:130]), sem.sDVE)
        wait("tensor", sem.sDVE, S.c(sem.sDVE))
        inc("tensor", lambda e: e.matmul(
            P[6][:, 256:384], onesf[0:1, :], rrow_f[0:1, :],
            start=True, stop=True), sem.sPE1)
        wait("vector", sem.sPE1, S.c(sem.sPE1))
        inc("vector", lambda e: e.tensor_add(
            SRr[:, :], SRr[:, :], P[6][:, 256:384]), sem.sDVE)
        if os.environ.get("K_NOCLAMP", "") != "1":
            wait("tensor", sem.sDVE, S.c(sem.sDVE))
            inc("tensor", lambda e: e.matmul(
                P[6][:, 0:128], onesf[0:1, :], ecrow[0:1, :],
                start=True, stop=True), sem.sPE1)
            wait("vector", sem.sPE1, S.c(sem.sPE1))
            inc("vector", lambda e: e.tensor_tensor(
                SRr[:, :], SRr[:, :], P[6][:, 0:128], OP.min), sem.sDVE)
        sr_tgt = S.c(sem.sDVE)
        sqf = sq.rearrange("p a b -> p (a b)")
        troute = sqf[:, 0:128]
        tr16a = sqf[:, 128:144]
        tr16b = sqf[:, 144:160]
        for k, mv in ((0, m1v), (1, m2v)):
            wait("vector", sem.sDVE, sr_tgt)
            cstep("vector", lambda e, mv=mv: e.tensor_tensor(
                troute, mv, SRr[:, :], OP.mult))
            cstep("vector", lambda e, k=k: e.tensor_reduce(
                srt[:, :, k:k + 1],
                troute.rearrange("p (a b) -> p a b", b=8), axis=AX, op=OP.add))
            srk = srt[:, :, k:k + 1].rearrange("p a b -> p (a b)")
            si = sint2[:, :, k:k + 1].rearrange("p a b -> p (a b)")
            cstep("vector", lambda e, srk=srk, si=si: e.tensor_copy(
                out=si, in_=srk))
            cstep("vector", lambda e, si=si: e.tensor_scalar(
                tri_a[:, :], si, 127, None, OP.bitwise_and))
            cstep("vector", lambda e, si=si: e.tensor_scalar(
                tri_b[:, :], si, 7, None, OP.arith_shift_right))
            cstep("vector", lambda e: e.tensor_scalar(
                tri_b[:, :], tri_b[:, :], 0, None, OP.add))
            spv = sint[:, :, k:k + 1].rearrange("p a b -> p (a b)")
            cstep("vector", lambda e, spv=spv: e.tensor_scalar(
                spv, tri_a[:, :], 5, None, OP.arith_shift_left))
            cstep("vector", lambda e: e.tensor_scalar(
                tri_c[:, :], tri_a[:, :], 4, None, OP.arith_shift_left))
            cstep("vector", lambda e, spv=spv: e.tensor_tensor(
                spv, spv, tri_c[:, :], OP.add))
            cstep("vector", lambda e: e.tensor_scalar(
                tri_c[:, :], tri_a[:, :], 3, None, OP.arith_shift_left))
            cstep("vector", lambda e, spv=spv: e.tensor_tensor(
                spv, spv, tri_c[:, :], OP.add))
            inc("vector", lambda e, spv=spv: e.tensor_tensor(
                spv, spv, tri_b[:, :], OP.add), sem.sDVE)
        sint_tgt = S.c(sem.sDVE)

        wait("gpsimd", sem.sDVE, sint_tgt)
        for tt in range(TT):
            for k in range(2):
                idma("gpsimd", sem.sGB, glist_d[:, :],
                     tokid16[:, tt:tt + 1],
                     out_off=sint[:, tt, k:k + 1])
        sc_tgt = S.c(sem.sGB)
        wait("gpsimd", sem.sGB, sc_tgt)
        glt = dma("gpsimd", sem.sGB, glii[:, 0:56],
                  glist_d.rearrange("(p j) one -> p (j one)", p=128))

        tload_tgts = {}
        gst_tgts = {}
        hdone_tgts = {}
        eod_tgts = []

        def emit_gathers(eidx):
            wait("gpsimd", sem.sGB, glt)
            if eidx >= 2:
                wait("gpsimd", sem.sST, tload_tgts[eidx - 2])
            for i in range(NT):
                gpar = i % 2
                if i >= 2:
                    wait("gpsimd", sem.sST, S.c(sem.sST) - 16)
                idma("gpsimd", sem.sGB, xgstage[:, gpar, :],
                     xnt_d[0:128, :],
                     in_off=glii[:, eidx * NT + i:eidx * NT + i + 1])
                wait("gpsimd", sem.sGB, S.c(sem.sGB))
                dma("gpsimd", sem.sST,
                    xg_d[eidx % 2, i * 128:(i + 1) * 128, :],
                    xgstage[:, gpar, :])
            gst_tgts[eidx] = S.c(sem.sST)

        def emit_tloads(eidx):
            wait("scalar", sem.sST, gst_tgts[eidx])
            if eidx >= 1:
                wait("scalar", sem.sPEh, hdone_tgts[eidx - 1])
            for cc in range(CCn):
                dma("scalar", sem.sST, xg_fm[:, cc, :],
                    xg_d[eidx % 2, :, cc * 128:(cc + 1) * 128], transpose=True)
            tload_tgts[eidx] = S.c(sem.sST)

        if call == 0 and ("moe" in init_parts):
            _emit_relayout_expert(0, parts=("w13",))
        emit_gathers(0)
        emit_tloads(0)
        for eidx in range(E):
            wait("tensor", sem.sST, tload_tgts[eidx])
            if eidx + 1 < E:
                emit_gathers(eidx + 1)
            if call == 0 and ("moe" in init_parts):
                if eidx == 0:
                    _emit_relayout_expert(0, parts=("w2",))
                if eidx + 1 < E:
                    _emit_relayout_expert(eidx + 1)

            for hh in range(HHn):
                par = hh % 4
                pp = hh % 2
                glob = S.c(sem.sPEh)
                if glob >= 4:
                    wait("sync", sem.sPEh, glob - 3)
                if call == 0 and relay_tgt:
                    wait("sync", sem.sWS, max(relay_tgt[("w1", eidx, 7)],
                                              relay_tgt[("w3", eidx, 7)]))
                t1 = dma("sync", sem.sW1, w1t[:, par, :, :],
                         w1c[eidx, hh, :, :, :])
                t3 = dma("sync", sem.sW3, w3t[:, par, :, :],
                         w3c[eidx, hh, :, :, :])
                wait("tensor", sem.sW1, t1)
                if hh <= 1:
                    wait("tensor", sem.sACT, S.c(sem.sACT))
                if glob >= 2:
                    wait("tensor", sem.sACT, S.c(sem.sACT) - 2)
                    wait("tensor", sem.sDVE, S.c(sem.sDVE) - 2)
                for cc in range(CCn):
                    run("tensor", lambda e, cc=cc, par=par, pp=pp: e.matmul(
                        P[4 * pp][:, :], w1t[:, par, cc, :],
                        xg_fm[:, cc, 0:512],
                        start=(cc == 0), stop=(cc == CCn - 1)))
                for cc in range(CCn):
                    fn = (lambda e, cc=cc, par=par, pp=pp: e.matmul(
                        P[4 * pp + 1][:, 0:384], w1t[:, par, cc, :],
                        xg_fm[:, cc, 512:896],
                        start=(cc == 0), stop=(cc == CCn - 1)))
                    if cc == CCn - 1:
                        inc("tensor", fn, sem.sPE1)
                    else:
                        run("tensor", fn)
                h1_tgt = S.c(sem.sPE1)
                wait("tensor", sem.sW3, t3)
                for cc in range(CCn):
                    run("tensor", lambda e, cc=cc, par=par, pp=pp: e.matmul(
                        P[4 * pp + 2][:, :], w3t[:, par, cc, :],
                        xg_fm[:, cc, 0:512],
                        start=(cc == 0), stop=(cc == CCn - 1)))
                for cc in range(CCn):
                    fn = (lambda e, cc=cc, par=par, pp=pp: e.matmul(
                        P[4 * pp + 3][:, 0:384], w3t[:, par, cc, :],
                        xg_fm[:, cc, 512:896],
                        start=(cc == 0), stop=(cc == CCn - 1)))
                    if cc == CCn - 1:
                        inc("tensor", fn, sem.sPEh)
                    else:
                        run("tensor", fn)
                wait("scalar", sem.sPE1, h1_tgt)
                if glob >= 2:
                    wait("scalar", sem.sDVE, S.c(sem.sDVE) - 2)
                run("scalar", lambda e, pp=pp: e.activation(
                    out=sl2[:, pp, 0:512], in_=P[4 * pp][:, :], func=AF.Silu))
                inc("scalar", lambda e, pp=pp: e.activation(
                    out=sl2[:, pp, 512:896], in_=P[4 * pp + 1][:, 0:384],
                    func=AF.Silu), sem.sACT, 2)
                wait("vector", sem.sPEh, S.c(sem.sPEh))
                wait("vector", sem.sACT, S.c(sem.sACT))
                run("vector", lambda e, hh=hh, pp=pp: e.tensor_tensor(
                    hidg[:, hh, 0:512], sl2[:, pp, 0:512],
                    P[4 * pp + 2][:, :], OP.mult))
                inc("vector", lambda e, hh=hh, pp=pp: e.tensor_tensor(
                    hidg[:, hh, 512:896], sl2[:, pp, 512:896],
                    P[4 * pp + 3][:, 0:384], OP.mult), sem.sDVE, 2)
            hdone_tgts[eidx] = S.c(sem.sPEh)
            hid_done = S.c(sem.sDVE)

            if eidx + 1 < E:
                emit_tloads(eidx + 1)

            wait("tensor", sem.sDVE, hid_done)
            for w, (t0, nt) in enumerate(((0, 4), (4, 3))):
                wait("tensor", sem.sACT, S.c(sem.sACT))
                for hh in range(HHn):
                    par = hh % 2
                    globe = S.c(sem.sPEeo)
                    if globe >= 2:
                        wait("sync", sem.sPEeo, globe - 1)
                    if call == 0 and relay_tgt:
                        wait("sync", sem.sWS, relay_tgt[("w2", eidx, hh)])
                    t2 = dma("sync", sem.sW2, w2t[:, par, :],
                             w2c[eidx, hh, :, :])
                    wait("tensor", sem.sW2, t2)
                    for j in range(2 * nt):
                        tl, ch = j // 2, j % 2
                        fn = (lambda e, hh=hh, par=par, tl=tl, ch=ch, j=j,
                              t0=t0: e.matmul(
                                  P[j][:, :],
                                  hidg[:, hh, (t0 + tl) * 128:
                                       (t0 + tl + 1) * 128],
                                  w2t[:, par, ch * 512:(ch + 1) * 512],
                                  start=(hh == 0), stop=(hh == HHn - 1)))
                        if j == 2 * nt - 1:
                            inc("tensor", fn, sem.sPEeo)
                        else:
                            run("tensor", fn)
                eo_tgt = S.c(sem.sPEeo)
                wait("scalar", sem.sPEeo, eo_tgt)
                for tl in range(nt):
                    spar = len(eod_tgts) % 2
                    if len(eod_tgts) >= 2:
                        wait("scalar", sem.sEOD, eod_tgts[-2])
                    run("scalar", lambda e, tl=tl, spar=spar: e.copy(
                        out=eostage[:, spar, 0:512], in_=P[2 * tl][:, :]))
                    inc("scalar", lambda e, tl=tl, spar=spar: e.copy(
                        out=eostage[:, spar, 512:1024], in_=P[2 * tl + 1][:, :]),
                        sem.sACT)
                    wait("sync", sem.sACT, S.c(sem.sACT))
                    base_row = eidx * CAP + (t0 + tl) * 128
                    eod_tgts.append(dma(
                        "sync", sem.sEOD, eo_d[base_row:base_row + 128, :],
                        eostage[:, spar, :]))
        eo_all = S.c(sem.sEOD)

        if dbg_what == "sr" and call == 0:
            wait("gpsimd", sem.sDVE, sint_tgt)
            dma("gpsimd", sem.sOUT,
                dbg_d[0:256, :].rearrange("(p a) b -> p (a b)", p=128),
                SRr[:, :])
        if dbg_what == "w1c" and call == 0:
            wait("gpsimd", sem.sWS, S.c(sem.sWS))
            dma("gpsimd", sem.sOUT,
                dbg_d.rearrange("(p a) b -> p (a b)", p=128),
                w1c[0, 0, :, :, :].rearrange("p cc w -> p (cc w)"))
        if dbg_what == "glist" and call == 0:
            wait("gpsimd", sem.sGB, S.c(sem.sGB))
            dma("gpsimd", sem.sOUT, dbg_d[0:128, 0:56], glii[:, 0:56])
        wait("gpsimd", sem.sEOD, eo_all)
        wait("gpsimd", sem.sDVE, sint_tgt)
        for tt in range(TT):
            if tt >= 1:
                wait("gpsimd", sem.sDVE2, S.c(sem.sDVE2))
            for k in range(2):
                idma("gpsimd", sem.sGB, xgstage[:, k, :], eo_d[0:128, :],
                     in_off=sint2[:, tt, k:k + 1])
            gb_tgt = S.c(sem.sGB)
            wait("vector", sem.sGB, gb_tgt)
            run("vector", lambda e, tt=tt: e.scalar_tensor_tensor(
                xres[:, tt, :], xgstage[:, 0, :], wts[:, tt, 0:1],
                xres[:, tt, :], OP.mult, OP.add))
            inc("vector", lambda e, tt=tt: e.scalar_tensor_tensor(
                xres[:, tt, :], xgstage[:, 1, :], wts[:, tt, 1:2],
                xres[:, tt, :], OP.mult, OP.add), sem.sDVE2)

    def emit_attn():
        _emit_gb_bcast()
        if dbg_what == "attnc":
            wait("gpsimd", sem.sDVE, S.c(sem.sDVE))
            dma("gpsimd", sem.sOUT, dbg_d[0:128, 0:4], biascols[:, 0:4])
            dma("gpsimd", sem.sOUT, dbg_d[128:256, 0:64],
                g_bc[:, 0:64].bitcast(F32))
            dma("gpsimd", sem.sOUT, dbg_d[256:384, 0:64], b_bc[:, 0:64])
        cc_tgt = emit_ln(is_attn=True)
        if dbg_what == "xn2":
            wait("gpsimd", sem.sXN, cc_tgt)
            dma("gpsimd", sem.sOUT, dbg_d, cc_in[:, 0:64])
        for tt in range(TT):
            dma("gpsimd", sem.sSP, resid_d[tt * 128:(tt + 1) * 128, :],
                xres[:, tt, :])
        sp_tgt = S.c(sem.sSP)
        wait("gpsimd", sem.sXN, cc_tgt)
        inc("gpsimd", lambda e: e.collective_compute(
            "AllGather", OP.bypass,
            replica_groups=[[0, 1], [2, 3], [4, 5], [6, 7]],
            ins=[cc_in.opt()], outs=[cc_out.opt()]), sem.sCC)
        cc_done = S.c(sem.sCC)
        wait("gpsimd", sem.sSP, sp_tgt)

        barrier()

        wait("sync", sem.sWS, wc_done)
        wait("sync", sem.sCC, cc_done)
        wait("sync", sem.sSP, sp_tgt)
        for cc in range(CCn):
            dma("sync", sem.sST, wph[:, cc, :], wkt_d[cc * 128:(cc + 1) * 128, :])
        wkt_tgt = S.c(sem.sST)


        def emit_proj_rms(n_units, proj_dst, rhs_of, gam, sq_tgts, sqrt_tgts,
                          recip_tgts, mult_tgts):
            for step in range(n_units + 1):
                kc = step
                pc = step - 1
                if pc >= 0:
                    wait("tensor", sem.sACT, sq_tgts[pc])
                    inc("tensor", lambda e, pc=pc: e.matmul(
                        P[2][0:2, :], ones2h[:, :], ksq[:, pc % 2, :],
                        start=True, stop=True), sem.sPE1)
                    ss_tgt = S.c(sem.sPE1)
                if kc < n_units:
                    if kc >= 2:
                        wait("tensor", sem.sACT, sq_tgts[kc - 2])
                    for cc in range(CCn):
                        fn = (lambda e, cc=cc, kc=kc: e.matmul(
                            P[kc % 2][:, :], wph[:, cc, kc * 128:(kc + 1) * 128],
                            rhs_of(kc, cc),
                            start=(cc == 0), stop=(cc == CCn - 1)))
                        if cc == CCn - 1:
                            inc("tensor", fn, sem.sPE1)
                        else:
                            run("tensor", fn)
                    proj_tgt = S.c(sem.sPE1)
                if pc >= 0:
                    wait("scalar", sem.sPE1, ss_tgt)
                    inc("scalar", lambda e, pc=pc: e.activation(
                        out=rkt[0:2, pc % 2, :], in_=P[2][0:2, :], func=AF.Sqrt,
                        scale=1.0 / DH), sem.sACT)
                    sqrt_tgts.append(S.c(sem.sACT))
                if kc < n_units:
                    wait("scalar", sem.sPE1, proj_tgt)
                    run("scalar", lambda e, kc=kc: e.copy(
                        out=proj_dst(kc), in_=P[kc % 2][:, :]))
                    inc("scalar", lambda e, kc=kc: e.activation(
                        out=ksq[:, kc % 2, :], in_=P[kc % 2][:, :],
                        func=AF.Square), sem.sACT)
                    sq_tgts.append(S.c(sem.sACT))
                if pc >= 0:
                    wait("vector", sem.sACT, sqrt_tgts[pc])
                    cstep("vector", lambda e, pc=pc: e.tensor_scalar_max(
                        rkt[0:2, pc % 2, :], rkt[0:2, pc % 2, :], EPS_RMS))
                    inc("vector", lambda e, pc=pc: e.reciprocal(
                        rkt[0:2, pc % 2, :], rkt[0:2, pc % 2, :]), sem.sDVE)
                    recip_tgts.append(S.c(sem.sDVE))
                    wait("tensor", sem.sDVE, recip_tgts[pc])
                    if pc >= 1:
                        wait("tensor", sem.sDVE, mult_tgts[pc - 1])
                    inc("tensor", lambda e, pc=pc: e.matmul(
                        P[3][:, :], gam[:, :], rkt[0:2, pc % 2, :],
                        start=True, stop=True), sem.sPE1)
                    bc_tgt = S.c(sem.sPE1)
                    wait("vector", sem.sPE1, bc_tgt)
                    inc("vector", lambda e, pc=pc: e.tensor_tensor(
                        proj_dst(pc), proj_dst(pc), P[3][:, :], OP.mult),
                        sem.sDVE)
                    mult_tgts.append(S.c(sem.sDVE))

        k_sg_pe = []
        for sg in range(NSG):
            par = sg % 2
            if sg >= 2:
                wait("sync", sem.sPE1, k_sg_pe[sg - 2])
            for cc in range(CCn):
                dma("sync", sem.sST, stage[:, par, cc, :],
                    cc_out[sg * 512:(sg + 1) * 512, cc * 128:(cc + 1) * 128],
                    transpose=True)
            st_tgt = S.c(sem.sST)
            wait("tensor", sem.sST, st_tgt)
            if sg == 0:
                wait("tensor", sem.sST, wkt_tgt)
            wait("tensor", sem.sACT, S.c(sem.sACT))
            wait("tensor", sem.sDVE, S.c(sem.sDVE))
            emit_proj_rms(
                CCn,
                (lambda kc, sg=sg: kfm[:, kc, sg * 512:(sg + 1) * 512]),
                (lambda kc, cc, par=par: stage[:, par, cc, :]),
                kgam2, [], [], [], [])
            k_sg_pe.append(S.c(sem.sPE1))

        wait("sync", sem.sPE1, S.c(sem.sPE1))
        for cc in range(CCn):
            dma("sync", sem.sST, wph[:, cc, :], wvt_d[cc * 128:(cc + 1) * 128, :])
        wvt_tgt = S.c(sem.sST)
        for kt in range(NKT - 1):
            run("gpsimd", lambda e, kt=kt: e.memset(vt_h[:, kt, :, 64:65], 1.0))
        inc("gpsimd", lambda e: e.memset(vt_h[:, NKT - 1, :, 64:65], 1.0),
            sem.sGB)
        vones = S.c(sem.sGB)
        v_sg_pe = []
        for sg in range(NSG):
            par = sg % 2
            if sg >= 2:
                wait("sync", sem.sPE1, v_sg_pe[sg - 2])
            for cc in range(CCn):
                dma("sync", sem.sST, stage[:, par, cc, :],
                    cc_out[sg * 512:(sg + 1) * 512, cc * 128:(cc + 1) * 128],
                    transpose=True)
            st_tgt = S.c(sem.sST)
            wait("tensor", sem.sST, st_tgt)
            if sg == 0:
                wait("tensor", sem.sST, wvt_tgt)
                wait("tensor", sem.sGB, vones)
            for i in range(4):
                kt = sg * 4 + i
                for mh in range(2):
                    u = 2 * i + mh
                    pV = P[4 + u % 2]
                    if sg or i or mh:
                        wait("tensor", sem.sDVE2, S.c(sem.sDVE2) - 8)
                    for cc in range(CCn):
                        fn = (lambda e, cc=cc, i=i, mh=mh, par=par, pV=pV:
                              e.matmul(
                                  pV[:, :],
                                  stage[:, par, cc, i * 128:(i + 1) * 128],
                                  wph[:, cc, mh * 512:(mh + 1) * 512],
                                  start=(cc == 0), stop=(cc == CCn - 1)))
                        if cc == CCn - 1:
                            inc("tensor", fn, sem.sPE1)
                        else:
                            run("tensor", fn)
                    wait("vector", sem.sPE1, S.c(sem.sPE1))
                    for hp in range(8):
                        head = mh * 8 + hp
                        inc("vector", lambda e, kt=kt, head=head, hp=hp, pV=pV:
                            e.tensor_copy(
                                out=vt_h[:, kt, head, 0:64],
                                in_=pV[:, hp * 64:(hp + 1) * 64]), sem.sDVE2)
            v_sg_pe.append(S.c(sem.sPE1))

        for qb in range(NQB):
            wait("sync", sem.sDVE2, S.c(sem.sDVE2))
            wait("sync", sem.sPE1, S.c(sem.sPE1))
            wait("tensor", sem.sDVE2, S.c(sem.sDVE2))
            for cc in range(CCn):
                dma("sync", sem.sST, wph[:, cc, :],
                    wqt_d[cc * 128:(cc + 1) * 128, :])
            for cc in range(CCn):
                dma("sync", sem.sST, stage[:, 0, cc, :],
                    cc_in[qb * 512:(qb + 1) * 512, cc * 128:(cc + 1) * 128],
                    transpose=True)
            qst_tgt = S.c(sem.sST)
            wait("tensor", sem.sST, qst_tgt)
            wait("tensor", sem.sACT, S.c(sem.sACT))
            wait("tensor", sem.sDVE, S.c(sem.sDVE))
            emit_proj_rms(
                CCn,
                (lambda qc: qfm[:, qc, :]),
                (lambda qc, cc: stage[:, 0, cc, :]),
                qgam2, [], [], [], [])
            q_done_dve = S.c(sem.sDVE)

            for pr in range(CCn):
                wait("tensor", sem.sDVE, S.c(sem.sDVE))
                hA, hB = 2 * pr, 2 * pr + 1
                for kt in range(NKT):
                    pn = kt % 2
                    pSA, pSB = P[0 + pn], P[2 + pn]
                    if S.c(sem.sACT) >= 1:
                        wait("tensor", sem.sACT, S.c(sem.sACT) - 1)
                    run("tensor", lambda e, pr=pr, kt=kt, pSA=pSA: e.matmul(
                        pSA[:, :], kfm[0:64, pr, kt * 128:(kt + 1) * 128],
                        qfm[0:64, pr, :], tile_position=(0, 0),
                        start=True, stop=True))
                    inc("tensor", lambda e, pr=pr, kt=kt, pSB=pSB: e.matmul(
                        pSB[:, :], kfm[64:128, pr, kt * 128:(kt + 1) * 128],
                        qfm[64:128, pr, :], tile_position=(64, 0),
                        start=True, stop=True), sem.sPE1)
                    wait("scalar", sem.sPE1, S.c(sem.sPE1))
                    run("scalar", lambda e, pn=pn, pSA=pSA: e.activation(
                        out=Pt[:, 2 * pn, :], in_=pSA[:, :], func=AF.Exp,
                        bias=biascols[:, 2:3]))
                    inc("scalar", lambda e, pn=pn, pSB=pSB: e.activation(
                        out=Pt[:, 2 * pn + 1, :], in_=pSB[:, :], func=AF.Exp,
                        bias=biascols[:, 2:3]), sem.sACT)
                    if kt >= 1:
                        po = (kt - 1) % 2
                        wait("tensor", sem.sACT, S.c(sem.sACT) - 1)
                        run("tensor", lambda e, kt=kt, po=po, hA=hA: e.matmul(
                            P[4][0:65, :], vt_h[:, kt - 1, hA, 0:65],
                            Pt[:, 2 * po, :],
                            start=(kt == 1), stop=False))
                        inc("tensor", lambda e, kt=kt, po=po, hB=hB: e.matmul(
                            P[5][0:65, :], vt_h[:, kt - 1, hB, 0:65],
                            Pt[:, 2 * po + 1, :],
                            start=(kt == 1), stop=False), sem.sPE1)
                po = (NKT - 1) % 2
                wait("tensor", sem.sACT, S.c(sem.sACT))
                run("tensor", lambda e, po=po, hA=hA: e.matmul(
                    P[4][0:65, :], vt_h[:, NKT - 1, hA, 0:65],
                    Pt[:, 2 * po, :], start=False, stop=True))
                inc("tensor", lambda e, po=po, hB=hB: e.matmul(
                    P[5][0:65, :], vt_h[:, NKT - 1, hB, 0:65],
                    Pt[:, 2 * po + 1, :], start=False, stop=True), sem.sPE1)
                pv_done = S.c(sem.sPE1)
                wait("vector", sem.sPE1, pv_done)
                for hb in range(2):
                    pO = P[4 + hb]
                    cstep("vector", lambda e, hb=hb, pO=pO: e.tensor_copy(
                        out=onrm[0:65, hb, :], in_=pO[0:65, :]))
                    inc("vector", lambda e, hb=hb: e.reciprocal(
                        rden[0:1, :], onrm[64:65, hb, :]), sem.sDVE)
                    wait("tensor", sem.sDVE, S.c(sem.sDVE))
                    inc("tensor", lambda e: e.matmul(
                        P[6][0:64, :], onesf[0:1, 0:64], rden[0:1, :],
                        start=True, stop=True), sem.sPE1)
                    wait("vector", sem.sPE1, S.c(sem.sPE1))
                    base = 64 * hb
                    inc("vector", lambda e, hb=hb, pr=pr, base=base:
                        e.tensor_tensor(
                            out_norm[base:base + 64, pr, :],
                            onrm[0:64, hb, :], P[6][0:64, :], OP.mult),
                        sem.sDVE)
            onorm_done = S.c(sem.sDVE)

            wait("sync", sem.sDVE, onorm_done)
            for cc in range(CCn):
                dma("sync", sem.sST, wph[:, cc, :],
                    wot_d[cc * 128:(cc + 1) * 128, :])
            wo_tgt = S.c(sem.sST)
            wait("tensor", sem.sST, wo_tgt)
            wait("tensor", sem.sDVE, onorm_done)
            prev_store = None
            for tt4 in range(4):
                tta = qb * 4 + tt4
                if prev_store is not None:
                    wait("sync", sem.sXN, prev_store)
                rs_t = dma("sync", sem.sXN, res_stage[:, :],
                           resid_d[tta * 128:(tta + 1) * 128, :])
                for mh in range(2):
                    pW = P[6 + (2 * tt4 + mh) % 2]
                    if tt4 or mh:
                        wait("tensor", sem.sDVE2, S.c(sem.sDVE2))
                    for pr in range(CCn):
                        run("tensor", lambda e, pr=pr, tt4=tt4, mh=mh, pW=pW:
                            e.matmul(
                                pW[:, :],
                                out_norm[:, pr, tt4 * 128:(tt4 + 1) * 128],
                                wph[:, pr, mh * 512:(mh + 1) * 512],
                                start=(pr == 0), stop=False))
                    inc("tensor", lambda e, mh=mh, pW=pW: e.matmul(
                        pW[:, :], ones1b[0:1, :],
                        borow[0:1, mh * 512:(mh + 1) * 512],
                        start=False, stop=True), sem.sPE1)
                    wait("vector", sem.sPE1, S.c(sem.sPE1))
                    wait("vector", sem.sXN, rs_t)
                    inc("vector", lambda e, mh=mh, pW=pW: e.tensor_add(
                        res_stage[:, mh * 512:(mh + 1) * 512], pW[:, :],
                        res_stage[:, mh * 512:(mh + 1) * 512]), sem.sDVE2)
                wait("sync", sem.sDVE2, S.c(sem.sDVE2))
                prev_store = dma("sync", sem.sXN,
                                 resid_d[tta * 128:(tta + 1) * 128, :],
                                 res_stage[:, :])

        barrier()
        wait("sync", sem.sXN, S.c(sem.sXN))
        for tt in range(TT):
            dma("sync", sem.sXN, xres[:, tt, :],
                resid_d[tt * 128:(tt + 1) * 128, :])
        wait("tensor", sem.sXN, S.c(sem.sXN))
        wait("vector", sem.sXN, S.c(sem.sXN))
        wait("scalar", sem.sXN, S.c(sem.sXN))
        barrier()

    use_sparse = os.environ.get("K_DENSE", "") != "1"
    ffn_fn = emit_ffn_sparse if use_sparse else emit_ffn
    if "ffn1" in stages:
        ffn_fn(0)
        barrier()
    if "attn" in stages:
        emit_attn()
    if "ffn2" in stages:
        ffn_fn(1)
        barrier()

    wait("sync", sem.sDVE2, S.c(sem.sDVE2))
    wait("sync", sem.sDVE, S.c(sem.sDVE))
    for tt in range(TT):
        dma("sync", sem.sOUT, out_d[tt * 128:(tt + 1) * 128, :], xres[:, tt, :])
    wait("sync", sem.sOUT, S.c(sem.sOUT))

    with nc.Block() as block:
        @block.sync
        def _(eng):
            for fn in S.q["sync"]:
                fn(eng)

        @block.scalar
        def _(eng):
            for fn in S.q["scalar"]:
                fn(eng)

        @block.vector
        def _(eng):
            for fn in S.q["vector"]:
                fn(eng)

        @block.tensor
        def _(eng):
            for fn in S.q["tensor"]:
                fn(eng)

        @block.gpsimd
        def _(eng):
            for fn in S.q["gpsimd"]:
                fn(eng)

    stack.close()
    return nc


_NC_CACHE = {}


def kernel(L0_attn_x, L0_attn_Wq, L0_attn_Wk, L0_attn_Wv, L0_attn_Wo,
           L0_attn_bo, L0_attn_ln_g, L0_attn_ln_b, L0_attn_q_gamma,
           L0_attn_k_gamma, L0_ffn_Wg, L0_ffn_W1, L0_ffn_W3, L0_ffn_W2,
           _stages=("ffn1", "attn", "ffn2")):
    x = np.ascontiguousarray(np.asarray(L0_attn_x, dtype=np.float32))
    B, N, Cm = x.shape
    assert (B, N, Cm) == (4, 4096, 1024)
    weights = {
        "wq": np.ascontiguousarray(L0_attn_Wq, dtype=np.float32),
        "wk": np.ascontiguousarray(L0_attn_Wk, dtype=np.float32),
        "wv": np.ascontiguousarray(L0_attn_Wv, dtype=np.float32),
        "wo": np.ascontiguousarray(L0_attn_Wo, dtype=np.float32),
        "bo": np.ascontiguousarray(np.asarray(L0_attn_bo, np.float32).reshape(1, C)),
        "lng": np.ascontiguousarray(np.asarray(L0_attn_ln_g, np.float32).reshape(1, C)),
        "lnb": np.ascontiguousarray(np.asarray(L0_attn_ln_b, np.float32).reshape(1, C)),
        "qg": np.ascontiguousarray(np.asarray(L0_attn_q_gamma, np.float32).reshape(1, DH)),
        "kg": np.ascontiguousarray(np.asarray(L0_attn_k_gamma, np.float32).reshape(1, DH)),
        "wg": np.ascontiguousarray(L0_ffn_Wg, dtype=np.float32),
        "w1": np.ascontiguousarray(L0_ffn_W1, dtype=np.float32),
        "w3": np.ascontiguousarray(L0_ffn_W3, dtype=np.float32),
        "w2": np.ascontiguousarray(L0_ffn_W2, dtype=np.float32),
    }
    key = tuple(_stages)
    if key not in _NC_CACHE:
        _NC_CACHE[key] = build_program(_stages)
    nc = _NC_CACHE[key]
    in_maps = []
    for core in range(8):
        bi, half = core // 2, core % 2
        shard = np.ascontiguousarray(x[bi, half * T:(half + 1) * T, :])
        m = {"x": shard}
        m.update(weights)
        in_maps.append(m)
    res = run_bass_kernel_spmd(nc, in_maps, core_ids=list(range(8)))
    out = np.empty((B, N, Cm), dtype=np.float32)
    global DBG
    DBG = []
    for core in range(8):
        bi, half = core // 2, core % 2
        out[bi, half * T:(half + 1) * T, :] = res.results[core]["out"]
        if "dbg" in res.results[core]:
            DBG.append(np.asarray(res.results[core]["dbg"]))
    return out
